# version 07
# speedup vs baseline: 1.0925x; 1.0109x over previous
import contextlib
import numpy as np
import ml_dtypes
import concourse.bass as bass
import concourse.mybir as mybir
from concourse.bass_utils import run_bass_kernel_spmd
from concourse.aot_env import allow_ici_all_to_all

F32 = mybir.dt.float32
BF16 = mybir.dt.bfloat16
I32 = mybir.dt.int32
F16 = mybir.dt.float16
AF = mybir.ActivationFunctionType
OP = mybir.AluOpType
AX = mybir.AxisListType

NCORES = 8
D = 2048
KC = D // 128
ENGINES = ("sync", "vector", "scalar", "tensor", "gpsimd")


class _Op:
    __slots__ = ("eng", "fn", "kind", "deps", "phase", "idx", "needs_inc",
                 "sem", "count", "tag")

    def __init__(self, eng, fn, kind, phase, idx, tag=""):
        self.eng, self.fn, self.kind = eng, fn, kind
        self.phase, self.idx = phase, idx
        self.deps = []
        self.needs_inc = False
        self.sem = None
        self.count = None
        self.tag = tag


class Auto:

    def __init__(self, nc, es):
        self.nc = nc
        self.es = es
        self.phase_no = 0
        self.pending = {e: [] for e in ENGINES}
        self.last_write = {}
        self.readers = {}
        self.sems = {}
        self.counts = {}
        self.rot = {}
        self.out_dma_keys = []

    def _sem(self, name):
        if name not in self.sems:
            self.sems[name] = self.es.enter_context(self.nc.semaphore(name))
            self.counts[name] = 0
        return self.sems[name]

    def op(self, eng, fn, reads=(), writes=(), kind="c", tag=""):
        rec = _Op(eng, fn, kind, self.phase_no, len(self.pending[eng]), tag)
        deps = []
        for k in list(reads) + list(writes):
            w = self.last_write.get(k)
            if w is not None:
                deps.append(w)
        for k in writes:
            deps.extend(self.readers.get(k, ()))
        seen = set()
        rec.deps = [d for d in deps if not (id(d) in seen or seen.add(id(d)))]
        for k in writes:
            self.last_write[k] = rec
            self.readers[k] = []
        for k in reads:
            self.readers.setdefault(k, []).append(rec)
        self.pending[eng].append(rec)
        return rec

    def dma(self, eng, out, in_, reads=(), writes=(), tag=""):
        return self.op(eng, (lambda e, out=out, in_=in_: e.dma_start(out=out, in_=in_)),
                       reads=reads, writes=writes, kind="dma", tag=tag)

    def cc(self, kind_s, alu, src_dram, dst_dram, rkey, wkey):
        def fn(e):
            with allow_ici_all_to_all():
                return e.collective_compute(
                    kind_s, alu, replica_groups=[list(range(NCORES))],
                    ins=[src_dram.ap().opt()], outs=[dst_dram.ap().opt()])
        rk = rkey if isinstance(rkey, list) else [rkey]
        return self.op("gpsimd", fn, reads=rk, writes=[wkey], kind="cc")

    def phase(self, name=""):
        cur = self.phase_no
        for eng in ENGINES:
            for rec in self.pending[eng]:
                for d in rec.deps:
                    if d.kind == "c" and d.eng == eng and d.phase == cur:
                        pass
                    elif d.kind == "c" and d.phase < cur:
                        pass
                    else:
                        d.needs_inc = True
        for eng in ENGINES:
            for rec in self.pending[eng]:
                if rec.kind == "dma":
                    r = self.rot.get(eng, 0)
                    self.rot[eng] = r + 1
                    rec.sem = f"D_{eng}_{r % 8}"
                    rec.needs_inc = True
                    self._sem(rec.sem)
                    self.counts[rec.sem] += 16
                    rec.count = self.counts[rec.sem]
                elif rec.kind == "cc":
                    r = self.rot.get("cc", 0)
                    self.rot["cc"] = r + 1
                    rec.sem = f"CC{r % 4}"
                    rec.needs_inc = True
                    self._sem(rec.sem)
                    self.counts[rec.sem] += 1
                    rec.count = self.counts[rec.sem]
                elif rec.needs_inc:
                    rec.sem = "P_" + eng
                    self._sem(rec.sem)
                    self.counts[rec.sem] += 1
                    rec.count = self.counts[rec.sem]
        scope = contextlib.ExitStack()
        scope.enter_context(self.nc.named_scope(f"ph{cur:03d}{name}"))
        with self.nc.Block() as block:
            for eng in ENGINES:
                recs = self.pending[eng]
                if not recs:
                    continue
                dec = getattr(block, eng)

                def body(e, recs=recs, eng=eng):
                    watermark = {}
                    last_drain = -1
                    for rec in recs:
                        need_drain = False
                        waits = {}
                        for d in rec.deps:
                            if d.kind == "c" and d.phase < cur:
                                continue
                            if d.kind == "c" and d.eng == eng:
                                if d.idx > last_drain and eng != "tensor":
                                    need_drain = True
                                elif d.idx > last_drain and eng == "tensor":
                                    pass
                                continue
                            if d.sem is None:
                                continue
                            waits[d.sem] = max(waits.get(d.sem, 0), d.count)
                        if rec.kind == "dma" and rec.count > 16:
                            waits[rec.sem] = max(waits.get(rec.sem, 0), rec.count - 16)
                        elif rec.kind == "cc" and rec.count > 1:
                            waits[rec.sem] = max(waits.get(rec.sem, 0), rec.count - 1)
                        if need_drain:
                            e.drain()
                            last_drain = rec.idx - 1
                        for s, c in waits.items():
                            if watermark.get(s, -1) < c:
                                e.wait_ge(self.sems[s], c)
                                watermark[s] = c
                        ins = rec.fn(e)
                        if rec.needs_inc:
                            if rec.kind == "dma":
                                ins.then_inc(self.sems[rec.sem], 16)
                            else:
                                ins.then_inc(self.sems[rec.sem], 1)

                dec(body)
        scope.close()
        self.pending = {e: [] for e in ENGINES}
        self.phase_no += 1

    def finish(self, eng="sync"):
        def fn(e):
            for s, c in sorted(self.counts.items()):
                if c > 0:
                    e.wait_ge(self.sems[s], c)
            return e.nop()
        self.op(eng, fn, reads=(), writes=(), kind="c")


def ceil_div(x, y):
    return (x + y - 1) // y


def nspans(T, W=512):
    return [(n0, min(W, T - n0)) for n0 in range(0, T, W)]


def q_load_act(a, dram_ap, sb, T, key, kin=KC, src_key=None):
    for k in range(kin):
        a.dma("sync", sb[:, k * T:(k + 1) * T], dram_ap[128 * k:128 * (k + 1), :],
              reads=[src_key] if src_key else (), writes=[(key, k)])


def q_store_act(a, sb, dram_ap, T, key, dst_key, kin=KC, eng="sync"):
    for k in range(kin):
        a.dma(eng, dram_ap[128 * k:128 * (k + 1), :], sb[:, k * T:(k + 1) * T],
              reads=[(key, k)], writes=[(dst_key, k)])


def q_stream_weight(a, w_dram, w_sb, kin, M, stage, key, wkey, quant=None,
                    src_key=None):
    stgk = lambda sl_: ("STG", id(stage), sl_)
    for k in range(kin):
        sl = stage[:, (k % 2) * M:(k % 2 + 1) * M]
        a.dma("sync", sl, w_dram[128 * k:128 * (k + 1), :],
              reads=([src_key] if src_key else []) + [stgk(k % 2)],
              writes=[(key, "stg", k % 2), stgk(k % 2)])
        dst = w_sb[:, k * M:(k + 1) * M]
        if quant is None:
            a.op("vector", (lambda e, sl=sl, dst=dst: e.tensor_copy(dst, sl)),
                 reads=[(key, "stg", k % 2), stgk(k % 2)], writes=[(wkey, k)])
        else:
            quant(a, sl, dst, k, (key, "stg", k % 2), (wkey, k))
            a.op("vector", lambda e: e.nop(), reads=[(wkey, k)], writes=[stgk(k % 2)])


def q_mm(a, tag, groups, lhs_fn, rhs_fn, kin, psums, evac, rkeys_fn, out_ap_fn):
    np_ = len(psums)
    for gi, g in enumerate(groups):
        ps = psums[gi % np_]
        pskey = ("PSM", id(ps))
        rk = list(rkeys_fn(g))
        for k in range(kin):
            def fn(e, g=g, k=k, ps=ps):
                return e.matmul(out_ap_fn(g, ps), lhs_fn(g, k), rhs_fn(g, k),
                                start=(k == 0), stop=(k == kin - 1))
            a.op("tensor", fn, reads=rk, writes=[pskey], tag=tag)
        evac(a, g, ps, pskey)


class Rounder:
    def __init__(self, a, ctx_sbuf, name, width):
        self.i1 = ctx_sbuf(name + "_ri", [128, width], I32)
        self.f1 = ctx_sbuf(name + "_rf1", [128, width], F32)
        self.f2 = ctx_sbuf(name + "_rf2", [128, width], F32)
        self.name = name

    def round(self, a, z_ap, out_ap, n, zkey, okey):
        i1, f1, f2 = self.i1[:, :n], self.f1[:, :n], self.f2[:, :n]
        nm = self.name
        a.op("vector", lambda e: e.tensor_scalar(f1, z_ap, 8192.5, None, OP.add),
             reads=[zkey], writes=[(nm, "f1")])
        a.op("vector", lambda e: e.tensor_copy(i1, f1),
             reads=[(nm, "f1")], writes=[(nm, "i1")])
        a.op("vector", lambda e: e.tensor_copy(f2, i1),
             reads=[(nm, "i1")], writes=[(nm, "f2")])
        a.op("vector", lambda e: e.scalar_tensor_tensor(f1, f2, -1.0, f1, OP.mult, OP.add),
             reads=[(nm, "f1"), (nm, "f2")], writes=[(nm, "f1")])
        a.op("vector", lambda e: e.tensor_scalar(f1, f1, -0.5, None, OP.add),
             reads=[(nm, "f1")], writes=[(nm, "f1")])
        a.op("vector", lambda e: e.tensor_copy(i1, f1),
             reads=[(nm, "f1")], writes=[(nm, "i1")])
        a.op("vector", lambda e: e.tensor_copy(f1, i1),
             reads=[(nm, "i1")], writes=[(nm, "f1")])
        a.op("vector", lambda e: e.tensor_tensor(f2, f2, f1, OP.add),
             reads=[(nm, "f1"), (nm, "f2")], writes=[(nm, "f2")])
        a.op("vector", lambda e: e.tensor_scalar(out_ap, f2, -8192.0, None, OP.add),
             reads=[(nm, "f2")], writes=[okey])


_S = 8192
_DEINT = np.concatenate([np.arange(0, 128, 2), np.arange(1, 128, 2)])
_PERM = [c for p in range(8) for c in (p, p + 8)]


def _host_consts():
    return {
        "c_ones_col": np.ones((128, 1), dtype=ml_dtypes.bfloat16),
        "c_ones_row": np.ones((1, 128), dtype=ml_dtypes.bfloat16),
        "c_id32": np.eye(128, dtype=np.float32),
        "c_ones_row32": np.ones((1, 128), dtype=np.float32),
        "c_idbf": np.eye(128, dtype=ml_dtypes.bfloat16),
    }


def _host_cs_cm(S):
    th = (1.0 / 10000.0 ** (np.arange(64, dtype=np.float32) / np.float32(64))).astype(np.float32)
    t = np.arange(S, dtype=np.float32)
    ang = np.outer(th, t)
    cs = np.concatenate([np.cos(ang), np.sin(ang)], axis=0).astype(np.float16)
    cm = np.zeros((128, 4 * 512), dtype=np.float32)
    qq = np.arange(128)[:, None]
    kk = np.arange(512)[None, :]
    for j in range(4):
        cm[:, j * 512:(j + 1) * 512] = np.where(kk <= qq + 128 * j, 0.0, -1e9)
    return cs, cm.astype(ml_dtypes.bfloat16)


def _launch(nc, in_maps, out_name):
    r = run_bass_kernel_spmd(nc, in_maps, core_ids=list(range(NCORES)))
    return np.concatenate([np.asarray(r.results[i][out_name])
                           for i in range(NCORES)], axis=1)


def kernel(**inputs):
    S, T = _S, _S // NCORES
    g = {k: np.asarray(v) for k, v in inputs.items()}
    c = _host_consts()
    cs, cm = _host_cs_cm(S)
    xT = np.ascontiguousarray(g["L0_attn_x"][0].T)

    base = dict(c)
    base.update(b0_wq=g["L0_attn_Wq"], b0_wk=g["L0_attn_Wk"], b0_wv=g["L0_attn_Wv"],
                b0_wo=g["L0_attn_Wo"],
                b0_wf=np.ascontiguousarray(g["L0_ffn_W"].T),
                b0_gamma_pk=np.ascontiguousarray(g["L0_ffn_gamma"].reshape(16, 128).T))

    colperm = np.concatenate([np.arange(128 * h, 128 * (h + 1)) for h in _PERM])
    wfcT = np.ascontiguousarray(g["L1_ffn_w_fc"].T)
    wpjT = np.ascontiguousarray(g["L1_ffn_w_proj"].T)
    slopes = (2.0 ** (-8.0 * np.arange(1, 17, dtype=np.float32) / np.float32(16))
              ).astype(np.float32)
    base.update(b1_wq=np.ascontiguousarray(g["L1_attn_Wq"][:, colperm]),
                b1_wk=g["L1_attn_Wk"], b1_wv=g["L1_attn_Wv"],
                b1_wo=np.ascontiguousarray(g["L1_attn_Wo"][colperm, :]),
                b1_cs=cs, b1_cmask=cm)

    M = np.ascontiguousarray(g["L2_attn_W_qkv"].T)
    wq2 = M[:, :2048].copy(); wk2 = M[:, 2048:4096].copy(); wv2 = M[:, 4096:].copy()
    for h in range(16):
        wq2[:, 128 * h:128 * (h + 1)] = wq2[:, 128 * h + _DEINT]
        wk2[:, 128 * h:128 * (h + 1)] = wk2[:, 128 * h + _DEINT]
    gT2 = np.ascontiguousarray(g["L2_ffn_gate_idx"].T)
    uT2 = np.ascontiguousarray(g["L2_ffn_up_idx"].T)
    dT2 = np.ascontiguousarray(g["L2_ffn_down_idx"].T)
    base.update(b2_wq=wq2, b2_wk=wk2, b2_wv=wv2,
                b2_wo=np.ascontiguousarray(g["L2_attn_W_o"].T),
                b2_lut=g["L2_ffn_lut"].reshape(1, 16), b2_cs=cs, b2_cmask=cm,
                b2_slopes=np.zeros((1, 2), dtype=np.float32))

    M3 = np.ascontiguousarray(g["L3_attn_Wxproj"].T)
    cw = g["L3_attn_conv_w"]
    gT3 = np.ascontiguousarray(g["L3_ffn_gate_idx"].T)
    uT3 = np.ascontiguousarray(g["L3_ffn_up_idx"].T)
    dT3 = np.ascontiguousarray(g["L3_ffn_down_idx"].T)
    base.update(b3_wbc=np.ascontiguousarray(M3[:, :1024]),
                b3_wz=np.ascontiguousarray(M3[:, 1024:1025]),
                b3_win=np.ascontiguousarray(g["L3_attn_Winproj"].T),
                b3_wout=np.ascontiguousarray(g["L3_attn_Woutproj"].T),
                b3_convw=np.ascontiguousarray(np.transpose(
                    cw[:, 0, :].reshape(16, 128, 4), (1, 0, 2)).reshape(128, 64)),
                b3_convb=np.ascontiguousarray(g["L3_attn_conv_b"].reshape(16, 128).T),
                b3_alog=np.ascontiguousarray(g["L3_attn_A_log"].reshape(4, 128).T),
                b3_dcoef=np.ascontiguousarray(g["L3_attn_D"].reshape(16, 128).T),
                b3_lut=g["L3_ffn_lut"].reshape(1, 16))

    maps = []
    for ci in range(NCORES):
        msk = np.zeros((1, 16), dtype=np.float32)
        if ci > 0:
            msk[0, ci - 1] = 1.0
        msk[0, 8:8 + ci] = 1.0
        m = dict(base, b3_msk=msk,
                 b0_xT=np.ascontiguousarray(xT[:, ci * T:(ci + 1) * T]),
                 b1_wfcs=np.ascontiguousarray(wfcT[:, ci * 1024:(ci + 1) * 1024]),
                 b1_wpjs=np.ascontiguousarray(wpjT[:, ci * 256:(ci + 1) * 256]),
                 b1_slopes=np.array([[slopes[ci], slopes[ci + 8]]], dtype=np.float32),
                 b2_gidx=np.ascontiguousarray(gT2[256 * ci:256 * (ci + 1), :]),
                 b2_uidx=np.ascontiguousarray(uT2[256 * ci:256 * (ci + 1), :]),
                 b2_didx=np.ascontiguousarray(dT2[:, 256 * ci:256 * (ci + 1)]),
                 b3_gidx=np.ascontiguousarray(gT3[256 * ci:256 * (ci + 1), :]),
                 b3_uidx=np.ascontiguousarray(uT3[256 * ci:256 * (ci + 1), :]),
                 b3_didx=np.ascontiguousarray(dT3[:, 256 * ci:256 * (ci + 1)]))
        maps.append(m)

    x4T = _launch(build_fused(S), maps, "x4T")
    return np.ascontiguousarray(x4T.T)[None].astype(np.float32)


def q_bcast_row(a, u, row_ap, rowkey, out_sb, outkey, T, tmptag):
    rowbf = u["rowbf"]
    a.op("vector", lambda e: e.tensor_copy(rowbf[:, :T], row_ap),
         reads=[rowkey], writes=[(tmptag, "rowbf")])
    for i, (n0, nn) in enumerate(nspans(T)):
        ps = u["psb"][i % 2]
        def mm(e, n0=n0, nn=nn, ps=ps):
            return e.matmul(ps[:, :nn], u["ones_row"][:, :], rowbf[:, n0:n0 + nn],
                            start=True, stop=True)
        a.op("tensor", mm, reads=[(tmptag, "rowbf"), "c"], writes=[("PSM", id(ps))])
        def ev(e, n0=n0, nn=nn, ps=ps):
            return e.tensor_copy(out_sb[:, n0:n0 + nn], ps[:, :nn])
        a.op("vector", ev, reads=[("PSM", id(ps))], writes=[(outkey, i)])
    a.op("vector", lambda e: e.nop(), reads=[(outkey, i) for i in range(len(nspans(T)))],
         writes=[outkey])


def q_layernorm(a, u, T, x_sb, xkey, xn, xnkey, eps=1e-5, keep32=False):
    NT = nspans(T)
    xbf, sqbf, rowbuf = u["xbf"], u["sqbf"], u["rowbuf"]
    mu_b, rs_b = u["mu_b"], u["rs_b"]
    for k in range(KC):
        sl = (k % 2) * T
        a.op("vector", (lambda e, k=k, sl=sl: e.tensor_copy(
            xbf[:, sl:sl + T], x_sb[:, k * T:(k + 1) * T])),
            reads=[(xkey, k)], writes=[("lnxbf", k % 2)])
        a.op("scalar", (lambda e, k=k, sl=sl: e.activation(
            sqbf[:, sl:sl + T], xbf[:, sl:sl + T], AF.Square)),
            reads=[("lnxbf", k % 2)], writes=[("lnsq", k % 2)])
        for i, (n0, nn) in enumerate(NT):
            a.op("tensor", (lambda e, k=k, sl=sl, n0=n0, nn=nn, i=i: e.matmul(
                u["pstat"][i][:1, :nn], u["ones_col"][:, :],
                xbf[:, sl + n0:sl + n0 + nn],
                start=(k == 0), stop=(k == KC - 1))),
                reads=[("lnxbf", k % 2), "c"], writes=[("PSM", id(u["pstat"][i]))], tag="ln")
            a.op("tensor", (lambda e, k=k, sl=sl, n0=n0, nn=nn, i=i: e.matmul(
                u["pstat"][len(NT) + i][:1, :nn], u["ones_col"][:, :],
                sqbf[:, sl + n0:sl + n0 + nn],
                start=(k == 0), stop=(k == KC - 1))),
                reads=[("lnsq", k % 2), "c"],
                writes=[("PSM", id(u["pstat"][len(NT) + i]))], tag="ln")
    for i, (n0, nn) in enumerate(NT):
        a.op("vector", (lambda e, i=i, n0=n0, nn=nn: e.tensor_scalar(
            rowbuf[:1, n0:n0 + nn], u["pstat"][i][:1, :nn], 1.0 / D, None, OP.mult)),
            reads=[("PSM", id(u["pstat"][i]))], writes=[("lnrow", "mu", i)])
        a.op("vector", (lambda e, i=i, n0=n0, nn=nn: e.tensor_scalar(
            rowbuf[:1, T + n0:T + n0 + nn], u["pstat"][len(NT) + i][:1, :nn],
            1.0 / D, None, OP.mult)),
            reads=[("PSM", id(u["pstat"][len(NT) + i]))], writes=[("lnrow", "va", i)])
    rmu = [("lnrow", "mu", i) for i in range(len(NT))]
    rva = [("lnrow", "va", i) for i in range(len(NT))]
    a.op("vector", lambda e: e.tensor_tensor(rowbuf[:1, 2 * T:3 * T],
                                             rowbuf[:1, 0:T], rowbuf[:1, 0:T], OP.mult),
         reads=rmu, writes=[("lnrow", "mu2")])
    a.op("vector", lambda e: e.tensor_tensor(rowbuf[:1, T:2 * T],
                                             rowbuf[:1, T:2 * T], rowbuf[:1, 2 * T:3 * T],
                                             OP.subtract),
         reads=rva + [("lnrow", "mu2")], writes=[("lnrow", "var")])
    a.op("vector", lambda e: e.tensor_scalar(rowbuf[:1, T:2 * T], rowbuf[:1, T:2 * T],
                                             eps, None, OP.add),
         reads=[("lnrow", "var")], writes=[("lnrow", "var")])
    a.op("scalar", lambda e: e.activation(rowbuf[:1, T:2 * T], rowbuf[:1, T:2 * T],
                                          AF.Sqrt),
         reads=[("lnrow", "var")], writes=[("lnrow", "var")])
    a.op("vector", lambda e: e.reciprocal(rowbuf[:1, T:2 * T], rowbuf[:1, T:2 * T]),
         reads=[("lnrow", "var")], writes=[("lnrow", "rstd")])
    q_bcast_row(a, u, rowbuf[:1, 0:T], rmu[-1], mu_b, "mu_b", T, "bmu")
    q_bcast_row(a, u, rowbuf[:1, T:2 * T], ("lnrow", "rstd"), rs_b, "rs_b", T, "brs")
    for k in range(KC):
        a.op("vector", (lambda e, k=k: e.tensor_tensor(
            x_sb[:, k * T:(k + 1) * T], x_sb[:, k * T:(k + 1) * T], mu_b[:, :T],
            OP.subtract)), reads=[(xkey, k), "mu_b"], writes=[(xkey, k)])
        a.op("vector", (lambda e, k=k: e.tensor_tensor(
            xn[:, k * T:(k + 1) * T], x_sb[:, k * T:(k + 1) * T], rs_b[:, :T],
            OP.mult)), reads=[(xkey, k), "rs_b"], writes=[(xnkey, k)])
        if keep32:
            a.op("vector", (lambda e, k=k: e.tensor_tensor(
                x_sb[:, k * T:(k + 1) * T], x_sb[:, k * T:(k + 1) * T], rs_b[:, :T],
                OP.mult)), reads=[(xkey, k), "rs_b", (xnkey, k)], writes=[(xkey, k), (xnkey, "32", k)])


def mk_utils(a, nc, es, T):
    sb = lambda n, shp, dt: es.enter_context(nc.sbuf_tensor(n, shp, dt))
    NT = nspans(T)
    u = {
        "ones_col": sb("u_ones_col", [128, 1], BF16),
        "ones_row": sb("u_ones_row", [1, 128], BF16),
        "id32": sb("u_id32", [128, 128], F32),
        "ones_row32": sb("u_ones_row32", [1, 128], F32),
        "idbf": sb("u_idbf", [128, 128], BF16),
        "rowbuf": sb("u_rowbuf", [1, max(3 * T, 2 * T + D)], F32),
        "rowbf": sb("u_rowbf", [1, max(T, D)], BF16),
        "mu_b": sb("u_mu_b", [128, T], F32),
        "rs_b": sb("u_rs_b", [128, T], F32),
    }
    return u


def q_load_consts(a, u, cdram):
    for name in ("ones_col", "ones_row", "id32", "idbf", "ones_row32"):
        a.dma("sync", u[name][:, :], cdram[name], writes=[("c", name)])
    a.op("sync", lambda e: e.nop(),
         reads=[("c", n) for n in ("ones_col", "ones_row", "id32", "idbf", "ones_row32")],
         writes=["c"])


def _drain_dmas(a):
    def fn(e):
        snap = {s2: c for s2, c in a.counts.items()
                if c > 0 and (s2.startswith("D_") or s2.startswith("CC"))}
        for s2, c in sorted(snap.items()):
            e.wait_ge(a.sems[s2], c)
        return e.nop()
    for eng in ENGINES:
        a.op(eng, fn)


class _NCPfx:

    def __init__(self, nc, pfx):
        self._nc = nc
        self._pfx = pfx

    def sbuf_tensor(self, name, *a, **k):
        return self._nc.sbuf_tensor(self._pfx + name, *a, **k)

    def dram_tensor(self, name, *a, **k):
        return self._nc.dram_tensor(self._pfx + name, *a, **k)

    def psum_tensor(self, name, *a, **k):
        return self._nc.psum_tensor(self._pfx + name, *a, **k)

    def semaphore(self, name, *a, **k):
        return self._nc.semaphore(self._pfx + name, *a, **k)

    def __getattr__(self, x):
        return getattr(self._nc, x)


def _mk_cdram(nc):
    return {
        "ones_col": nc.dram_tensor("c_ones_col", [128, 1], BF16, kind="ExternalInput").ap(),
        "ones_row": nc.dram_tensor("c_ones_row", [1, 128], BF16, kind="ExternalInput").ap(),
        "id32": nc.dram_tensor("c_id32", [128, 128], F32, kind="ExternalInput").ap(),
        "idbf": nc.dram_tensor("c_idbf", [128, 128], BF16, kind="ExternalInput").ap(),
        "ones_row32": nc.dram_tensor("c_ones_row32", [1, 128], F32, kind="ExternalInput").ap(),
    }


def build_fused(S):
    T = S // NCORES
    nc = bass.Bass("TRN2", target_bir_lowering=False)
    es0 = contextlib.ExitStack()
    with es0:
        a = Auto(nc, es0)
        cdram = _mk_cdram(nc)
        xin = nc.dram_tensor("b0_xT", [D, T], F32, kind="ExternalInput").ap()
        h1 = nc.dram_tensor("h1", [D, T], F32).ap()
        h2 = nc.dram_tensor("h2", [D, T], F32).ap()
        h3 = nc.dram_tensor("h3", [D, T], F32).ap()
        x4T = nc.dram_tensor("x4T", [D, T], F32, kind="ExternalOutput").ap()
        build_l0(S, dict(nc=nc, pfx="b0_", a=a, cdram=cdram, x_in=xin, x_out=h1))
        build_l1(S, dict(nc=nc, pfx="b1_", a=a, cdram=cdram, x_in=h1, x_out=h2))
        build_l2(S, dict(nc=nc, pfx="b2_", a=a, cdram=cdram, x_in=h2, x_out=h3))
        build_l3(S, dict(nc=nc, pfx="b3_", a=a, cdram=cdram, x_in=h3, x_out=x4T))
    return nc


def build_l0(S, ctx=None):
    T = S // NCORES
    TC = T // 128
    NT = nspans(T)
    ND = nspans(D)
    nc = _NCPfx(ctx["nc"], ctx["pfx"]) if ctx else bass.Bass("TRN2", target_bir_lowering=False)
    xT = ctx["x_in"] if ctx else nc.dram_tensor("xT", [D, T], F32, kind="ExternalInput").ap()
    wq_d = nc.dram_tensor("wq", [D, D], F32, kind="ExternalInput").ap()
    wk_d = nc.dram_tensor("wk", [D, D], F32, kind="ExternalInput").ap()
    wv_d = nc.dram_tensor("wv", [D, D], F32, kind="ExternalInput").ap()
    wo_d = nc.dram_tensor("wo", [D, D], F32, kind="ExternalInput").ap()
    wf_d = nc.dram_tensor("wf", [D, D], F32, kind="ExternalInput").ap()
    gamma_d = nc.dram_tensor("gamma_pk", [128, KC], F32, kind="ExternalInput").ap()
    cdram = ctx["cdram"] if ctx else _mk_cdram(nc)
    x1T = ctx["x_out"] if ctx else nc.dram_tensor("x1T", [D, T], F32, kind="ExternalOutput").ap()
    v_dram = nc.dram_tensor("v_d", [D, T], BF16)
    k_dram = nc.dram_tensor("k_d", [T, D], BF16)
    scp_dram = nc.dram_tensor("sc_part", [D, D], F32)
    scf_dram = nc.dram_tensor("sc_full", [D, D], F32)

    es = contextlib.ExitStack()
    with es:
        a = ctx["a"] if ctx else Auto(nc, es)
        sb = lambda n, shp, dt, st=es, side=None: st.enter_context(nc.sbuf_tensor(n, shp, dt, side=side))
        u = mk_utils(a, nc, es, T)
        PS = [es.enter_context(nc.psum_tensor(f"PS{i}", [128, 512], F32)) for i in range(4)]
        PR = [es.enter_context(nc.psum_tensor(f"PR{i}", [1, 512], F32)) for i in range(4)]
        u["pstat"] = PR
        u["psb"] = PS[2:4]
        xn = sb("xn", [128, KC * T], BF16)
        stage = sb("stage", [128, 2 * D], F32)
        gam = sb("gam", [128, KC], F32)

        r0 = contextlib.ExitStack()
        x_sb = sb("x_sb", [128, KC * T], F32, r0)
        u["xbf"] = sb("xbf", [128, 2 * T], BF16, r0)
        u["sqbf"] = sb("sqbf", [128, 2 * T], BF16, r0)
        q_load_consts(a, u, cdram)
        a.dma("sync", gam[:, :], gamma_d, writes=["gam"])
        q_load_act(a, xT, x_sb, T, "x")
        q_layernorm(a, u, T, x_sb, "x", xn, "xn")
        _drain_dmas(a)
        a.phase()
        r0.close()
        XN = [("xn", k) for k in range(KC)]

        r1 = contextlib.ExitStack()
        q_sb = sb("q_sb", [128, TC * D], BF16, r1)
        kstg = sb("kstg", [128, 2 * D], BF16, r1)
        vstg = sb("vstg", [128, 2 * T], BF16, r1)
        scstg = sb("scstg", [128, 2 * D], F32, r1)
        kstrip = sb("kstrip", [128, 2 * TC * 128], BF16, r1)
        r1b = contextlib.ExitStack()
        w_sb = sb("w_sb", [128, KC * D], BF16, r1b)

        def tok_proj(wdram, dst_sb, dstkey, tag):
            q_stream_weight(a, wdram, w_sb, KC, D, stage, tag, tag + "w")
            wk_keys = [(tag + "w", k) for k in range(KC)]
            groups = [(t, n0, nn) for t in range(TC) for (n0, nn) in ND]
            q_mm(a, tag + "mm", groups,
                 lambda g, k: xn[:, k * T + g[0] * 128:k * T + (g[0] + 1) * 128],
                 lambda g, k: w_sb[:, k * D + g[1]:k * D + g[1] + g[2]],
                 KC, PS[0:2],
                 lambda a_, g, ps, pk: a_.op(
                     "vector",
                     (lambda e, g=g, ps=ps: e.tensor_copy(
                         dst_sb[:, g[0] * D + g[1]:g[0] * D + g[1] + g[2]],
                         ps[:, :g[2]])),
                     reads=[pk], writes=[(dstkey, g[0], g[1])]),
                 lambda g: XN + wk_keys,
                 lambda g, ps: ps[:, :g[2]])

        tok_proj(wq_d, q_sb, "q", "pq")
        a.phase()

        q_stream_weight(a, wk_d, w_sb, KC, D, stage, "pk", "pkw")
        pkw_keys = [("pkw", k) for k in range(KC)]
        kgroups = [(t, n0, nn) for t in range(TC) for (n0, nn) in ND]

        def k_evac(a_, g, ps, pk):
            t, n0, nn = g
            slot = t % 2
            a_.op("vector", (lambda e: e.tensor_copy(
                kstg[:, slot * D + n0:slot * D + n0 + nn], ps[:, :nn])),
                reads=[pk], writes=[("kstg", slot, n0)])
            if n0 + nn >= D:
                a_.dma("scalar", k_dram[t * 128:(t + 1) * 128, :],
                       kstg[:, slot * D:slot * D + D],
                       reads=[("kstg", slot, x0) for (x0, _) in ND],
                       writes=[("k_d", t)])
        q_mm(a, "pkmm", kgroups,
             lambda g, k: xn[:, k * T + g[0] * 128:k * T + (g[0] + 1) * 128],
             lambda g, k: w_sb[:, k * D + g[1]:k * D + g[1] + g[2]],
             KC, PS[0:2], k_evac, lambda g: XN + pkw_keys,
             lambda g, ps: ps[:, :g[2]])
        a.phase()

        q_stream_weight(a, wv_d, w_sb, KC, D, stage, "pv", "pvw")
        pvw_keys = [("pvw", k) for k in range(KC)]
        vgroups = [(m, n0, nn) for m in range(KC) for (n0, nn) in NT]

        def v_evac(a_, g, ps, pk):
            m, n0, nn = g
            slot = m % 2
            a_.op("vector", (lambda e: e.tensor_copy(
                vstg[:, slot * T + n0:slot * T + n0 + nn], ps[:, :nn])),
                reads=[pk], writes=[("vstg", slot, n0)])
            if n0 + nn >= T:
                a_.dma("scalar", v_dram[128 * m:128 * (m + 1), :],
                       vstg[:, slot * T:slot * T + T],
                       reads=[("vstg", slot, x0) for (x0, _) in NT],
                       writes=[("v_d", m)])
        q_mm(a, "pvmm", vgroups,
             lambda g, k: w_sb[:, k * D + g[0] * 128:k * D + (g[0] + 1) * 128],
             lambda g, k: xn[:, k * T + g[1]:k * T + g[1] + g[2]],
             KC, PS[0:2], v_evac, lambda g: XN + pvw_keys,
             lambda g, ps: ps[:, :g[2]])
        _drain_dmas(a)
        a.phase()

        r1b.close()
        q_keys = [("q", t, e) for t in range(TC) for (e, _) in ND]
        groups = [(f, e0, nn) for f in range(KC) for (e0, nn) in ND]

        def sc_evac(a_, g, ps, pk):
            f, e0, nn = g
            slot = f % 2
            a_.op("vector", (lambda e: e.tensor_scalar(
                scstg[:, slot * D + e0:slot * D + e0 + nn], ps[:, :nn],
                0.125, None, OP.mult)),
                reads=[pk], writes=[("scstg", slot, e0)])
            if e0 + nn >= D:
                a_.dma("scalar", scp_dram[128 * f:128 * (f + 1), :],
                       scstg[:, slot * D:slot * D + D],
                       reads=[("scstg", slot, ee0) for (ee0, _) in ND],
                       writes=[("scp", f)])
        for f in range(KC):
            slot = f % 2
            for t in range(TC):
                a.dma("sync",
                      kstrip[:, slot * TC * 128 + t * 128:slot * TC * 128 + (t + 1) * 128],
                      k_dram[t * 128:(t + 1) * 128, f * 128:(f + 1) * 128],
                      reads=[("k_d", t)] + ([("scmm_used", f - 2)] if f >= 2 else []),
                      writes=[("kstrip", slot, t)])
            for ie, (e0, nn) in enumerate(ND):
                ps = PS[(f * len(ND) + ie) % 2]
                pskey = ("PSM", id(ps))
                for t in range(TC):
                    def fn(e, f=f, slot=slot, t=t, e0=e0, nn=nn, ps=ps):
                        return e.matmul(ps[:, :nn],
                                        kstrip[:, slot * TC * 128 + t * 128:slot * TC * 128 + (t + 1) * 128],
                                        q_sb[:, t * D + e0:t * D + e0 + nn],
                                        start=(t == 0), stop=(t == TC - 1))
                    a.op("tensor", fn,
                         reads=[("kstrip", slot, t), ("q", t, e0)],
                         writes=[pskey] + ([("scmm_used", f)] if ie == len(ND) - 1 and t == TC - 1 else []))
                sc_evac(a, (f, e0, nn), ps, pskey)
        a.cc("AllReduce", OP.add, scp_dram, scf_dram,
             [("scp", f) for f in range(KC)], "scf")
        _drain_dmas(a)
        a.phase()
        r1.close()

        r2 = contextlib.ExitStack()
        p_sbT = sb("p_sbT", [128, KC * D], BF16, r2)
        maxcol = sb("maxcol", [128, 2], F32, r2)
        negm = sb("negm", [128, 1], F32, r2)
        linv_col = sb("linv_col", [128, KC], F32, r2)
        r2s = contextlib.ExitStack()
        scstg2 = sb("scstg2", [128, 2 * D], F32, r2s)
        for f in range(KC):
            rd = ["scf"] + ([("mxacc",)] if f >= 2 else [])
            a.dma("sync", scstg2[:, (f % 2) * D:(f % 2 + 1) * D],
                  scf_dram[128 * f:128 * (f + 1), :], reads=rd, writes=[("scld", f)])
            a.op("vector", (lambda e, f=f: e.reduce_max(
                maxcol[:, 1:2], scstg2[:, (f % 2) * D:(f % 2 + 1) * D], axis=AX.X)),
                reads=[("scld", f)], writes=[("mx1",)])
            if f == 0:
                a.op("vector", lambda e: e.tensor_copy(maxcol[:, 0:1], maxcol[:, 1:2]),
                     reads=[("mx1",)], writes=[("mxacc",)])
            else:
                a.op("vector", (lambda e: e.tensor_tensor(
                    maxcol[:, 0:1], maxcol[:, 0:1], maxcol[:, 1:2], OP.max)),
                    reads=[("mx1",), ("mxacc",)], writes=[("mxacc",)])
        pt = PS[3]
        a.op("tensor", lambda e: e.transpose(pt[:1, :128], maxcol[:, 0:1], u["id32"][:, :]),
             reads=[("mxacc",), "c"], writes=[("PSM", id(PS[3]))])
        a.op("vector", lambda e: e.reduce_max(u["rowbuf"][:1, 0:1], pt[:1, :128], axis=AX.X),
             reads=[("PSM", id(PS[3]))], writes=[("mrow",)])
        a.op("vector", lambda e: e.tensor_copy(u["rowbf"][:1, 0:1], u["rowbuf"][:1, 0:1]),
             reads=[("mrow",)], writes=[("mbf",)])
        a.op("tensor", lambda e: e.matmul(pt[:, 0:1], u["ones_row"][:, :],
                                          u["rowbf"][:1, 0:1], start=True, stop=True),
             reads=[("mbf",), "c", ("mrow",)], writes=[("PSM", id(PS[3]))])
        a.op("vector", lambda e: e.tensor_scalar(negm[:, :], pt[:, 0:1], -1.0, None, OP.mult),
             reads=[("PSM", id(PS[3]))], writes=[("negm",)])
        _drain_dmas(a)
        a.phase()

        for f in range(KC):
            rd = [("pexp", f - 2)] if f >= 2 else ["scf"]
            a.dma("sync", scstg2[:, (f % 2) * D:(f % 2 + 1) * D],
                  scf_dram[128 * f:128 * (f + 1), :], reads=rd, writes=[("scl2", f)])
            a.op("scalar", (lambda e, f=f: e.activation(
                p_sbT[:, f * D:(f + 1) * D], scstg2[:, (f % 2) * D:(f % 2 + 1) * D],
                AF.Exp, bias=negm[:, :])),
                reads=[("scl2", f), ("negm",)], writes=[("pexp", f)])
            for i, (e0, nn) in enumerate(ND):
                a.op("tensor", (lambda e, f=f, e0=e0, nn=nn, i=i: e.matmul(
                    PR[i][:, :nn], u["ones_col"][:, :],
                    p_sbT[:, f * D + e0:f * D + e0 + nn],
                    start=(f == 0), stop=(f == KC - 1))),
                    reads=[("pexp", f), "c"], writes=[("PSM", id(PR[i]))])
        LP = 2 * T
        for i, (e0, nn) in enumerate(ND):
            a.op("vector", (lambda e, i=i, e0=e0, nn=nn: e.tensor_copy(
                u["rowbuf"][:1, LP + e0:LP + e0 + nn], PR[i][:, :nn])),
                reads=[("PSM", id(PR[i]))], writes=[("lrow", i)])
        a.op("vector", lambda e: e.reciprocal(u["rowbuf"][:1, LP:LP + D],
                                              u["rowbuf"][:1, LP:LP + D]),
             reads=[("lrow", i) for i in range(len(ND))], writes=[("linvrow",)])
        for j in range(KC):
            a.op("tensor", (lambda e, j=j: e.transpose(
                pt[:, j:j + 1],
                u["rowbuf"][:1, LP + 128 * j:LP + 128 * (j + 1)], u["id32"][:1, :1])),
                reads=[("linvrow",), "c"], writes=[("PSM", id(PS[3])), ("ltr", j)], tag="ltr")
        a.op("vector", lambda e: e.tensor_copy(linv_col[:, :], pt[:, :KC]),
             reads=[("PSM", id(PS[3]))] + [("ltr", j) for j in range(KC)], writes=[("linv",)])
        a.phase()

        r2s.close()
        r2b = contextlib.ExitStack()
        v_sb = sb("v_sb", [128, KC * T], BF16, r2)
        attn_sb = sb("attn_sb", [128, KC * T], BF16, r2b, side="right")
        for k in range(KC):
            a.dma("sync", v_sb[:, k * T:(k + 1) * T], v_dram[128 * k:128 * (k + 1), :],
                  reads=[("v_d", k)], writes=[("vre", k)])
        pexp_keys = [("pexp", k) for k in range(KC)]
        vre_keys = [("vre", k) for k in range(KC)]
        groups = [(m, n0, nn) for m in range(KC) for (n0, nn) in NT]
        q_mm(a, "avmm", groups,
             lambda g, k: p_sbT[:, k * D + g[0] * 128:k * D + (g[0] + 1) * 128],
             lambda g, k: v_sb[:, k * T + g[1]:k * T + g[1] + g[2]],
             KC, PS[0:2],
             lambda a_, g, ps, pk: a_.op(
                 "vector",
                 (lambda e, g=g, ps=ps: e.tensor_scalar(
                     attn_sb[:, g[0] * T + g[1]:g[0] * T + g[1] + g[2]],
                     ps[:, :g[2]], linv_col[:, g[0]:g[0] + 1], None, OP.mult)),
                 reads=[pk, ("linv",)], writes=[("attn", g[0], g[1])]),
             lambda g: pexp_keys + vre_keys,
             lambda g, ps: ps[:, :g[2]])
        _drain_dmas(a)
        a.phase()
        r2.close()

        x1p_dram = nc.dram_tensor("x1p", [D, T], F32)
        r3 = contextlib.ExitStack()
        w2_sb = sb("w2_sb", [128, KC * D], BF16, r3)
        xstg = sb("xstg", [128, 4 * 512], F32, r3)
        q_stream_weight(a, wo_d, w2_sb, KC, D, stage, "po", "pow")
        pow_keys = [("pow", k) for k in range(KC)]
        attn_keys = [("attn", m, n0) for m in range(KC) for (n0, _) in NT]

        def resid_evac(tag, srcdram, dstdram):
            def cb(a_, g, ps, pk):
                m, n0, nn = g
                gi = (m * len(NT)) + [x[0] for x in NT].index(n0)
                sl_in = xstg[:, (gi % 2) * 512:(gi % 2) * 512 + nn]
                sl_out = xstg[:, (2 + gi % 2) * 512:(2 + gi % 2) * 512 + nn]
                a_.dma("scalar", sl_in, srcdram[128 * m:128 * (m + 1), n0:n0 + nn],
                       reads=[(tag, "xin", gi % 2)], writes=[(tag, "xld", gi % 2)])
                a_.op("vector", (lambda e: e.tensor_tensor(sl_out, sl_in, ps[:, :nn],
                                                           OP.add)),
                      reads=[pk, (tag, "xld", gi % 2), (tag, "xst", gi % 2)],
                      writes=[(tag, "xin", gi % 2), (tag, "xout", gi % 2)])
                a_.dma("scalar", dstdram[128 * m:128 * (m + 1), n0:n0 + nn], sl_out,
                       reads=[(tag, "xout", gi % 2)],
                       writes=[(tag, "xst", gi % 2), (tag, "done", m, n0)])
            return cb
        q_mm(a, "pomm", groups,
             lambda g, k: w2_sb[:, k * D + g[0] * 128:k * D + (g[0] + 1) * 128],
             lambda g, k: attn_sb[:, k * T + g[1]:k * T + g[1] + g[2]],
             KC, PS[0:2], resid_evac("rp", xT, x1p_dram),
             lambda g: attn_keys + pow_keys,
             lambda g, ps: ps[:, :g[2]])
        _drain_dmas(a)
        a.phase()
        r2b.close()

        r4 = contextlib.ExitStack()
        xq = sb("xq", [128, KC * T], BF16, r4)
        sqs = sb("sqs", [128, 2 * T], BF16, r4)
        WT = max(T, D // 2)
        macc = sb("macc", [128, WT], F32, r4)
        zstg = sb("zstg", [128, WT], F32, r4)
        amax_col = sb("amax_col", [128, TC], F32, r4)
        wsc_col = sb("wsc_col", [128, 1], F32, r4)
        rnd = Rounder(a, lambda n, shp, dt: sb(n, shp, dt, r4), "rd", T)
        minc = rnd.f1

        for k in range(KC):
            a.op("scalar", (lambda e, k=k: e.activation(
                sqs[:, (k % 2) * T:(k % 2 + 1) * T], xn[:, k * T:(k + 1) * T],
                AF.Square)),
                reads=[("xn", k)],
                writes=[("rsqslot", k % 2)])
            for i, (n0, nn) in enumerate(NT):
                a.op("tensor", (lambda e, k=k, i=i, n0=n0, nn=nn: e.matmul(
                    PR[i][:, :nn], u["ones_col"][:, :],
                    sqs[:, (k % 2) * T + n0:(k % 2) * T + n0 + nn],
                    start=(k == 0), stop=(k == KC - 1))),
                    reads=[("rsqslot", k % 2), "c"], writes=[("PSM", id(PR[i])), ("rmsps", i)])
        for i, (n0, nn) in enumerate(NT):
            a.op("vector", (lambda e, i=i, n0=n0, nn=nn: e.tensor_scalar(
                u["rowbuf"][:1, n0:n0 + nn], PR[i][:, :nn], 1.0 / D, None, OP.mult)),
                reads=[("PSM", id(PR[i])), ("rmsps", i)], writes=[("rmsrow", i)])
        a.op("vector", lambda e: e.tensor_scalar(u["rowbuf"][:1, 0:T], u["rowbuf"][:1, 0:T],
                                                 1e-6, None, OP.add),
             reads=[("rmsrow", i) for i in range(len(NT))], writes=[("rmsrow",)])
        a.op("scalar", lambda e: e.activation(u["rowbuf"][:1, 0:T], u["rowbuf"][:1, 0:T],
                                              AF.Sqrt),
             reads=[("rmsrow",)], writes=[("rmsrow",)])
        a.op("vector", lambda e: e.reciprocal(u["rowbuf"][:1, 0:T], u["rowbuf"][:1, 0:T]),
             reads=[("rmsrow",)], writes=[("rmsrow",)])
        q_bcast_row(a, u, u["rowbuf"][:1, 0:T], ("rmsrow",), u["rs_b"], "rmsb", T, "brm")
        for k in range(KC):
            a.op("vector", (lambda e, k=k: e.tensor_scalar(
                zstg[:, :T], xn[:, k * T:(k + 1) * T], gam[:, k:k + 1], None, OP.mult)),
                reads=[("xn", k), "gam", ("ZM",)], writes=[("ZM",)])
            a.op("vector", (lambda e, k=k: e.tensor_tensor(
                zstg[:, :T], zstg[:, :T], u["rs_b"][:, :T], OP.mult)),
                reads=[("ZM",), "rmsb"], writes=[("ZM",)])
            if k == 0:
                a.op("vector", lambda e: e.tensor_copy(macc[:, :T], zstg[:, :T]),
                     reads=[("ZM",)], writes=[("macc",)])
                a.op("vector", lambda e: e.tensor_scalar(minc[:, :T], zstg[:, :T],
                                                         -1.0, None, OP.mult),
                     reads=[("ZM",)], writes=[("mincc",), ("rd", "f1")])
            else:
                a.op("vector", (lambda e: e.tensor_tensor(
                    macc[:, :T], macc[:, :T], zstg[:, :T], OP.max)),
                    reads=[("macc",), ("ZM",)], writes=[("macc",)])
                a.op("vector", (lambda e: e.scalar_tensor_tensor(
                    minc[:, :T], zstg[:, :T], -1.0, minc[:, :T], OP.mult, OP.max)),
                    reads=[("mincc",), ("ZM",)], writes=[("mincc",), ("rd", "f1")])
        a.op("vector", lambda e: e.tensor_tensor(macc[:, :T], macc[:, :T], minc[:, :T],
                                                 OP.max),
             reads=[("macc",), ("mincc",)], writes=[("macc",)])
        for t in range(TC):
            a.op("tensor", (lambda e, t=t: e.transpose(
                PS[3][:, :128], macc[:, t * 128:(t + 1) * 128], u["id32"][:, :])),
                reads=[("macc",), "c"],
                writes=[("PSM", id(PS[3]))])
            a.op("vector", (lambda e, t=t: e.reduce_max(
                amax_col[:, t:t + 1], PS[3][:, :128], axis=AX.X)),
                reads=[("PSM", id(PS[3]))], writes=[("amx", t)])
        for t in range(TC):
            a.op("tensor", (lambda e, t=t: e.transpose(
                PS[3][:1, :128], amax_col[:, t:t + 1], u["id32"][:, :])),
                reads=[("amx", t)],
                writes=[("PSM", id(PS[3]))])
            a.op("vector", (lambda e, t=t: e.tensor_copy(
                u["rowbuf"][:1, T + t * 128:T + (t + 1) * 128], PS[3][:1, :128])),
                reads=[("PSM", id(PS[3]))], writes=[("arow", t)])
        a.op("vector", lambda e: e.tensor_scalar(u["rowbuf"][:1, T:2 * T],
                                                 u["rowbuf"][:1, T:2 * T], 1e-5, 1.0 / 127.0,
                                                 OP.add, OP.mult),
             reads=[("arow", t) for t in range(TC)], writes=[("aprow",)])
        a.op("vector", lambda e: e.reciprocal(u["rowbuf"][:1, 0:T], u["rowbuf"][:1, T:2 * T]),
             reads=[("aprow",)], writes=[("ascrow",)])
        q_bcast_row(a, u, u["rowbuf"][:1, 0:T], ("ascrow",), u["mu_b"], "ascb", T, "bas")
        for k in range(KC):
            a.op("vector", (lambda e, k=k: e.tensor_scalar(
                zstg[:, :T], xn[:, k * T:(k + 1) * T], gam[:, k:k + 1], None, OP.mult)),
                reads=[("xn", k), "gam", ("ZM",)], writes=[("ZM",)])
            a.op("vector", (lambda e: e.tensor_tensor(
                zstg[:, :T], zstg[:, :T], u["rs_b"][:, :T], OP.mult)),
                reads=[("ZM",), "rmsb"], writes=[("ZM",)])
            a.op("vector", (lambda e: e.tensor_tensor(
                zstg[:, :T], zstg[:, :T], u["mu_b"][:, :T], OP.mult)),
                reads=[("ZM",), "ascb"], writes=[("ZM",)])
            rnd.round(a, zstg[:, :T], xq[:, k * T:(k + 1) * T], T, ("ZM",), ("xq", k))
        _drain_dmas(a)
        a.phase()

        wacc = sb("wacc", [128, 2], F32, r4)
        for k in range(KC):
            a.dma("sync", stage[:, (k % 2) * D:(k % 2 + 1) * D],
                  wf_d[128 * k:128 * (k + 1), :],
                  reads=[("wabs", k - 2)] if k >= 2 else (), writes=[("wfld", k)])
            a.op("vector", (lambda e, k=k: e.reduce_sum(
                wacc[:, 1:2], stage[:, (k % 2) * D:(k % 2 + 1) * D], axis=AX.X,
                apply_absolute_value=True)),
                reads=[("wfld", k)], writes=[("wabs1",), ("wabs", k)])
            if k == 0:
                a.op("vector", lambda e: e.tensor_copy(wacc[:, 0:1], wacc[:, 1:2]),
                     reads=[("wabs1",)], writes=[("wacc",)])
            else:
                a.op("vector", lambda e: e.tensor_tensor(wacc[:, 0:1], wacc[:, 0:1],
                                                         wacc[:, 1:2], OP.add),
                     reads=[("wabs1",), ("wacc",)], writes=[("wacc",)])
        a.op("tensor", lambda e: e.transpose(PS[3][:1, :128], wacc[:, 0:1], u["id32"][:, :]),
             reads=[("wacc",), "c"], writes=[("PSM", id(PS[3]))])
        a.op("vector", lambda e: e.reduce_sum(u["rowbuf"][:1, 0:1], PS[3][:1, :128],
                                              axis=AX.X),
             reads=[("PSM", id(PS[3]))], writes=[("wsum",)])
        a.op("vector", lambda e: e.tensor_scalar(u["rowbuf"][:1, 0:1], u["rowbuf"][:1, 0:1],
                                                 1.0 / (D * D), 1e-5, OP.mult, OP.add),
             reads=[("wsum",)], writes=[("wsum",)])
        a.op("vector", lambda e: e.reciprocal(u["rowbuf"][:1, 0:1], u["rowbuf"][:1, 0:1]),
             reads=[("wsum",)], writes=[("wsum",)])
        a.op("vector", lambda e: e.tensor_copy(u["rowbf"][:1, 0:1], u["rowbuf"][:1, 0:1]),
             reads=[("wsum",)], writes=[("wsumbf",)])
        a.op("tensor", lambda e: e.matmul(PS[3][:, 0:1], u["ones_row"][:, :],
                                          u["rowbf"][:1, 0:1], start=True, stop=True),
             reads=[("wsumbf",), "c"], writes=[("PSM", id(PS[3]))])
        a.op("vector", lambda e: e.tensor_copy(wsc_col[:, :], PS[3][:, 0:1]),
             reads=[("PSM", id(PS[3]))], writes=[("wsc",)])
        _drain_dmas(a)
        a.phase()

        def tern_quant(a_, src, dst, k, rkey, wkey):
            for h in range(2):
                hs = slice(h * (D // 2), (h + 1) * (D // 2))
                W = D // 2
                a_.op("vector", (lambda e, src=src, hs=hs: e.tensor_scalar(
                    zstg[:, :W], src[:, hs], wsc_col[:, 0:1], None, OP.mult)),
                    reads=[rkey, ("wsc",), ("ZM",), ("MC",)], writes=[("ZM",)])
                a_.op("vector", (lambda e, hs=hs: e.tensor_scalar(
                    macc[:, :W], zstg[:, :W], 0.5, None, OP.is_ge)),
                    reads=[("ZM",)], writes=[("MC",)])
                a_.op("vector", (lambda e, hs=hs: e.tensor_scalar(
                    zstg[:, :W], zstg[:, :W], -0.5, None, OP.is_le)),
                    reads=[("ZM",)], writes=[("ZM",)])
                a_.op("vector", (lambda e, dst=dst, hs=hs: e.tensor_tensor(
                    dst[:, hs], macc[:, :W], zstg[:, :W], OP.subtract)),
                    reads=[("ZM",), ("MC",)], writes=[wkey if h else (wkey, "h0")])
        q_stream_weight(a, wf_d, w2_sb, KC, D, stage, "wf", "wfq", quant=tern_quant)
        wfq_keys = [("wfq", k) for k in range(KC)]
        xq_keys = [("xq", k) for k in range(KC)]
        groups = [(m, n0, nn) for m in range(KC) for (n0, nn) in NT]

        q_bcast_row(a, u, u["rowbuf"][:1, T:2 * T], ("aprow",), u["rs_b"], "sob", T, "bso")
        winv = sb("winv", [128, 1], F32, r4)
        a.op("vector", lambda e: e.reciprocal(winv[:, :], wsc_col[:, :]),
             reads=[("wsc",)], writes=[("winv",)])

        def ffn_evac(a_, g, ps, pk):
            m, n0, nn = g
            gi = (m * len(NT)) + [x[0] for x in NT].index(n0)
            sl_in = xstg[:, (gi % 2) * 512:(gi % 2) * 512 + nn]
            sl_out = xstg[:, (2 + gi % 2) * 512:(2 + gi % 2) * 512 + nn]
            a_.dma("scalar", sl_in, x1p_dram[128 * m:128 * (m + 1), n0:n0 + nn],
                   reads=[("rp", "done", m, n0), ("ff", "xin", gi % 2)],
                   writes=[("ff", "xld", gi % 2)])
            a_.op("vector", (lambda e: e.tensor_scalar(
                ps[:, :nn], ps[:, :nn], winv[:, 0:1], None, OP.mult)),
                reads=[pk, ("winv",)], writes=[pk])
            a_.op("vector", (lambda e: e.tensor_tensor(
                ps[:, :nn], ps[:, :nn], u["rs_b"][:, n0:n0 + nn], OP.mult)),
                reads=[pk, "sob"], writes=[pk])
            a_.op("vector", (lambda e: e.tensor_tensor(sl_out, sl_in, ps[:, :nn],
                                                       OP.add)),
                  reads=[pk, ("ff", "xld", gi % 2), ("ff", "xst", gi % 2)],
                  writes=[("ff", "xin", gi % 2), ("ff", "xout", gi % 2)])
            a_.dma("scalar", x1T[128 * m:128 * (m + 1), n0:n0 + nn], sl_out,
                   reads=[("ff", "xout", gi % 2)],
                   writes=[("ff", "xst", gi % 2), "OUT"])
        q_mm(a, "ffmm", groups,
             lambda g, k: w2_sb[:, k * D + g[0] * 128:k * D + (g[0] + 1) * 128],
             lambda g, k: xq[:, k * T + g[1]:k * T + g[1] + g[2]],
             KC, PS[0:2], ffn_evac,
             lambda g: xq_keys + wfq_keys,
             lambda g, ps: ps[:, :g[2]])
        a.finish()
        a.phase()
        r4.close()
        r3.close()
    return nc


LN2 = 0.6931471805599453


class Fp8Quant:
    NI = 2
    MAGIC = 12582912.0

    def __init__(self, a, sbf, name, W):
        self.nm = name
        n = self.NI
        self.t = sbf(name + "_t", [128, n * W], F32)
        self.eb = sbf(name + "_eb", [128, n * W], I32)
        self.ef = sbf(name + "_ef", [128, n * W], F32)
        self.e2 = sbf(name + "_e2", [128, n * W], F32)
        self.qi = sbf(name + "_qi", [128, n * W], F32)
        self.W = W

    def emit(self, a, src_ap, dst_ap, n, skey, dkey, sinv_col, sinv_key):
        self.emit_jobs(a, [(src_ap, dst_ap, n, skey, dkey, sinv_col, sinv_key)])

    def emit_jobs(self, a, jobs):
        nm = self.nm
        W = self.W
        assert len(jobs) <= self.NI

        def bufs(j, n):
            o = j * W
            return (self.t[:, o:o + n], self.eb[:, o:o + n], self.ef[:, o:o + n],
                    self.e2[:, o:o + n], self.qi[:, o:o + n])

        for j, (src_ap, dst_ap, n, skey, dkey, sinv_col, sinv_key) in enumerate(jobs):
            t, eb, ef, e2, qi = bufs(j, n)
            a.op("vector", (lambda e, t=t, src_ap=src_ap, sinv_col=sinv_col:
                            e.tensor_scalar(t, src_ap, sinv_col, None, OP.mult)),
                 reads=[skey, sinv_key, (nm, "t", j)], writes=[(nm, "t", j)])
        for j, (src_ap, dst_ap, n, skey, dkey, sinv_col, sinv_key) in enumerate(jobs):
            t, eb, ef, e2, qi = bufs(j, n)
            a.op("vector", (lambda e, t=t, eb=eb: e.tensor_scalar(
                eb, t.bitcast(I32), 23, 255,
                OP.logical_shift_right, OP.bitwise_and)),
                 reads=[(nm, "t", j)], writes=[(nm, "eb", j)])
        for j, (src_ap, dst_ap, n, skey, dkey, sinv_col, sinv_key) in enumerate(jobs):
            t, eb, ef, e2, qi = bufs(j, n)
            a.op("vector", (lambda e, ef=ef, eb=eb: e.tensor_copy(ef, eb)),
                 reads=[(nm, "eb", j)], writes=[(nm, "ef", j)])
        for j, (src_ap, dst_ap, n, skey, dkey, sinv_col, sinv_key) in enumerate(jobs):
            t, eb, ef, e2, qi = bufs(j, n)
            a.op("vector", (lambda e, ef=ef: e.tensor_scalar(
                ef, ef, 121.0, None, OP.max)),
                 reads=[(nm, "ef", j)], writes=[(nm, "efc", j)])
        for j, (src_ap, dst_ap, n, skey, dkey, sinv_col, sinv_key) in enumerate(jobs):
            t, eb, ef, e2, qi = bufs(j, n)
            a.op("vector", (lambda e, ef=ef, e2=e2: e.tensor_scalar(
                e2, ef, -130.0, -LN2, OP.add, OP.mult)),
                 reads=[(nm, "efc", j)], writes=[(nm, "e2", j)])
        for j, (src_ap, dst_ap, n, skey, dkey, sinv_col, sinv_key) in enumerate(jobs):
            t, eb, ef, e2, qi = bufs(j, n)
            a.op("scalar", (lambda e, e2=e2, qi=qi: e.activation(qi, e2, AF.Exp)),
                 reads=[(nm, "e2", j)], writes=[(nm, "qi", j)])
        for j, (src_ap, dst_ap, n, skey, dkey, sinv_col, sinv_key) in enumerate(jobs):
            t, eb, ef, e2, qi = bufs(j, n)
            a.op("vector", (lambda e, t=t, qi=qi: e.tensor_tensor(t, t, qi, OP.mult)),
                 reads=[(nm, "t", j), (nm, "qi", j)], writes=[(nm, "t", j)])
        for j, (src_ap, dst_ap, n, skey, dkey, sinv_col, sinv_key) in enumerate(jobs):
            t, eb, ef, e2, qi = bufs(j, n)
            a.op("vector", (lambda e, t=t: e.tensor_scalar(
                t, t, self.MAGIC, None, OP.add)),
                 reads=[(nm, "t", j)], writes=[(nm, "t", j)])
        for j, (src_ap, dst_ap, n, skey, dkey, sinv_col, sinv_key) in enumerate(jobs):
            t, eb, ef, e2, qi = bufs(j, n)
            a.op("vector", (lambda e, t=t: e.tensor_scalar(
                t, t, -self.MAGIC, None, OP.add)),
                 reads=[(nm, "t", j)], writes=[(nm, "t", j)])
        for j, (src_ap, dst_ap, n, skey, dkey, sinv_col, sinv_key) in enumerate(jobs):
            t, eb, ef, e2, qi = bufs(j, n)
            a.op("vector", (lambda e, ef=ef, e2=e2: e.tensor_scalar(
                e2, ef, -130.0, LN2, OP.add, OP.mult)),
                 reads=[(nm, "efc", j), (nm, "qi", j)], writes=[(nm, "e2", j)])
        for j, (src_ap, dst_ap, n, skey, dkey, sinv_col, sinv_key) in enumerate(jobs):
            t, eb, ef, e2, qi = bufs(j, n)
            a.op("scalar", (lambda e, e2=e2, qi=qi: e.activation(qi, e2, AF.Exp)),
                 reads=[(nm, "e2", j)], writes=[(nm, "qi", j)])
        for j, (src_ap, dst_ap, n, skey, dkey, sinv_col, sinv_key) in enumerate(jobs):
            t, eb, ef, e2, qi = bufs(j, n)
            dk = dkey if isinstance(dkey, list) else [dkey]
            a.op("vector", (lambda e, t=t, qi=qi, dst_ap=dst_ap:
                            e.tensor_tensor(dst_ap, t, qi, OP.mult)),
                 reads=[(nm, "t", j), (nm, "qi", j)], writes=dk)


def q_scalar_bcast(a, u, src_row_ap, srckey, dst_col, dstkey, ps, neg=False):
    a.op("tensor", lambda e: e.matmul(ps[:, 0:1], u["ones_row32"][:, :],
                                      src_row_ap, start=True, stop=True),
         reads=[srckey, "c"], writes=[("PSM", id(ps))])
    a.op("vector", lambda e: e.tensor_scalar(dst_col, ps[:, 0:1],
                                             -1.0 if neg else 1.0, None, OP.mult),
         reads=[("PSM", id(ps))], writes=[dstkey])


def mk_resid_evac(a, xstg, NT, tag, srcdram, dstdram, srckey_fn, out_key):
    def cb(a_, g, ps, pk):
        m, n0, nn = g
        gi = (m * len(NT)) + [x[0] for x in NT].index(n0)
        sl_in = xstg[:, (gi % 2) * 512:(gi % 2) * 512 + nn]
        sl_out = xstg[:, (2 + gi % 2) * 512:(2 + gi % 2) * 512 + nn]
        rd = [(tag, "xin", gi % 2)]
        sk = srckey_fn(m, n0) if srckey_fn else None
        if sk:
            rd.append(sk)
        a_.dma("scalar", sl_in, srcdram[128 * m:128 * (m + 1), n0:n0 + nn],
               reads=rd, writes=[(tag, "xld", gi % 2)])
        a_.op("vector", (lambda e: e.tensor_tensor(sl_out, sl_in, ps[:, :nn], OP.add)),
              reads=[pk, (tag, "xld", gi % 2), (tag, "xst", gi % 2)],
              writes=[(tag, "xin", gi % 2), (tag, "xout", gi % 2)])
        a_.dma("scalar", dstdram[128 * m:128 * (m + 1), n0:n0 + nn], sl_out,
               reads=[(tag, "xout", gi % 2)],
               writes=[(tag, "xst", gi % 2), (out_key, m, n0)])
    return cb


def build_l1(S, ctx=None):
    T = S // NCORES
    TC = T // 128
    NB = S // 128
    NT = nspans(T)
    ND = nspans(D)
    FD = 4 * D
    WSL = FD // NCORES
    PSL = D // NCORES
    nc = _NCPfx(ctx["nc"], ctx["pfx"]) if ctx else bass.Bass("TRN2", target_bir_lowering=False)
    xT = ctx["x_in"] if ctx else nc.dram_tensor("xT", [D, T], F32, kind="ExternalInput").ap()
    wq_d = nc.dram_tensor("wq", [D, D], F32, kind="ExternalInput").ap()
    wk_d = nc.dram_tensor("wk", [D, 512], F32, kind="ExternalInput").ap()
    wv_d = nc.dram_tensor("wv", [D, 512], F32, kind="ExternalInput").ap()
    wo_d = nc.dram_tensor("wo", [D, D], F32, kind="ExternalInput").ap()
    wfcs_d = nc.dram_tensor("wfcs", [D, WSL], F32, kind="ExternalInput").ap()
    wpjs_d = nc.dram_tensor("wpjs", [FD, PSL], F32, kind="ExternalInput").ap()
    cs_d = nc.dram_tensor("cs", [128, S], F16, kind="ExternalInput").ap()
    slopes_d = nc.dram_tensor("slopes", [1, 2], F32, kind="ExternalInput").ap()
    cmask_d = nc.dram_tensor("cmask", [128, 4 * 512], BF16, kind="ExternalInput").ap()
    cdram = ctx["cdram"] if ctx else _mk_cdram(nc)
    x2T = ctx["x_out"] if ctx else nc.dram_tensor("x2T", [D, T], F32, kind="ExternalOutput").ap()
    qa_i = nc.dram_tensor("qa_i", [D, T], F16)
    qa_o = nc.dram_tensor("qa_o", [D, T], F16)
    ka_i = nc.dram_tensor("ka_i", [D, T], F16)
    ka_o = nc.dram_tensor("ka_o", [D, T], F16)
    va_i = nc.dram_tensor("va_i", [S, 256], F16)
    va_o = nc.dram_tensor("va_o", [S, 256], F16)
    ya_i = nc.dram_tensor("ya_i", [D, T], F16)
    ldbg = nc.dram_tensor("ldbg", [2 * (S // 128) + 2, 512], F32)
    ya_o = nc.dram_tensor("ya_o", [D, T], F16)
    mx_i = nc.dram_tensor("mx_i", [1, 4], F32)
    mx_o = nc.dram_tensor("mx_o", [1, 4], F32)
    hx_i = nc.dram_tensor("hx_i", [1, 1], F32)
    hx_o = nc.dram_tensor("hx_o", [1, 1], F32)
    w8fc_p = nc.dram_tensor("w8fc_p", [D, WSL], BF16)
    w8fc_g = nc.dram_tensor("w8fc_g", [NCORES * D, WSL], BF16)
    w8pj_p = nc.dram_tensor("w8pj_p", [FD, PSL], BF16)
    w8pj_g = nc.dram_tensor("w8pj_g", [NCORES * FD, PSL], BF16)
    h_d = nc.dram_tensor("h_d", [FD, T], F32)
    x2p_d = nc.dram_tensor("x2p", [D, T], F32)
    xn_d = nc.dram_tensor("xn_d", [D, T], F16)
    xn32_d = nc.dram_tensor("xn32_d", [D, T], F32)

    es = contextlib.ExitStack()
    with es:
        a = ctx["a"] if ctx else Auto(nc, es)
        sb = lambda n, shp, dt, st=es, side=None: st.enter_context(
            nc.sbuf_tensor(n, shp, dt, side=side))
        u = mk_utils(a, nc, es, T)
        PS = [es.enter_context(nc.psum_tensor(f"PS{i}", [128, 512], F32)) for i in range(5)]
        PR = [es.enter_context(nc.psum_tensor("PR0", [1, 512], F32))]
        PSC = es.enter_context(nc.psum_tensor("PSC", [128, 512], BF16))
        PSC16 = es.enter_context(nc.psum_tensor("PSC16", [128, 128], F16))
        u["pstat"] = [PS[0], PS[1], PS[2], PR[0]]
        u["psb"] = PS[3:5]
        u["idf16"] = sb("u_idf16", [128, 128], F16)
        stage = sb("stage", [128, 2 * D], F32)
        iota_f = sb("iota_f", [128, 1], F32)
        iota_i = sb("iota_i", [128, 1], I32)
        slope_col = sb("slope_col", [128, 2], F32)
        scl = sb("scl", [128, 8], F32)
        srow = sb("srow", [1, 24], F32)
        r0x = contextlib.ExitStack()
        xn = sb("xn", [128, KC * T], F16, r0x)

        r0 = contextlib.ExitStack()
        x_sb = sb("x_sb", [128, KC * T], F32, r0)
        u["xbf"] = sb("xbf", [128, 2 * T], BF16, r0)
        u["sqbf"] = sb("sqbf", [128, 2 * T], BF16, r0)
        q_load_consts(a, u, cdram)
        a.op("vector", lambda e: e.tensor_copy(u["idf16"][:, :], u["id32"][:, :]),
             reads=[("c", "id32")], writes=[("c", "idf16")])
        a.op("vector", lambda e: e.memset(srow[:1, :], 0.0), writes=[("srow0",)])
        a.op("gpsimd", lambda e: e.iota(iota_i[:, :], [[1, 1]], base=0, channel_multiplier=1),
             writes=[("iota_i",)])
        a.op("vector", lambda e: e.tensor_copy(iota_f[:, :], iota_i[:, :]),
             reads=[("iota_i",)], writes=[("iota",)])
        q_load_act(a, xT, x_sb, T, "x")
        q_layernorm(a, u, T, x_sb, "x", xn, "xn", keep32=True)
        for k in range(KC):
            a.dma("sync", xn32_d[128 * k:128 * (k + 1), :], x_sb[:, k * T:(k + 1) * T],
                  reads=[(("xn", "32", k))], writes=[("xn32_d", k)])
        _drain_dmas(a)
        a.phase()
        r0.close()
        XN = [("xn", k) for k in range(KC)]

        r1 = contextlib.ExitStack()
        w_sb = sb("w_sb", [128, KC * D], F16, r1)
        qstg = sb("qstg", [128, 2 * T], F16, r1)
        q_stream_weight(a, wq_d, w_sb, KC, D, stage, "pq", "pqw")
        pqw_keys = [("pqw", k) for k in range(KC)]
        qgroups = [(m, n0, nn) for m in range(KC) for (n0, nn) in NT]

        def q_evac(a_, g, ps, pk):
            m, n0, nn = g
            slot = m % 2
            a_.op("vector", (lambda e: e.tensor_copy(
                qstg[:, slot * T + n0:slot * T + n0 + nn], ps[:, :nn])),
                reads=[pk], writes=[("qstg", slot, n0)])
            if n0 + nn >= T:
                a_.dma("scalar", qa_i[128 * m:128 * (m + 1), :],
                       qstg[:, slot * T:slot * T + T],
                       reads=[("qstg", slot, x0) for (x0, _) in NT],
                       writes=[("qa_i", m)])
        q_mm(a, "pqmm", qgroups,
             lambda g, k: w_sb[:, k * D + g[0] * 128:k * D + (g[0] + 1) * 128],
             lambda g, k: xn[:, k * T + g[1]:k * T + g[1] + g[2]],
             KC, PS[0:2], q_evac, lambda g: XN + pqw_keys,
             lambda g, ps: ps[:, :g[2]])
        for k in range(KC):
            a.op("vector", (lambda e, k=k: e.reduce_max(
                scl[:, 1:2], xn[:, k * T:(k + 1) * T], axis=AX.X,
                apply_absolute_value=True)),
                reads=[("xn", k)], writes=[("xm1",)])
            if k == 0:
                a.op("vector", lambda e: e.tensor_copy(scl[:, 0:1], scl[:, 1:2]),
                     reads=[("xm1",)], writes=[("xmacc",)])
            else:
                a.op("vector", lambda e: e.tensor_tensor(scl[:, 0:1], scl[:, 0:1],
                                                         scl[:, 1:2], OP.max),
                     reads=[("xm1",), ("xmacc",)], writes=[("xmacc",)])
        a.op("tensor", lambda e: e.transpose(PS[2][:1, :128], scl[:, 0:1], u["id32"][:, :]),
             reads=[("xmacc",), "c"], writes=[("PSM", id(PS[2]))])
        a.op("vector", lambda e: e.reduce_max(srow[:1, 0:1], PS[2][:1, :128], axis=AX.X),
             reads=[("PSM", id(PS[2])), ("srow0",)], writes=[("sxl",)])
        a.phase()

        kv_sb = sb("kv_sb", [128, 4 * T], F16, r1)
        vt_sb = sb("vt_sb", [128, TC * 512], F16, r1)
        q_stream_weight(a, wk_d, w_sb, KC, 512, stage, "pk", "pkw")
        kgroups = [(m, n0, nn) for m in range(4) for (n0, nn) in NT]
        q_mm(a, "pkmm", kgroups,
             lambda g, k: w_sb[:, k * 512 + g[0] * 128:k * 512 + (g[0] + 1) * 128],
             lambda g, k: xn[:, k * T + g[1]:k * T + g[1] + g[2]],
             KC, PS[0:2],
             lambda a_, g, ps, pk: a_.op(
                 "vector",
                 (lambda e, g=g, ps=ps: e.tensor_copy(
                     kv_sb[:, g[0] * T + g[1]:g[0] * T + g[1] + g[2]], ps[:, :g[2]])),
                 reads=[pk], writes=[("kf", g[0], g[1])]),
             lambda g: XN + [("pkw", k) for k in range(KC)],
             lambda g, ps: ps[:, :g[2]])
        q_stream_weight(a, wv_d, w_sb, KC, 512, stage, "pv", "pvw")
        vgroups = [(t, n0, nn) for t in range(TC) for (n0, nn) in nspans(512)]
        q_mm(a, "pvmm", vgroups,
             lambda g, k: xn[:, k * T + g[0] * 128:k * T + (g[0] + 1) * 128],
             lambda g, k: w_sb[:, k * 512 + g[1]:k * 512 + g[1] + g[2]],
             KC, PS[0:2],
             lambda a_, g, ps, pk: a_.op(
                 "vector",
                 (lambda e, g=g, ps=ps: e.tensor_copy(
                     vt_sb[:, g[0] * 512 + g[1]:g[0] * 512 + g[1] + g[2]], ps[:, :g[2]])),
                 reads=[pk], writes=[("vt", g[0], g[1])]),
             lambda g: XN + [("pvw", k) for k in range(KC)],
             lambda g, ps: ps[:, :g[2]])
        kf_all = [("kf", g, n0) for g in range(4) for (n0, _) in NT]
        vt_all = [("vt", t, n0) for t in range(TC) for (n0, _) in nspans(512)]
        for j in range(NCORES):
            for sgi, grp in enumerate((j // 4, 2 + j // 4)):
                a.dma("scalar", ka_i[256 * j + 128 * sgi:256 * j + 128 * (sgi + 1), :],
                      kv_sb[:, grp * T:(grp + 1) * T],
                      reads=kf_all, writes=[("ka_i", j, sgi)])
        for j in range(NCORES):
            for sgi, grp in enumerate((j // 4, 2 + j // 4)):
                a.dma("sync",
                      va_i[j * T:(j + 1) * T,
                           128 * sgi:128 * (sgi + 1)].rearrange(
                          "(t p) c -> p t c", t=TC, p=128),
                      vt_sb[:, :].rearrange("p (t d) -> p t d", t=TC, d=512)[
                          :, :, grp * 128:(grp + 1) * 128],
                      reads=vt_all, writes=[("va_i", j, 0, sgi)])
        def slice_absmax(wdram, kin, M, tagn, out_idx):
            for k in range(kin):
                a.dma("sync", stage[:, (k % 2) * M:(k % 2) * M + M],
                      wdram[128 * k:128 * (k + 1), :],
                      reads=([(tagn, "rd", k - 2)] if k >= 2 else []) + [("STG", id(stage), k % 2)],
                      writes=[(tagn, "ld", k % 2), ("STG", id(stage), k % 2)])
                a.op("vector", (lambda e, k=k: e.reduce_max(
                    scl[:, 3:4], stage[:, (k % 2) * M:(k % 2) * M + M], axis=AX.X,
                    apply_absolute_value=True)),
                    reads=[(tagn, "ld", k % 2), ("STG", id(stage), k % 2)],
                    writes=[(tagn, "m1"), (tagn, "rd", k)])
                if k == 0:
                    a.op("vector", lambda e: e.tensor_copy(scl[:, 2:3], scl[:, 3:4]),
                         reads=[(tagn, "m1")], writes=[(tagn, "acc")])
                else:
                    a.op("vector", lambda e: e.tensor_tensor(scl[:, 2:3], scl[:, 2:3],
                                                             scl[:, 3:4], OP.max),
                         reads=[(tagn, "m1"), (tagn, "acc")], writes=[(tagn, "acc")])
            a.op("tensor", lambda e: e.transpose(PS[2][:1, :128], scl[:, 2:3],
                                                 u["id32"][:, :]),
                 reads=[(tagn, "acc"), "c"], writes=[("PSM", id(PS[2]))])
            a.op("vector", (lambda e: e.reduce_max(
                srow[:1, out_idx:out_idx + 1], PS[2][:1, :128], axis=AX.X)),
                reads=[("PSM", id(PS[2]))], writes=[(tagn, "gmax")])
        slice_absmax(wfcs_d, KC, WSL, "wfm", 1)
        slice_absmax(wpjs_d, FD // 128, PSL, "wpm", 2)
        a.dma("scalar", mx_i[:1, 0:4], srow[:1, 0:4],
              reads=[("sxl",), ("wfm", "gmax"), ("wpm", "gmax"), ("srow0",)], writes=[("mx_i",)])
        a.cc("AllReduce", OP.max, mx_i, mx_o, [("mx_i",)], "mx_o")
        a.cc("AllToAll", OP.bypass, qa_i, qa_o, [("qa_i", m) for m in range(KC)], "qa_o")
        a.cc("AllToAll", OP.bypass, ka_i, ka_o,
             [("ka_i", j, sgi) for j in range(NCORES) for sgi in range(2)], "ka_o")
        a.cc("AllToAll", OP.bypass, va_i, va_o,
             [("va_i", j, 0, sgi) for j in range(NCORES)
              for sgi in range(2)], "va_o")
        q_store_act(a, xn, xn_d, T, "xn", "xn_d", eng="sync")
        _drain_dmas(a)
        a.phase()
        r1.close()
        r0x.close()
        return _l1_attn(nc, es, a, u, sb, PS, PR, locals())
    return nc


def _l1_attn(nc, es, a, u, sb, PS, PR, L):
    PSC = L["PSC"]
    PSC16 = L["PSC16"]
    S, T, TC, NB = L["S"], L["T"], L["TC"], L["NB"]
    NT, ND = L["NT"], L["ND"]
    xn, stage = L["xn"], L["stage"]
    iota_f, slope_col, scl, srow = L["iota_f"], L["slope_col"], L["scl"], L["srow"]
    qa_o, ka_o, va_o, ya_i, ya_o = L["qa_o"], L["ka_o"], L["va_o"], L["ya_i"], L["ya_o"]
    ldbg = L["ldbg"]
    cs_d, slopes_d, cmask_d = L["cs_d"], L["slopes_d"], L["cmask_d"]
    wo_d, x2T, x2p_d, xT = L["wo_d"], L["x2T"], L["x2p_d"], L["xT"]
    mx_o = L.get("mx_o")

    r2 = contextlib.ExitStack()
    qh = sb("qh", [128, 2 * S], F16, r2)
    kh = sb("kh", [128, 2 * S], F16, r2)
    vh = sb("vh", [128, 2 * S], F16, r2)
    cs_sb = sb("cs_sb", [128, S], F16, r2)
    cmask_sb = sb("cmask_sb", [128, 4 * 512], BF16, r2)
    sqt = sb("sqt", [128, 1024], F16, r2)
    bias0 = sb("bias0", [128, 2], F32, r2)
    a.dma("sync", cs_sb[:, :], cs_d, writes=[("cs",)])
    a.dma("sync", cmask_sb[:, :], cmask_d, writes=[("cm",)])
    a.dma("sync", stage[:1, 0:2].bitcast(F32) if False else stage[:1, 0:2], slopes_d,
          writes=[("slp_r",)])
    for r in range(NCORES):
        for h in range(2):
            a.dma("sync", qh[:, h * S + r * T:h * S + (r + 1) * T],
                  qa_o[256 * r + 128 * h:256 * r + 128 * (h + 1), :],
                  reads=["qa_o"], writes=[("qh", h, r)])
            a.dma("sync", kh[:, h * S + r * T:h * S + (r + 1) * T],
                  ka_o[256 * r + 128 * h:256 * r + 128 * (h + 1), :],
                  reads=["ka_o"], writes=[("kh", h, r)])
    for h in range(2):
        a.dma("scalar", vh[:, h * S:(h + 1) * S].rearrange(
                  "p (b c) -> p b c", b=NB, c=128),
              va_o[:, 128 * h:128 * (h + 1)].rearrange(
                  "(b p) c -> p b c", b=NB, p=128),
              reads=["va_o"], writes=[("vh", h, 0)])
    for h in range(2):
        q_scalar_bcast(a, u, stage[:1, h:h + 1], ("slp_r",), slope_col[:, h:h + 1],
                       ("slope", h), PS[2])

    def rope(nm, arena, h, qkeys):
        base = h * S
        A = lambda r0, r1: arena[r0:r1, base:base + S]
        RT = ("RT",)
        a.dma("sync", rtmp[0:64, 0:S], A(64, 128),
              reads=qkeys + [RT], writes=[(nm, "sw1"), RT])
        a.dma("sync", rtmp[64:128, 0:S], A(0, 64),
              reads=qkeys + [RT], writes=[(nm, "sw2"), RT])
        a.op("vector", lambda e: e.tensor_tensor(A(0, 64), A(0, 64),
                                                 cs_sb[0:64, 0:S], OP.mult),
             reads=qkeys + [("cs",), (nm, "sw2")], writes=[(nm, "q1")])
        a.op("vector", lambda e: e.tensor_tensor(rtmp[0:64, 0:S], rtmp[0:64, 0:S],
                                                 cs2_sb[0:64, 0:S], OP.mult),
             reads=[(nm, "sw1"), ("cs2",), RT], writes=[(nm, "p1"), RT])
        a.op("vector", lambda e: e.tensor_tensor(A(0, 64), A(0, 64),
                                                 rtmp[0:64, 0:S], OP.subtract),
             reads=[(nm, "q1"), (nm, "p1"), RT], writes=[(nm, "q1")])
        a.op("vector", lambda e: e.tensor_tensor(A(64, 128), A(64, 128),
                                                 cs2_sb[64:128, 0:S], OP.mult),
             reads=qkeys + [("cs2",), (nm, "sw1")], writes=[(nm, "q2")])
        a.op("vector", lambda e: e.tensor_tensor(rtmp[64:128, 0:S], rtmp[64:128, 0:S],
                                                 cs_sb[64:128, 0:S], OP.mult),
             reads=[(nm, "sw2"), ("cs",), RT], writes=[(nm, "p2"), RT])
        a.op("vector", lambda e: e.tensor_tensor(A(64, 128), A(64, 128),
                                                 rtmp[64:128, 0:S], OP.add),
             reads=[(nm, "q2"), (nm, "p2"), RT], writes=[(nm, "done")])

    NBQ = S // 128
    mcol = sb("mcol", [128, 2 * NBQ], F32, r2)
    nrow = sb("nrow", [1, 512], F32, r2)
    r2r = contextlib.ExitStack()
    rtmp = sb("rtmp", [128, S], F16, r2r)
    cs2_sb = sb("cs2_sb", [128, S], F16, r2r)
    a.dma("sync", cs2_sb[0:64, 0:S], cs_sb[64:128, 0:S],
          reads=[("cs",)], writes=[("cs2",)])
    a.dma("sync", cs2_sb[64:128, 0:S], cs_sb[0:64, 0:S],
          reads=[("cs",)], writes=[("cs2",)])
    for h in range(2):
        qk = [("qh", h, r) for r in range(NCORES)]
        rope(f"rq{h}", qh, h, qk)
        a.op("vector", (lambda e, h=h: e.tensor_scalar(
            qh[:, h * S:(h + 1) * S], qh[:, h * S:(h + 1) * S],
            0.08838834764831845, None, OP.mult)),
            reads=[(f"rq{h}", "q1"), (f"rq{h}", "done")], writes=[(f"rq{h}", "done")])
        kk = [("kh", h, r) for r in range(NCORES)]
        rope(f"rk{h}", kh, h, kk)

    def kmaxnorm(nm, h, dkeys):
        for i, (n0, nn) in enumerate(nspans(S, 1024)):
            a.op("scalar", (lambda e, n0=n0, nn=nn: e.activation(
                sqt[:, 0:nn], kh[:, h * S + n0:h * S + n0 + nn], AF.Square)),
                reads=dkeys + [("SQT",)], writes=[(nm, "sq", i), ("SQT",)])
            for j, (m0, mm) in enumerate(nspans(nn)):
                a.op("tensor", (lambda e, m0=m0, mm=mm: e.matmul(
                    PR[0][:1, :mm], u["ones_col"][:, :], sqt[:, m0:m0 + mm],
                    start=True, stop=True)),
                    reads=[(nm, "sq", i), "c", ("SQT",)], writes=[("PSM", id(PR[0]))])
                a.op("vector", (lambda e: e.reduce_max(
                    scl[:1, 5:6], PR[0][:1, :mm], axis=AX.X)),
                    reads=[("PSM", id(PR[0])), ("SCL5",)],
                    writes=[(nm, "m1"), ("SCL5",)])
                if i == 0 and j == 0:
                    a.op("vector", lambda e: e.tensor_copy(srow[:1, 5:6], scl[:1, 5:6]),
                         reads=[(nm, "m1"), ("srow0",), ("SCL5",)], writes=[(nm, "acc")])
                else:
                    a.op("vector", (lambda e: e.tensor_tensor(
                        srow[:1, 5:6], srow[:1, 5:6], scl[:1, 5:6], OP.max)),
                        reads=[(nm, "m1"), (nm, "acc"), ("SCL5",)], writes=[(nm, "acc")])

    for h in range(2):
        kmaxnorm(f"nk{h}", h, [(f"rk{h}", "done"), (f"rk{h}", "q1")])
        a.op("scalar", (lambda e, h=h: e.activation(srow[:1, 6 + h:7 + h],
                                                    srow[:1, 5:6], AF.Sqrt)),
             reads=[(f"nk{h}", "acc"), ("srow0",)], writes=[("kmax", h)])
        nm = f"nq{h}"
        qdk = [(f"rq{h}", "done")]
        for i, (n0, nn) in enumerate(nspans(S)):
            a.op("scalar", (lambda e, n0=n0, nn=nn: e.activation(
                sqt[:, 0:nn], qh[:, h * S + n0:h * S + n0 + nn], AF.Square)),
                reads=qdk + [("SQT",)], writes=[(nm, "sq", i), ("SQT",)])
            a.op("tensor", (lambda e, nn=nn: e.matmul(
                PR[0][:1, :nn], u["ones_col"][:, :], sqt[:, 0:nn],
                start=True, stop=True)),
                reads=[(nm, "sq", i), "c", ("SQT",)], writes=[("PSM", id(PR[0]))])
            a.op("vector", (lambda e, nn=nn: e.tensor_copy(nrow[:1, :nn], PR[0][:1, :nn])),
                 reads=[("PSM", id(PR[0])), ("NRW",)], writes=[(nm, "r", i), ("NRW",)])
            a.op("scalar", (lambda e, nn=nn: e.activation(nrow[:1, :nn], nrow[:1, :nn],
                                                          AF.Sqrt)),
                 reads=[(nm, "r", i), ("NRW",)], writes=[(nm, "r", i), ("NRW",)])
            a.op("vector", (lambda e, h=h, nn=nn: e.tensor_scalar(
                nrow[:1, :nn], nrow[:1, :nn], srow[:1, 6 + h:7 + h], 1.0,
                OP.mult, OP.add)),
                reads=[(nm, "r", i), ("kmax", h), ("NRW",)], writes=[(nm, "r", i), ("NRW",)])
            for j in range(nn // 128):
                t = (n0 + j * 128) // 128
                a.op("tensor", (lambda e, j=j: e.transpose(
                    PS[4][:, 0:1], nrow[:1, 128 * j:128 * (j + 1)], u["id32"][:1, :1])),
                    reads=[(nm, "r", i), ("NRW",), "c"], writes=[("PSM", id(PS[4]))])
                a.op("vector", (lambda e, t=t, h=h: e.tensor_scalar(
                    mcol[:, h * NBQ + t:h * NBQ + t + 1], PS[4][:, 0:1], -1.0, None,
                    OP.mult)),
                    reads=[("PSM", id(PS[4]))], writes=[("mc", h, t)])
    a.op("vector", lambda e: e.tensor_scalar(slope_col[:, 0:2], slope_col[:, 0:2],
                                             -1.0, None, OP.mult),
         reads=[("slope", 0), ("slope", 1)], writes=[("negsl",)])
    a.phase()
    r2r.close()

    ramp = sb("ramp", [128, 2 * 5 * 512], F32, r2)
    pt_sb = sb("pt", [128, 2 * 512], BF16, r2)
    ptt = sb("ptt", [128, 512], BF16, r2)
    lacc = sb("lacc", [128, 24], F32, r2)
    ystg = sb("ystg", [128, 2 * 128], F16, r2)
    yout = sb("yout", [128, 2 * 128], F16, r2)
    irow = sb("irow", [1, 512], F32, r2)
    irbf = sb("irbf", [1, 512], BF16, r2)
    bcol = sb("bcol", [128, 4], F32, r2)
    basc = sb("basc", [128, 2], F32, r2)
    vh_keys = [[("vh", h, 0)] for h in range(2)]
    NO_RAMP = bool(L.get("no_alibi"))
    a.op("gpsimd", lambda e: e.iota(irow[:1, :].bitcast(I32), [[1, 512]], base=0,
                                    channel_multiplier=0),
         writes=[("irowi",)])
    a.op("vector", lambda e: e.tensor_copy(irow[:1, :], irow[:1, :].bitcast(I32)),
         reads=[("irowi",)], writes=[("irbf",)])
    a.op("tensor", lambda e: e.matmul(PS[4][:, :512], u["ones_row32"][:, :], irow[:1, :],
                                      start=True, stop=True),
         reads=[("irbf",), "c"], writes=[("PSM", id(PS[4]))])
    for h in range(2 if not NO_RAMP else 0):
        a.op("vector", (lambda e, h=h: e.tensor_scalar(
            sqt[:, 0:512], PS[4][:, :512], slope_col[:, h:h + 1], None, OP.mult)),
            reads=[("PSM", id(PS[4])), ("negsl",), ("SQT",)],
            writes=[("rmp0", h), ("SQT",)])
        a.op("vector", (lambda e, h=h: e.tensor_scalar(
            scl[:, 6:7], slope_col[:, h:h + 1], 511.0, None, OP.mult)),
            reads=[("negsl",), ("SCL6",)], writes=[("r511", h), ("SCL6",)])
        a.op("vector", (lambda e, h=h: e.tensor_scalar(
            ramp[:, h * 5 * 512:h * 5 * 512 + 512], sqt[:, 0:512], -1.0,
            scl[:, 6:7], OP.mult, OP.add)),
            reads=[("rmp0", h), ("r511", h), ("SQT",), ("SCL6",)], writes=[("ramp", h, 0)])
        for v in range(4):
            a.op("vector", (lambda e, h=h, v=v: e.tensor_tensor(
                ramp[:, (h * 5 + 1 + v) * 512:(h * 5 + 2 + v) * 512],
                ramp[:, h * 5 * 512:h * 5 * 512 + 512],
                cmask_sb[:, v * 512:(v + 1) * 512], OP.add)),
                reads=[("ramp", h, 0), ("cm",)], writes=[("ramp", h, 1 + v)])

    def flash(h):
        nm = f"fl{h}"
        qdone = (f"rq{h}", "done")
        kdone = [(f"rk{h}", "done"), (f"rk{h}", "q1")]
        noal = bool(L.get("no_alibi"))
        WINB = L.get("winA", 6) if h == 0 else NB + 9
        steps = []
        for t in range(NBQ):
            q0 = t * 128
            kb_hi = q0 // 512
            kb_lo = max(0, kb_hi - WINB)
            for bi, b in enumerate(range(kb_lo, kb_hi + 1)):
                k0 = b * 512
                kw = 512 if k0 + 512 <= q0 else (q0 + 128 - k0)
                var = 0 if k0 + 512 <= q0 else 1 + ((q0 - k0) // 128)
                steps.append(dict(t=t, q0=q0, k0=k0, kw=kw, var=var, bi=bi,
                                  first=(bi == 0), last=(b == kb_hi)))
        n = len(steps)

        def eS(g):
            st = steps[g]
            psx = PS[g % 2]
            a.op("tensor", (lambda e, h=h, q0=st["q0"], k0=st["k0"], kw=st["kw"],
                            psx=psx: e.matmul(
                psx[:, :kw], qh[:, h * S + q0:h * S + q0 + 128],
                kh[:, h * S + k0:h * S + k0 + kw], start=True, stop=True)),
                 reads=[qdone] + kdone, writes=[("PSM", id(psx))], tag=nm)

        def eBase(g):
            st = steps[g]
            if noal or not st["first"]:
                return
            t = st["t"]
            sl = t % 2
            a.op("gpsimd", (lambda e, h=h, sl=sl: e.tensor_scalar(
                basc[:, sl:sl + 1], iota_f[:, :], slope_col[:, h:h + 1],
                None, OP.mult)),
                reads=[("iota",), ("negsl",), ("bau", sl)], writes=[("ba0", sl)])
            a.op("gpsimd", (lambda e, h=h, t=t, sl=sl: e.tensor_tensor(
                basc[:, sl:sl + 1], basc[:, sl:sl + 1],
                mcol[:, h * NBQ + t:h * NBQ + t + 1], OP.add)),
                reads=[("ba0", sl), ("mc", h, t)], writes=[("bas", sl)])

        def ePre(g):
            st = steps[g]
            t, kw, var = st["t"], st["kw"], st["var"]
            psx = PS[g % 2]
            pskey = ("PSM", id(psx))
            if noal:
                if var > 0:
                    a.op("vector", (lambda e, var=var, kw=kw, psx=psx: e.tensor_tensor(
                        psx[:, :kw], psx[:, :kw],
                        cmask_sb[:, (var - 1) * 512:(var - 1) * 512 + kw], OP.add)),
                        reads=[pskey, ("cm",)], writes=[pskey])
                bias_ap = mcol[:, h * NBQ + t:h * NBQ + t + 1]
                bias_keys = [("mc", h, t)]
            else:
                a.op("vector", (lambda e, h=h, var=var, kw=kw, psx=psx: e.tensor_tensor(
                    psx[:, :kw], psx[:, :kw],
                    ramp[:, (h * 5 + var) * 512:(h * 5 + var) * 512 + kw], OP.add)),
                    reads=[pskey, ("ramp", h, var)], writes=[pskey])
                bci = g % 4
                cst = float(st["q0"] - st["k0"] - 511)
                sl = t % 2
                a.op("vector", (lambda e, h=h, cst=cst, bci=bci, sl=sl:
                                e.scalar_tensor_tensor(
                    bcol[:, bci:bci + 1], slope_col[:, h:h + 1], cst,
                    basc[:, sl:sl + 1], OP.mult, OP.add)),
                    reads=[("negsl",), ("bas", sl), ("bcu", bci)],
                    writes=[("bc", bci), ("bau", sl)])
                bias_ap = bcol[:, bci:bci + 1]
                bias_keys = [("bc", bci)]
            bi = st["bi"]
            a.op("scalar", (lambda e, kw=kw, psx=psx, bias_ap=bias_ap, g=g, bi=bi:
                            e.activation(
                pt_sb[:, (g % 2) * 512:(g % 2) * 512 + kw], psx[:, :kw], AF.Exp,
                bias=bias_ap, accum_out=lacc[:, 2 + bi:3 + bi])),
                 reads=[pskey, ("la1", bi)] + bias_keys,
                 writes=[("pt", g % 2), ("la1", bi)]
                        + ([] if noal else [("bcu", g % 4)]))

        def ePV(g):
            st = steps[g]
            t, k0, kw = st["t"], st["k0"], st["kw"]
            po = PS[2 + (t % 2)]
            pok = ("PSM", id(po))
            ncc = ceil_div(kw, 128)
            for cc in range(ncc):
                a.op("tensor", (lambda e, g=g, cc=cc: e.transpose(
                    PSC[:, cc * 128:cc * 128 + 128],
                    pt_sb[:, (g % 2) * 512 + cc * 128:(g % 2) * 512 + cc * 128 + 128],
                    u["idbf"][:, :])),
                    reads=[("pt", g % 2), "c"], writes=[("psct", cc)], tag=nm)
            a.op("scalar", (lambda e, kw=kw: e.activation(
                ptt[:, 0:kw], PSC[:, 0:kw], AF.Copy)),
                reads=[("psct", cc2) for cc2 in range(ncc)]
                      + [("ptt", cc2) for cc2 in range(4)],
                writes=[("pttv",)])
            for cc in range(ncc):
                a.op("tensor", (lambda e, h=h, k0=k0, cc=cc, po=po,
                                st_=(st["first"] and cc == 0),
                                sp=(st["last"] and cc == ncc - 1):
                                e.matmul(po[:, :128],
                                         ptt[:, cc * 128:cc * 128 + 128],
                                         vh[:, h * S + (k0 + cc * 128):
                                            h * S + (k0 + cc * 128) + 128],
                                         start=st_, stop=sp)),
                     reads=[("pttv",)] + vh_keys[h],
                     writes=[pok, ("ptt", cc)], tag=nm)
            if not st["last"]:
                return
            q0 = st["q0"]
            nblk = st["bi"] + 1
            a.op("vector", (lambda e, nblk=nblk: e.reduce_sum(
                lacc[:, 0:1], lacc[:, 2:2 + nblk], axis=AX.X)),
                reads=[("la1", b2) for b2 in range(nblk)], writes=[("la",)])
            a.op("vector", lambda e: e.reciprocal(lacc[:, 0:1], lacc[:, 0:1]),
                 reads=[("la",)], writes=[("la",)])
            yslot = t % 2
            a.op("vector", (lambda e, po=po, yslot=yslot: e.tensor_scalar(
                ystg[:, yslot * 128:(yslot + 1) * 128], po[:, :128], lacc[:, 0:1],
                None, OP.mult)),
                reads=[pok, ("la",), ("yts", yslot)], writes=[("ys", yslot)])
            a.op("tensor", (lambda e, yslot=yslot: e.transpose(
                PSC16[:, :128], ystg[:, yslot * 128:(yslot + 1) * 128], u["idf16"][:, :])),
                reads=[("ys", yslot), "c", ("yoc", yslot)],
                writes=[("PSM", id(PSC16))])
            a.op("vector", (lambda e, yslot=yslot: e.tensor_copy(
                yout[:, yslot * 128:(yslot + 1) * 128], PSC16[:, :128])),
                reads=[("PSM", id(PSC16))], writes=[("yo", yslot), ("yts", yslot)])
            r = q0 // T
            c0 = q0 % T
            a.dma("scalar", ya_i[256 * r + 128 * h:256 * r + 128 * (h + 1), c0:c0 + 128],
                  yout[:, yslot * 128:(yslot + 1) * 128],
                  reads=[("yo", yslot)], writes=[("yoc", yslot), ("ya_i", h, t)])

        eS(0)
        eBase(0)
        ePre(0)
        if n > 1:
            eS(1)
        for g in range(n):
            ePV(g)
            if g + 2 < n:
                eS(g + 2)
            if g + 1 < n:
                eBase(g + 1)
                ePre(g + 1)
    flash(0)
    flash(1)
    ya_keys = [("ya_i", hh, t) for hh in range(2) for t in range(NBQ)]
    a.cc("AllToAll", OP.bypass, ya_i, ya_o, ya_keys, "ya_o")
    _drain_dmas(a)
    a.phase()
    r2.close()

    r3 = contextlib.ExitStack()
    xstg = sb("xstg", [128, 4 * 512], F32, r3)
    r3b = contextlib.ExitStack()
    w_sb = sb("wo_sb", [128, KC * D], F16, r3b)
    y_sb = sb("y_sb", [128, KC * T], F16, r3b)
    q_load_act(a, ya_o, y_sb, T, "yf", src_key="ya_o")
    q_stream_weight(a, wo_d, w_sb, KC, D, stage, "po", "pow")
    NTl = NT
    groups = [(m, n0, nn) for m in range(KC) for (n0, nn) in NTl]
    q_mm(a, "pomm", groups,
         lambda g, k: w_sb[:, k * D + g[0] * 128:k * D + (g[0] + 1) * 128],
         lambda g, k: y_sb[:, k * T + g[1]:k * T + g[1] + g[2]],
         KC, PS[0:2],
         mk_resid_evac(a, xstg, NTl, "rp", xT, x2p_d, None, "x2p"),
         lambda g: [("yf", k) for k in range(KC)] + [("pow", k) for k in range(KC)],
         lambda g, ps: ps[:, :g[2]])
    _drain_dmas(a)
    a.phase()
    r3b.close()
    return L.get("ffn_fn", _l1_ffn)(nc, es, a, u, sb, PS, PR, L, r3, w_sb, xstg)


def _l1_ffn(nc, es, a, u, sb, PS, PR, L, r3, w_sb, xstg):
    S, T, TC = L["S"], L["T"], L["TC"]
    NT, ND = L["NT"], L["ND"]
    FD, WSL, PSL = L["FD"], L["WSL"], L["PSL"]
    stage, srow = L["stage"], L["srow"]
    xn_d = L["xn32_d"]
    wfcs_d, wpjs_d = L["wfcs_d"], L["wpjs_d"]
    w8fc_p, w8fc_g, w8pj_p, w8pj_g = L["w8fc_p"], L["w8fc_g"], L["w8pj_p"], L["w8pj_g"]
    h_d, x2p_d, x2T = L["h_d"], L["x2p_d"], L["x2T"]
    mx_o, hx_i, hx_o = L["mx_o"], L["hx_i"], L["hx_o"]

    r4 = contextlib.ExitStack()
    sc2 = sb("sc2", [128, 8], F32, r4)
    fq = Fp8Quant(a, (lambda n, shp, dt: sb(n, shp, dt, r4)), "fq", 512)
    w8stg = sb("w8stg", [128, 2 * WSL], BF16, r4)
    hstg = sb("hstg", [128, 2 * 512], F32, r4)
    wstrip = sb("wstrip", [128, 2 * 16 * 128], BF16, r4)
    a.dma("sync", srow[:1, 12:16], mx_o[:1, :], reads=["mx_o"], writes=[("mxr",)])
    for j, nmk in ((12, "sx"), (13, "swf"), (14, "swp")):
        a.op("vector", (lambda e, j=j: e.tensor_scalar(
            srow[:1, j:j + 1], srow[:1, j:j + 1], 1.0 / 448.0, 1e-6, OP.mult, OP.max)),
            reads=[("mxr",)], writes=[(nmk, "s")])
    a.op("vector", lambda e: e.tensor_tensor(srow[:1, 11:12], srow[:1, 12:13],
                                             srow[:1, 13:14], OP.mult),
         reads=[("sx", "s"), ("swf", "s"), ("srow0",)], writes=[("sxw", "s")])
    for j, nmk in ((12, "sx"), (13, "swf"), (14, "swp")):
        a.op("vector", (lambda e, j=j: e.reciprocal(srow[:1, j + 4:j + 5], srow[:1, j:j + 1])),
             reads=[(nmk, "s")], writes=[(nmk, "i")])
    q_scalar_bcast(a, u, srow[:1, 16:17], ("sx", "i"), sc2[:, 0:1], ("sc2", 0), PS[2])
    q_scalar_bcast(a, u, srow[:1, 17:18], ("swf", "i"), sc2[:, 1:2], ("sc2", 1), PS[2])
    q_scalar_bcast(a, u, srow[:1, 18:19], ("swp", "i"), sc2[:, 2:3], ("sc2", 2), PS[2])
    q_scalar_bcast(a, u, srow[:1, 11:12], ("sxw", "s"), sc2[:, 3:4], ("sc2", 3), PS[2])

    r4b = contextlib.ExitStack()
    x8 = sb("x8", [128, KC * T], BF16, r4b)
    xnld = sb("xnld", [128, 2 * 512], F32, r4b)
    allj = [(k, n0, nn) for k in range(KC) for (n0, nn) in NT]
    pend = []
    for ji, (k, n0, nn) in enumerate(allj):
        sl = (ji % 2) * 512
        a.dma("sync", xnld[:, sl:sl + nn], xn_d[128 * k:128 * (k + 1), n0:n0 + nn],
              reads=[("xn32_d", k)] + ([("x8", "emq", ji - 2)] if ji >= 2 else []),
              writes=[("xnl", ji % 2)])
        pend.append((ji, k, n0,
                     (xnld[:, sl:sl + nn], x8[:, k * T + n0:k * T + n0 + nn], nn,
                      ("xnl", ji % 2), [("x8", k, n0), ("x8", "emq", ji)],
                      sc2[:, 0:1], ("sc2", 0))))
        if len(pend) == 2 or ji == len(allj) - 1:
            fq.emit_jobs(a, [p[3] for p in pend])
            pend = []
    def wquant(wdram, kin, M, wp, tagn, sidx):
        jobs = [(k, n0, nn) for k in range(kin) for (n0, nn) in nspans(M)]
        pend = []
        for idx, (k, n0, nn) in enumerate(jobs):
            if n0 == 0:
                a.dma("sync", stage[:, (k % 2) * M:(k % 2) * M + M],
                      wdram[128 * k:128 * (k + 1), :],
                      reads=([(tagn, "q", k - 2)] if k >= 2 else [])
                            + [("STG", id(stage), k % 2)],
                      writes=[(tagn, "ld", k % 2), ("STG", id(stage), k % 2)])
            pend.append((k, n0,
                         (stage[:, (k % 2) * M + n0:(k % 2) * M + n0 + nn],
                          w8stg[:, (k % 2) * M + n0:(k % 2) * M + n0 + nn], nn,
                          (tagn, "ld", k % 2),
                          [(tagn, "q8", k % 2, n0), ("W8S",),
                           ("STG", id(stage), k % 2)],
                          sc2[:, sidx:sidx + 1], ("sc2", sidx))))
            if len(pend) == 2 or idx == len(jobs) - 1:
                fq.emit_jobs(a, [p[2] for p in pend])
                for k2 in sorted(set(p[0] for p in pend)):
                    if max(n03 for (k3, n03, _) in pend if k3 == k2) \
                            == nspans(M)[-1][0]:
                        a.dma("scalar", wp[128 * k2:128 * (k2 + 1), :],
                              w8stg[:, (k2 % 2) * M:(k2 % 2) * M + M],
                              reads=[(tagn, "q8", k2 % 2, n03)
                                     for (n03, _) in nspans(M)] + [("W8S",)],
                              writes=[(tagn, "q", k2), (tagn, "st", k2)])
                pend = []
    wquant(wfcs_d, KC, WSL, w8fc_p, "qfc", 1)
    wquant(wpjs_d, FD // 128, PSL, w8pj_p, "qpj", 2)
    a.cc("AllGather", OP.bypass, w8fc_p, w8fc_g,
         [("qfc", "st", k) for k in range(KC)], "w8fc_g")
    a.cc("AllGather", OP.bypass, w8pj_p, w8pj_g,
         [("qpj", "st", k) for k in range(FD // 128)], "w8pj_g")
    _drain_dmas(a)
    a.phase()

    a.op("vector", lambda e: e.memset(sc2[:, 5:6], 0.0), writes=[("hmacc",)])
    for m in range(FD // 128):
        r = m // (WSL // 128)
        cm = (m % (WSL // 128)) * 128
        slot = m % 2
        a.dma("sync",
              wstrip[:, slot * 2048:(slot + 1) * 2048].rearrange(
                  "p (k c) -> p k c", k=KC, c=128),
              w8fc_g[D * r:D * (r + 1), cm:cm + 128].rearrange(
                  "(k p) c -> p k c", k=KC, p=128),
              reads=["w8fc_g"] + ([("fcg", m - 2)] if m >= 2 else []),
              writes=[("ws", slot)])
        for i, (n0, nn) in enumerate(NT):
            ps = PS[i % 2]
            pk = ("PSM", id(ps))
            for k in range(KC):
                a.op("tensor", (lambda e, slot=slot, k=k, n0=n0, nn=nn, ps=ps: e.matmul(
                    ps[:, :nn],
                    wstrip[:, slot * 2048 + k * 128:slot * 2048 + (k + 1) * 128],
                    x8[:, k * T + n0:k * T + n0 + nn],
                    start=(k == 0), stop=(k == KC - 1))),
                    reads=[("ws", slot), ("x8", k, n0)],
                    writes=[pk] + ([("fcg", m)] if k == KC - 1 and i == len(NT) - 1 else []))
            hs = hstg[:, i * 512:i * 512 + nn]
            a.op("vector", (lambda e, hs=hs, ps=ps, nn=nn: e.tensor_scalar(
                hs, ps[:, :nn], sc2[:, 3:4], 0.0, OP.mult, OP.max)),
                reads=[pk, ("sc2", 3), ("hst", i)], writes=[("h0", i)])
            a.op("scalar", (lambda e, hs=hs: e.activation(hs, hs, AF.Square)),
                 reads=[("h0", i)], writes=[("h0", i)])
            a.op("vector", (lambda e, hs=hs: e.reduce_max(sc2[:, 4:5], hs, axis=AX.X)),
                 reads=[("h0", i)], writes=[("hm1",)])
            a.op("vector", lambda e: e.tensor_tensor(sc2[:, 5:6], sc2[:, 5:6],
                                                     sc2[:, 4:5], OP.max),
                 reads=[("hm1",), ("hmacc",)], writes=[("hmacc",)])
        a.dma("scalar", h_d[128 * m:128 * (m + 1), 0:T], hstg[:, 0:T],
              reads=[("h0", i2) for i2 in range(len(NT))],
              writes=[("hst", i2) for i2 in range(len(NT))]
                     + [("h_d", m, n0) for (n0, _) in NT])
    a.op("tensor", lambda e: e.transpose(PS[2][:1, :128], sc2[:, 5:6], u["id32"][:, :]),
         reads=[("hmacc",), "c"], writes=[("PSM", id(PS[2]))])
    a.op("vector", lambda e: e.reduce_max(srow[:1, 10:11], PS[2][:1, :128], axis=AX.X),
         reads=[("PSM", id(PS[2])), ("srow0",)], writes=[("hxl",)])
    a.dma("scalar", hx_i[:1, :], srow[:1, 10:11], reads=[("hxl",)], writes=[("hx_i",)])
    a.cc("AllReduce", OP.max, hx_i, hx_o, [("hx_i",)], "hx_o")
    _drain_dmas(a)
    a.phase()
    r4b.close()

    r4c = contextlib.ExitStack()
    h8 = sb("h8", [128, (FD // 128) * 512], BF16, r4c)
    wpstrip = sb("wpstrip", [128, 2 * (FD // 128) * 128], BF16, r4c)
    hload = sb("hload", [128, 2 * 512], F32, r4c)
    a.dma("sync", srow[:1, 10:11], hx_o[:1, :], reads=["hx_o"], writes=[("hxg",)])
    a.op("vector", lambda e: e.tensor_scalar(srow[:1, 10:11], srow[:1, 10:11],
                                             1.0 / 448.0, 1e-6, OP.mult, OP.max),
         reads=[("hxg",)], writes=[("sh", "s")])
    a.op("vector", lambda e: e.tensor_tensor(srow[:1, 9:10], srow[:1, 10:11],
                                             srow[:1, 14:15], OP.mult),
         reads=[("sh", "s"), ("swp", "s"), ("srow0",)], writes=[("shw", "s")])
    a.op("vector", lambda e: e.reciprocal(srow[:1, 19:20], srow[:1, 10:11]),
         reads=[("sh", "s")], writes=[("sh", "i")])
    q_scalar_bcast(a, u, srow[:1, 19:20], ("sh", "i"), sc2[:, 6:7], ("sc2", 6), PS[2])
    q_scalar_bcast(a, u, srow[:1, 9:10], ("shw", "s"), sc2[:, 7:8], ("sc2", 7), PS[2])

    NKF = FD // 128
    for i, (n0, nn) in enumerate(NT):
        pend = []
        for k in range(NKF):
            a.dma("sync", hload[:, (k % 2) * 512:(k % 2) * 512 + nn],
                  h_d[128 * k:128 * (k + 1), n0:n0 + nn],
                  reads=[("h_d", k, n0)] + ([("h8q", i, k - 2)] if k >= 2 else []),
                  writes=[("hld", k % 2)])
            pend.append((hload[:, (k % 2) * 512:(k % 2) * 512 + nn],
                         h8[:, k * 512:k * 512 + nn], nn,
                         ("hld", k % 2), ("h8q", i, k), sc2[:, 6:7], ("sc2", 6)))
            if len(pend) == 2 or k == NKF - 1:
                fq.emit_jobs(a, pend)
                pend = []
        for m in range(KC):
            r = (m * 128) // PSL
            cm = (m * 128) % PSL
            slot = m % 2
            prev = ("pjg", i, m - 2) if m > 1 else (("pjg", i - 1, KC - 2) if i > 0 else None)
            a.dma("sync",
                  wpstrip[:, slot * NKF * 128:(slot + 1) * NKF * 128].rearrange(
                      "p (k c) -> p k c", k=NKF, c=128),
                  w8pj_g[FD * r:FD * (r + 1), cm:cm + 128].rearrange(
                      "(k p) c -> p k c", k=NKF, p=128),
                  reads=["w8pj_g"] + ([prev] if prev else []),
                  writes=[("wp", slot)])
            ps = PS[m % 2]
            pk = ("PSM", id(ps))
            for k in range(NKF):
                a.op("tensor", (lambda e, k=k, n0=n0, nn=nn, ps=ps, slot=slot:
                                e.matmul(
                    ps[:, :nn], wpstrip[:, slot * NKF * 128 + k * 128:
                                        slot * NKF * 128 + (k + 1) * 128],
                    h8[:, k * 512:k * 512 + nn],
                    start=(k == 0), stop=(k == NKF - 1))),
                    reads=[("wp", slot), ("h8q", i, k)],
                    writes=[pk] + ([("pjg", i, m)] if k == NKF - 1 else []))
            a.op("vector", (lambda e, ps=ps, nn=nn: e.tensor_scalar(
                ps[:, :nn], ps[:, :nn], sc2[:, 7:8], None, OP.mult)),
                reads=[pk, ("sc2", 7)], writes=[pk])
            mk_resid_evac(a, xstg, [(n0, nn)], "ff", x2p_d, x2T,
                          (lambda mm, nn0: ("x2p", mm, nn0)), "x2T")(
                a, (m, n0, nn), ps, pk)
    a.finish()
    a.phase()
    r4c.close()
    r4.close()
    r3.close()
    return nc


def build_l2(S, ctx=None):
    T = S // NCORES
    TC = T // 128
    NB = S // 128
    NT = nspans(T)
    ND = nspans(D)
    FI = 5504
    NKI = FI // 128
    NSI = nspans(FI)
    nc = _NCPfx(ctx["nc"], ctx["pfx"]) if ctx else bass.Bass("TRN2", target_bir_lowering=False)
    xT = ctx["x_in"] if ctx else nc.dram_tensor("xT", [D, T], F32, kind="ExternalInput").ap()
    wq_d = nc.dram_tensor("wq", [D, D], F32, kind="ExternalInput").ap()
    wk_d = nc.dram_tensor("wk", [D, D], F32, kind="ExternalInput").ap()
    wv_d = nc.dram_tensor("wv", [D, D], F32, kind="ExternalInput").ap()
    wo_d = nc.dram_tensor("wo", [D, D], F32, kind="ExternalInput").ap()
    gidx_d = nc.dram_tensor("gidx", [256, FI], I32, kind="ExternalInput").ap()
    uidx_d = nc.dram_tensor("uidx", [256, FI], I32, kind="ExternalInput").ap()
    didx_d = nc.dram_tensor("didx", [FI, 256], I32, kind="ExternalInput").ap()
    lut_d = nc.dram_tensor("lut", [1, 16], F32, kind="ExternalInput").ap()
    cs_d = nc.dram_tensor("cs", [128, S], F16, kind="ExternalInput").ap()
    slopes_d = nc.dram_tensor("slopes", [1, 2], F32, kind="ExternalInput").ap()
    cmask_d = nc.dram_tensor("cmask", [128, 4 * 512], BF16, kind="ExternalInput").ap()
    cdram = ctx["cdram"] if ctx else _mk_cdram(nc)
    x2T = ctx["x_out"] if ctx else nc.dram_tensor("x2T", [D, T], F32, kind="ExternalOutput").ap()
    qa_i = nc.dram_tensor("qa_i", [D, T], F16)
    qa_o = nc.dram_tensor("qa_o", [D, T], F16)
    ka_i = nc.dram_tensor("ka_i", [D, T], F16)
    ka_o = nc.dram_tensor("ka_o", [D, T], F16)
    va_i = nc.dram_tensor("va_i", [S, 256], F16)
    va_o = nc.dram_tensor("va_o", [S, 256], F16)
    ya_i = nc.dram_tensor("ya_i", [D, T], F16)
    ldbg = nc.dram_tensor("ldbg", [2 * (S // 128) + 2, 512], F32)
    ya_o = nc.dram_tensor("ya_o", [D, T], F16)
    w8g_p = nc.dram_tensor("w8g_p", [256, FI], BF16)
    w8g_g = nc.dram_tensor("w8g_g", [D, FI], BF16)
    w8u_p = nc.dram_tensor("w8u_p", [256, FI], BF16)
    w8u_g = nc.dram_tensor("w8u_g", [D, FI], BF16)
    w8d_p = nc.dram_tensor("w8d_p", [FI, 256], BF16)
    w8d_g = nc.dram_tensor("w8d_g", [NCORES * FI, 256], BF16)
    h_d = nc.dram_tensor("h_d", [FI, T], BF16)
    x2p_d = nc.dram_tensor("x2p", [D, T], F32)
    xn_d = nc.dram_tensor("xn_d", [D, T], F16)
    winA = NB + 9
    no_alibi = True
    ffn_fn = _l2_ffn

    es = contextlib.ExitStack()
    with es:
        a = ctx["a"] if ctx else Auto(nc, es)
        sb = lambda n, shp, dt, st=es, side=None: st.enter_context(
            nc.sbuf_tensor(n, shp, dt, side=side))
        u = mk_utils(a, nc, es, T)
        PS = [es.enter_context(nc.psum_tensor(f"PS{i}", [128, 512], F32)) for i in range(5)]
        PR = [es.enter_context(nc.psum_tensor("PR0", [1, 512], F32))]
        PSC = es.enter_context(nc.psum_tensor("PSC", [128, 512], BF16))
        PSC16 = es.enter_context(nc.psum_tensor("PSC16", [128, 128], F16))
        u["pstat"] = [PS[0], PS[1], PS[2], PR[0]]
        u["psb"] = PS[3:5]
        u["idf16"] = sb("u_idf16", [128, 128], F16)
        stage = sb("stage", [128, 2 * D], F32)
        iota_f = sb("iota_f", [128, 1], F32)
        iota_i = sb("iota_i", [128, 1], I32)
        slope_col = sb("slope_col", [128, 2], F32)
        scl = sb("scl", [128, 8], F32)
        srow = sb("srow", [1, 24], F32)
        r0x = contextlib.ExitStack()
        xn = sb("xn", [128, KC * T], F16, r0x)

        r0 = contextlib.ExitStack()
        x_sb = sb("x_sb", [128, KC * T], F32, r0)
        u["xbf"] = sb("xbf", [128, 2 * T], BF16, r0)
        u["sqbf"] = sb("sqbf", [128, 2 * T], BF16, r0)
        q_load_consts(a, u, cdram)
        a.op("vector", lambda e: e.tensor_copy(u["idf16"][:, :], u["id32"][:, :]),
             reads=[("c", "id32")], writes=[("c", "idf16")])
        a.op("vector", lambda e: e.memset(srow[:1, :], 0.0), writes=[("srow0",)])
        a.op("gpsimd", lambda e: e.iota(iota_i[:, :], [[1, 1]], base=0, channel_multiplier=1),
             writes=[("iota_i",)])
        a.op("vector", lambda e: e.tensor_copy(iota_f[:, :], iota_i[:, :]),
             reads=[("iota_i",)], writes=[("iota",)])
        q_load_act(a, xT, x_sb, T, "x")
        q_layernorm(a, u, T, x_sb, "x", xn, "xn")
        _drain_dmas(a)
        a.phase()
        r0.close()
        XN = [("xn", k) for k in range(KC)]

        r1 = contextlib.ExitStack()
        w_sb = sb("w_sb", [128, KC * D], F16, r1)
        r1q = contextlib.ExitStack()
        qstg = sb("qstg", [128, 2 * T], F16, r1q)
        q_stream_weight(a, wq_d, w_sb, KC, D, stage, "pq", "pqw")
        pqw_keys = [("pqw", k) for k in range(KC)]
        qgroups = [(m, n0, nn) for m in range(KC) for (n0, nn) in NT]

        def q_evac(a_, g, ps, pk):
            m, n0, nn = g
            slot = m % 2
            a_.op("vector", (lambda e: e.tensor_copy(
                qstg[:, slot * T + n0:slot * T + n0 + nn], ps[:, :nn])),
                reads=[pk], writes=[("qstg", slot, n0)])
            if n0 + nn >= T:
                a_.dma("scalar", qa_i[128 * m:128 * (m + 1), :],
                       qstg[:, slot * T:slot * T + T],
                       reads=[("qstg", slot, x0) for (x0, _) in NT],
                       writes=[("qa_i", m)])
        q_mm(a, "pqmm", qgroups,
             lambda g, k: w_sb[:, k * D + g[0] * 128:k * D + (g[0] + 1) * 128],
             lambda g, k: xn[:, k * T + g[1]:k * T + g[1] + g[2]],
             KC, PS[0:2], q_evac, lambda g: XN + pqw_keys,
             lambda g, ps: ps[:, :g[2]])
        _drain_dmas(a)
        a.phase()
        r1q.close()

        kv_sb = sb("kv_sb", [128, KC * T], F16, r1)
        vt_sb = sb("vt_sb", [128, TC * D], F16, r1)
        q_stream_weight(a, wk_d, w_sb, KC, D, stage, "pk", "pkw")
        kgroups = [(m, n0, nn) for m in range(KC) for (n0, nn) in NT]
        q_mm(a, "pkmm", kgroups,
             lambda g, k: w_sb[:, k * D + g[0] * 128:k * D + (g[0] + 1) * 128],
             lambda g, k: xn[:, k * T + g[1]:k * T + g[1] + g[2]],
             KC, PS[0:2],
             lambda a_, g, ps, pk: a_.op(
                 "vector",
                 (lambda e, g=g, ps=ps: e.tensor_copy(
                     kv_sb[:, g[0] * T + g[1]:g[0] * T + g[1] + g[2]], ps[:, :g[2]])),
                 reads=[pk], writes=[("kf", g[0], g[1])]),
             lambda g: XN + [("pkw", k) for k in range(KC)],
             lambda g, ps: ps[:, :g[2]])
        q_stream_weight(a, wv_d, w_sb, KC, D, stage, "pv", "pvw")
        vgroups = [(t, n0, nn) for t in range(TC) for (n0, nn) in ND]
        q_mm(a, "pvmm", vgroups,
             lambda g, k: xn[:, k * T + g[0] * 128:k * T + (g[0] + 1) * 128],
             lambda g, k: w_sb[:, k * D + g[1]:k * D + g[1] + g[2]],
             KC, PS[0:2],
             lambda a_, g, ps, pk: a_.op(
                 "vector",
                 (lambda e, g=g, ps=ps: e.tensor_copy(
                     vt_sb[:, g[0] * D + g[1]:g[0] * D + g[1] + g[2]], ps[:, :g[2]])),
                 reads=[pk], writes=[("vt", g[0], g[1])]),
             lambda g: XN + [("pvw", k) for k in range(KC)],
             lambda g, ps: ps[:, :g[2]])
        kf_all = [("kf", g, n0) for g in range(KC) for (n0, _) in NT]
        vt_all = [("vt", t, n0) for t in range(TC) for (n0, _) in ND]
        for j in range(NCORES):
            for sgi in range(2):
                grp = 2 * j + sgi
                a.dma("scalar", ka_i[256 * j + 128 * sgi:256 * j + 128 * (sgi + 1), :],
                      kv_sb[:, grp * T:(grp + 1) * T],
                      reads=kf_all, writes=[("ka_i", j, sgi)])
        for j in range(NCORES):
            for sgi in range(2):
                grp = 2 * j + sgi
                a.dma("sync",
                      va_i[j * T:(j + 1) * T,
                           128 * sgi:128 * (sgi + 1)].rearrange(
                          "(t p) c -> p t c", t=TC, p=128),
                      vt_sb[:, :].rearrange("p (t d) -> p t d", t=TC, d=D)[
                          :, :, grp * 128:(grp + 1) * 128],
                      reads=vt_all, writes=[("va_i", j, 0, sgi)])
        a.cc("AllToAll", OP.bypass, qa_i, qa_o, [("qa_i", m) for m in range(KC)], "qa_o")
        a.cc("AllToAll", OP.bypass, ka_i, ka_o,
             [("ka_i", j, sgi) for j in range(NCORES) for sgi in range(2)], "ka_o")
        a.cc("AllToAll", OP.bypass, va_i, va_o,
             [("va_i", j, 0, sgi) for j in range(NCORES)
              for sgi in range(2)], "va_o")
        q_store_act(a, xn, xn_d, T, "xn", "xn_d", eng="sync")
        _drain_dmas(a)
        a.phase()
        r1.close()
        r0x.close()
        return _l1_attn(nc, es, a, u, sb, PS, PR, locals())
    return nc


def _l2_ffn(nc, es, a, u, sb, PS, PR, L, r3, w_sb, xstg):
    S, T, TC = L["S"], L["T"], L["TC"]
    NT = L["NT"]
    FI, NKI, NSI = L["FI"], L["NKI"], L["NSI"]
    stage, srow, scl = L["stage"], L["srow"], L["scl"]
    xn_d = L["xn_d"]
    gidx_d, uidx_d, didx_d, lut_d = L["gidx_d"], L["uidx_d"], L["didx_d"], L["lut_d"]
    w8g_p, w8g_g = L["w8g_p"], L["w8g_g"]
    w8u_p, w8u_g = L["w8u_p"], L["w8u_g"]
    w8d_p, w8d_g = L["w8d_p"], L["w8d_g"]
    h_d, x2p_d, x2T = L["h_d"], L["x2p_d"], L["x2T"]
    if xstg is None:
        xstg = sb("xstg2", [128, 4 * 512], F32, r3)

    r4 = contextlib.ExitStack()
    idxs = sb("idxs", [128, 2 * 512], I32, r4)
    fidx = sb("fidx", [128, 2 * 512], F32, r4)
    wt16 = sb("wt16", [128, 32 * 512], BF16, r4)
    wacc = sb("wacc", [128, 2 * 512], BF16, r4)
    lutc = sb("lutc", [128, 16], F32, r4)
    a.dma("sync", srow[:1, 20:24].bitcast(F32) if False else stage[:1, 0:16], lut_d,
          reads=[("STG", id(stage), 0)], writes=[("lutr",), ("STG", id(stage), 0)])
    for v in range(16):
        q_scalar_bcast(a, u, stage[:1, v:v + 1], ("lutr",), lutc[:, v:v + 1],
                       ("lutc", v), PS[2])
    lutk = [("lutc", v) for v in range(16)]

    def matz(tag, src_d, dst_d, tiles):
        pendt = []

        def flush():
            for (i, r0, c0, w) in pendt:
                slot = i % 2
                a.op("vector", (lambda e, slot=slot, w=w: e.tensor_copy(
                    fidx[:, slot * 512:slot * 512 + w],
                    idxs[:, slot * 512:slot * 512 + w])),
                    reads=[(tag, "ld", slot), ("FID", slot), ("IDX", slot)],
                    writes=[(tag, "f", slot), ("FID", slot)])
            for v in range(16):
                for (i, r0, c0, w) in pendt:
                    slot = i % 2
                    a.op("vector", (lambda e, v=v, w=w, slot=slot: e.tensor_scalar(
                        wt16[:, (v * 2 + slot) * 512:(v * 2 + slot) * 512 + w],
                        fidx[:, slot * 512:slot * 512 + w], float(v),
                        lutc[:, v:v + 1], OP.is_equal, OP.mult)),
                        reads=[(tag, "f", slot), ("FID", slot), ("W16", v, slot)]
                              + lutk,
                        writes=[(tag, "v", v, slot), ("W16", v, slot)])
            for lvl, stepn in ((1, 8), (2, 4), (3, 2)):
                for j in range(stepn):
                    for (i, r0, c0, w) in pendt:
                        slot = i % 2
                        a.op("vector", (lambda e, j=j, stepn=stepn, w=w, slot=slot:
                                        e.tensor_tensor(
                            wt16[:, (j * 2 + slot) * 512:(j * 2 + slot) * 512 + w],
                            wt16[:, (j * 2 + slot) * 512:(j * 2 + slot) * 512 + w],
                            wt16[:, ((j + stepn) * 2 + slot) * 512:
                                 ((j + stepn) * 2 + slot) * 512 + w], OP.add)),
                            reads=[(tag, "v", j, slot), (tag, "v", j + stepn, slot),
                                   ("W16", j, slot), ("W16", j + stepn, slot)],
                            writes=[(tag, "v", j, slot), ("W16", j, slot)])
            for (i, r0, c0, w) in pendt:
                slot = i % 2
                a.op("vector", (lambda e, slot=slot, w=w: e.tensor_tensor(
                    wacc[:, slot * 512:slot * 512 + w],
                    wt16[:, slot * 512:slot * 512 + w],
                    wt16[:, (2 + slot) * 512:(2 + slot) * 512 + w], OP.add)),
                    reads=[(tag, "v", 0, slot), (tag, "v", 1, slot),
                           (tag, "wa", slot),
                           ("W16", 0, slot), ("W16", 1, slot), ("WAC", slot)],
                    writes=[(tag, "acc", slot), ("WAC", slot)])
            for (i, r0, c0, w) in pendt:
                slot = i % 2
                a.dma("scalar", dst_d[r0:r0 + 128, c0:c0 + w],
                      wacc[:, slot * 512:slot * 512 + w],
                      reads=[(tag, "acc", slot), ("WAC", slot)],
                      writes=[(tag, "st", i), (tag, "wa", slot)])

        for i, (r0, c0, w) in enumerate(tiles):
            slot = i % 2
            a.dma("sync", idxs[:, slot * 512:slot * 512 + w],
                  src_d[r0:r0 + 128, c0:c0 + w],
                  reads=([(tag, "st", i - 2)] if i >= 2 else []),
                  writes=[(tag, "ld", slot), ("IDX", slot)])
            pendt.append((i, r0, c0, w))
            if len(pendt) == 2 or i == len(tiles) - 1:
                flush()
                pendt = []
        return [(tag, "st", i) for i in range(len(tiles))]

    gtiles = [(128 * r, c0, w) for r in range(2) for (c0, w) in NSI]
    dtiles = [(128 * r, 0, 256) for r in range(NKI)]
    gk = matz("mg", gidx_d, w8g_p, gtiles)
    uk = matz("mu", uidx_d, w8u_p, gtiles)
    dk = matz("md", didx_d, w8d_p, dtiles)
    a.cc("AllGather", OP.bypass, w8g_p, w8g_g, gk, "w8g_g")
    a.cc("AllGather", OP.bypass, w8u_p, w8u_g, uk, "w8u_g")
    a.cc("AllGather", OP.bypass, w8d_p, w8d_g, dk, "w8d_g")
    _drain_dmas(a)
    a.phase()
    r4.close()

    import os as _os
    if _os.environ.get("KDBG_NOFFN"):
        a.finish()
        a.phase()
        r3.close()
        return nc
    r5 = contextlib.ExitStack()
    xnl = sb("xnl", [128, KC * T], F16, r5)
    gustrip = sb("gustrip", [128, 2 * 2 * KC * 128], BF16, r5)
    hsil = sb("hsil", [128, 2 * 512], F32, r5)
    hbf = sb("hbf", [128, 2 * 512], BF16, r5)
    q_load_act(a, xn_d, xnl, T, "xl", src_key="xn_d")
    XL = [("xl", k) for k in range(KC)]
    GW = 2 * KC * 128
    for ic in range(NKI):
        slot = ic % 2
        for wi, wg in enumerate((w8g_g, w8u_g)):
            a.dma("sync",
                  gustrip[:, slot * GW + wi * KC * 128:
                          slot * GW + (wi + 1) * KC * 128].rearrange(
                      "p (k c) -> p k c", k=KC, c=128),
                  wg[0:D, 128 * ic:128 * ic + 128].rearrange(
                      "(k p) c -> p k c", k=KC, p=128),
                  reads=(["w8g_g", "w8u_g"] + ([("gu", ic - 2)] if ic >= 2 else [])),
                  writes=[("gw", slot, wi)])
        for i, (n0, nn) in enumerate(NT):
            pg = PS[2 * slot]
            pu = PS[2 * slot + 1]
            pgk, puk = ("PSM", id(pg)), ("PSM", id(pu))
            for wi, (ps, pk) in enumerate(((pg, pgk), (pu, puk))):
                for k in range(KC):
                    a.op("tensor", (lambda e, slot=slot, wi=wi, k=k, n0=n0, nn=nn,
                                    ps=ps: e.matmul(
                        ps[:, :nn],
                        gustrip[:, slot * GW + (wi * KC + k) * 128:
                                slot * GW + (wi * KC + k) * 128 + 128],
                        xnl[:, k * T + n0:k * T + n0 + nn],
                        start=(k == 0), stop=(k == KC - 1))),
                        reads=[("gw", slot, wi)] + XL,
                        writes=[pk] + ([("gu", ic)] if k == KC - 1 and wi == 1
                                       and i == len(NT) - 1 else []))
            a.op("scalar", (lambda e, i=i, nn=nn, pg=pg: e.activation(
                hsil[:, i * 512:i * 512 + nn], pg[:, :nn], AF.Silu)),
                reads=[pgk, ("HSL", i)], writes=[("sil", i), ("HSL", i)])
            a.op("vector", (lambda e, i=i, nn=nn, pu=pu: e.tensor_tensor(
                hbf[:, i * 512:i * 512 + nn], hsil[:, i * 512:i * 512 + nn],
                pu[:, :nn], OP.mult)),
                reads=[("sil", i), puk, ("HBF", i), ("HSL", i)],
                writes=[("hb", i), ("HBF", i)])
            a.dma("scalar", h_d[128 * ic:128 * (ic + 1), n0:n0 + nn],
                  hbf[:, i * 512:i * 512 + nn],
                  reads=[("hb", i)], writes=[("h_d", ic, n0), ("HBF", i)])
    _drain_dmas(a)
    a.phase()
    r5.close()

    r6 = contextlib.ExitStack()
    hres = sb("hres", [128, NKI * 512], BF16, r6)
    dstrip = sb("dstrip", [128, 2 * NKI * 128], BF16, r6)
    WSP = NT[0][1]
    for i, (n0, nn) in enumerate(NT):
        a.dma("sync",
              hres[:, 0:NKI * WSP].rearrange("p (k c) -> p k c", k=NKI, c=WSP),
              h_d[0:FI, n0:n0 + nn].rearrange("(k p) c -> p k c", k=NKI, p=128),
              reads=[("h_d", k, n0) for k in range(NKI)]
                    + ([("dmm", i - 1, m2) for m2 in (KC - 1,)] if i >= 1 else []),
              writes=[("hr", i)])
        for m in range(KC):
            r = m // 2
            cm = (m % 2) * 128
            slot = m % 2
            prev = ("dmm", i, m - 2) if m >= 2 else (("dmm", i - 1, KC - 2) if i >= 1 else None)
            a.dma("sync",
                  dstrip[:, slot * NKI * 128:(slot + 1) * NKI * 128].rearrange(
                      "p (k c) -> p k c", k=NKI, c=128),
                  w8d_g[FI * r:FI * (r + 1), cm:cm + 128].rearrange(
                      "(k p) c -> p k c", k=NKI, p=128),
                  reads=["w8d_g"] + ([prev] if prev else []),
                  writes=[("dw", slot)])
            ps = PS[m % 2]
            pk = ("PSM", id(ps))
            for k in range(NKI):
                a.op("tensor", (lambda e, slot=slot, k=k, n0=n0, nn=nn, ps=ps: e.matmul(
                    ps[:, :nn],
                    dstrip[:, slot * NKI * 128 + k * 128:slot * NKI * 128 + (k + 1) * 128],
                    hres[:, k * WSP:k * WSP + nn],
                    start=(k == 0), stop=(k == NKI - 1))),
                    reads=[("dw", slot), ("hr", i)],
                    writes=[pk] + ([("dmm", i, m)] if k == NKI - 1 else []))
            mk_resid_evac(a, xstg, [(n0, nn)], "ff", x2p_d, x2T,
                          (lambda mm, nn0: ("x2p", mm, nn0)), "x2T")(
                a, (m, n0, nn), ps, pk)
    a.finish()
    a.phase()
    r6.close()
    r3.close()
    return nc


def build_l3(S, ctx=None):
    T = S // NCORES
    TC = T // 128
    NT = nspans(T)
    ND = nspans(D)
    FI = 5504
    NKI = FI // 128
    NSI = nspans(FI)
    nc = _NCPfx(ctx["nc"], ctx["pfx"]) if ctx else bass.Bass("TRN2", target_bir_lowering=False)
    xT = ctx["x_in"] if ctx else nc.dram_tensor("xT", [D, T], F32, kind="ExternalInput").ap()
    wbc_d = nc.dram_tensor("wbc", [D, 1024], F32, kind="ExternalInput").ap()
    wz_d = nc.dram_tensor("wz", [D, 1], F32, kind="ExternalInput").ap()
    win_d = nc.dram_tensor("win", [D, 512], F32, kind="ExternalInput").ap()
    wout_d = nc.dram_tensor("wout", [512, D], F32, kind="ExternalInput").ap()
    convw_d = nc.dram_tensor("convw", [128, 64], F32, kind="ExternalInput").ap()
    convb_d = nc.dram_tensor("convb", [128, 16], F32, kind="ExternalInput").ap()
    alog_d = nc.dram_tensor("alog", [128, 4], F32, kind="ExternalInput").ap()
    dcoef_d = nc.dram_tensor("dcoef", [128, 16], F32, kind="ExternalInput").ap()
    msk_d = nc.dram_tensor("msk", [1, 16], F32, kind="ExternalInput").ap()
    gidx_d = nc.dram_tensor("gidx", [256, FI], I32, kind="ExternalInput").ap()
    uidx_d = nc.dram_tensor("uidx", [256, FI], I32, kind="ExternalInput").ap()
    didx_d = nc.dram_tensor("didx", [FI, 256], I32, kind="ExternalInput").ap()
    lut_d = nc.dram_tensor("lut", [1, 16], F32, kind="ExternalInput").ap()
    cdram = ctx["cdram"] if ctx else _mk_cdram(nc)
    x2T = ctx["x_out"] if ctx else nc.dram_tensor("x2T", [D, T], F32, kind="ExternalOutput").ap()
    hal_i = nc.dram_tensor("hal_i", [D, 3], F16)
    hal_g = nc.dram_tensor("hal_g", [NCORES * D, 3], F16)
    ed_i = nc.dram_tensor("ed_i", [128, 8], F32)
    ed_g = nc.dram_tensor("ed_g", [NCORES * 128, 8], F32)
    w8g_p = nc.dram_tensor("w8g_p", [256, FI], BF16)
    w8g_g = nc.dram_tensor("w8g_g", [D, FI], BF16)
    w8u_p = nc.dram_tensor("w8u_p", [256, FI], BF16)
    w8u_g = nc.dram_tensor("w8u_g", [D, FI], BF16)
    w8d_p = nc.dram_tensor("w8d_p", [FI, 256], BF16)
    w8d_g = nc.dram_tensor("w8d_g", [NCORES * FI, 256], BF16)
    h_d = nc.dram_tensor("h_d", [FI, T], BF16)
    x2p_d = nc.dram_tensor("x2p", [D, T], F32)
    xn_d = nc.dram_tensor("xn_d", [D, T], F16)

    es = contextlib.ExitStack()
    with es:
        a = ctx["a"] if ctx else Auto(nc, es)
        sb = lambda n, shp, dt, st=es, side=None: st.enter_context(
            nc.sbuf_tensor(n, shp, dt, side=side))
        u = mk_utils(a, nc, es, T)
        PS = [es.enter_context(nc.psum_tensor(f"PS{i}", [128, 512], F32)) for i in range(5)]
        PR = [es.enter_context(nc.psum_tensor("PR0", [1, 512], F32))]
        u["pstat"] = [PS[0], PS[1], PS[2], PR[0]]
        u["psb"] = PS[3:5]
        stage = sb("stage", [128, 2 * D], F32)
        scl = sb("scl", [128, 8], F32)
        srow = sb("srow", [1, 24], F32)
        convw_sb = sb("convw_sb", [128, 64], F32)
        convb_sb = sb("convb_sb", [128, 16], F32)
        alog_sb = sb("alog_sb", [128, 4], F32)
        acol = sb("acol", [128, 4], F32)
        dd_sb = sb("dd_sb", [128, 16], F32)
        mrow = sb("mrow", [1, 16], F32)
        mcz = sb("mcz", [128, 24], F32)
        sdcol = sb("sdcol", [128, 1], F32)
        drw = sb("drw", [1, 1024], F32)
        drw2 = sb("drw2", [1, 1024], F32)
        drw3 = sb("drw3", [1, 1024], F32)
        edst = sb("edst", [128, 8], F32)
        edl = sb("edl", [128, 64], F32)
        carry = sb("carry", [128, 4], F32)
        cfold = sb("cfold", [128, 2], F32)
        rP = contextlib.ExitStack()
        xc = sb("xc", [128, KC * T], F16, rP)
        hf16 = sb("hf16", [128, 4 * T], F16, rP)

        r0 = contextlib.ExitStack()
        x_sb = sb("x_sb", [128, KC * T], F32, r0)
        u["xbf"] = sb("xbf", [128, 2 * T], BF16, r0)
        u["sqbf"] = sb("sqbf", [128, 2 * T], BF16, r0)
        xn = sb("xn0", [128, KC * T], F16, r0)
        q_load_consts(a, u, cdram)
        a.op("vector", lambda e: e.memset(srow[:1, :], 0.0), writes=[("srow0",)])
        for dst, src, key in ((convw_sb, convw_d, "cw"), (convb_sb, convb_d, "cb"),
                              (alog_sb, alog_d, "al"), (dd_sb, dcoef_d, "dd"),
                              (mrow, msk_d, "mr")):
            a.dma("sync", dst[:dst.shape[0], :], src, writes=[(key,)])
        a.op("scalar", lambda e: e.activation(acol[:, :], alog_sb[:, :], AF.Exp),
             reads=[("al",)], writes=[("ac0",)])
        a.op("vector", lambda e: e.tensor_scalar(acol[:, :], acol[:, :], -1.0, None,
                                                 OP.mult),
             reads=[("ac0",)], writes=[("ac",)])
        for r in range(8):
            q_scalar_bcast(a, u, mrow[:1, r:r + 1], ("mr",), mcz[:, r:r + 1],
                           ("hm", r), PS[2])
            q_scalar_bcast(a, u, mrow[:1, 8 + r:9 + r], ("mr",), mcz[:, 8 + r:9 + r],
                           ("pm", r), PS[2])
            a.op("vector", (lambda e, r=r: e.tensor_scalar(
                mcz[:, 16 + r:17 + r], mcz[:, 8 + r:9 + r], -1.0, 1.0,
                OP.mult, OP.add)),
                reads=[("pm", r)], writes=[("ipm", r)])
        q_load_act(a, xT, x_sb, T, "x")
        q_layernorm(a, u, T, x_sb, "x", xn, "xn")
        for k in range(KC):
            a.dma("sync", hal_i[128 * k:128 * (k + 1), :],
                  xn[:, k * T + T - 3:k * T + T],
                  reads=[("xn", k)], writes=[("hal_i", k)])
        a.cc("AllGather", OP.bypass, hal_i, hal_g,
             [("hal_i", k) for k in range(KC)], "hal_g")
        q_store_act(a, xn, xn_d, T, "xn", "xn_d", eng="scalar")
        _drain_dmas(a)
        a.phase()
        r0.close()

        r1 = contextlib.ExitStack()
        bc_sb = sb("bc_sb", [128, 8 * T], F32, r1)
        w_sb = sb("w_sb", [128, KC * 1024], F16, r1)
        wzc = sb("wzc", [128, 16], F32, r1)
        wzc16 = sb("wzc16", [128, 16], F16, r1)
        r1x = contextlib.ExitStack()
        xnl = sb("xnl3", [128, KC * T], F16, r1x)
        q_load_act(a, xn_d, xnl, T, "xl", src_key="xn_d")
        XN = [("xl", k) for k in range(KC)]
        q_stream_weight(a, wbc_d, w_sb, KC, 1024, stage, "pb", "pbw")
        bgroups = [(m, n0, nn) for m in range(8) for (n0, nn) in NT]
        q_mm(a, "pbmm", bgroups,
             lambda g, k: w_sb[:, k * 1024 + g[0] * 128:k * 1024 + (g[0] + 1) * 128],
             lambda g, k: xnl[:, k * T + g[1]:k * T + g[1] + g[2]],
             KC, PS[0:2],
             lambda a_, g, ps, pk: a_.op(
                 "vector",
                 (lambda e, g=g, ps=ps: e.tensor_copy(
                     bc_sb[:, g[0] * T + g[1]:g[0] * T + g[1] + g[2]], ps[:, :g[2]])),
                 reads=[pk], writes=[("bc", g[0], g[1])]),
             lambda g: XN + [("pbw", k) for k in range(KC)],
             lambda g, ps: ps[:, :g[2]])
        for k in range(KC):
            a.dma("sync", wzc[:, k:k + 1], wz_d[128 * k:128 * (k + 1), :],
                  writes=[("wz", k)])
        a.op("vector", lambda e: e.tensor_copy(wzc16[:, :], wzc[:, :]),
             reads=[("wz", k2) for k2 in range(KC)], writes=[("wz16",)])
        for i, (n0, nn) in enumerate(NT):
            for k in range(KC):
                a.op("tensor", (lambda e, k=k, n0=n0, nn=nn: e.matmul(
                    PR[0][:1, :nn], wzc16[:, k:k + 1],
                    xnl[:, k * T + n0:k * T + n0 + nn],
                    start=(k == 0), stop=(k == KC - 1))),
                    reads=[("wz16",)] + XN, writes=[("PSM", id(PR[0]))])
            a.op("vector", (lambda e, n0=n0, nn=nn: e.tensor_copy(
                drw[:1, n0:n0 + nn], PR[0][:1, :nn])),
                reads=[("PSM", id(PR[0]))], writes=[("zd", i)])
        zdk = [("zd", i2) for i2 in range(len(NT))]
        a.op("scalar", lambda e: e.activation(drw2[:1, 0:T], drw[:1, 0:T], AF.Relu),
             reads=zdk, writes=[("sp1",)])
        a.op("scalar", lambda e: e.activation(drw3[:1, 0:T], drw[:1, 0:T], AF.Abs),
             reads=zdk, writes=[("sp2",)])
        a.op("scalar", lambda e: e.activation(drw3[:1, 0:T], drw3[:1, 0:T], AF.Exp,
                                              scale=-1.0),
             reads=[("sp2",)], writes=[("sp2",)])
        a.op("vector", lambda e: e.tensor_scalar(drw3[:1, 0:T], drw3[:1, 0:T], 1.0,
                                                 None, OP.add),
             reads=[("sp2",)], writes=[("sp2",)])
        a.op("scalar", lambda e: e.activation(drw3[:1, 0:T], drw3[:1, 0:T], AF.Ln),
             reads=[("sp2",)], writes=[("sp2",)])
        a.op("vector", lambda e: e.tensor_tensor(drw[:1, 0:T], drw2[:1, 0:T],
                                                 drw3[:1, 0:T], OP.add),
             reads=[("sp1",), ("sp2",)], writes=[("dl",)])
        a.op("vector", lambda e: e.reduce_sum(srow[:1, 1:2], drw[:1, 0:T], axis=AX.X),
             reads=[("dl",), ("srow0",)], writes=[("sd",)])
        q_scalar_bcast(a, u, srow[:1, 1:2], ("sd",), sdcol[:, 0:1], ("sdc",), PS[2])
        a.phase()

        r2 = contextlib.ExitStack()
        hstg = sb("hstg", [128, 8 * 48], F16, r2)
        halo = sb("halo", [128, 48], F32, r2)
        xe = sb("xe", [128, T + 3], F32, r2)
        xct = sb("xct", [128, T], F32, r2)
        for r in range(8):
            a.dma("sync", hstg[:, r * 48:(r + 1) * 48].rearrange(
                      "p (k c) -> p k c", k=KC, c=3),
                  hal_g[r * D:(r + 1) * D, :].rearrange(
                      "(k p) c -> p k c", k=KC, p=128),
                  reads=["hal_g"], writes=[("hst", r, 0)])
        hstk = [("hst", r, 0) for r in range(8)]
        for r in range(8):
            if r == 0:
                a.op("vector", lambda e: e.tensor_scalar(
                    halo[:, :], hstg[:, 0:48], mcz[:, 0:1], None, OP.mult),
                    reads=hstk + [("hm", 0)], writes=[("halo",)])
            else:
                a.op("vector", (lambda e, r=r: e.scalar_tensor_tensor(
                    halo[:, :], hstg[:, r * 48:(r + 1) * 48], mcz[:, r:r + 1],
                    halo[:, :], OP.mult, OP.add)),
                    reads=hstk + [("hm", r), ("halo",)], writes=[("halo",)])
        for k in range(KC):
            a.op("vector", (lambda e, k=k: e.tensor_copy(
                xe[:, 0:3], halo[:, 3 * k:3 * (k + 1)])),
                reads=[("halo",), ("XE",)], writes=[("xe", "h"), ("XE",)])
            a.op("vector", (lambda e, k=k: e.tensor_copy(
                xe[:, 3:T + 3], xnl[:, k * T:(k + 1) * T])),
                reads=[("xl", k), ("XE",)], writes=[("xe", "m"), ("XE",)])
            a.op("vector", (lambda e, k=k: e.tensor_scalar(
                xct[:, :], xe[:, 0:T], convw_sb[:, 4 * k:4 * k + 1], None, OP.mult)),
                reads=[("xe", "h"), ("xe", "m"), ("XE",), ("cw",), ("XCT",)],
                writes=[("xct",), ("XCT",)])
            for j in range(1, 4):
                a.op("vector", (lambda e, k=k, j=j: e.scalar_tensor_tensor(
                    xct[:, :], xe[:, j:j + T], convw_sb[:, 4 * k + j:4 * k + j + 1],
                    xct[:, :], OP.mult, OP.add)),
                    reads=[("xe", "h"), ("xe", "m"), ("XE",), ("cw",), ("xct",), ("XCT",)],
                    writes=[("xct",), ("XCT",)])
            a.op("vector", (lambda e, k=k: e.tensor_scalar(
                xc[:, k * T:(k + 1) * T], xct[:, :], convb_sb[:, k:k + 1], None,
                OP.add)),
                reads=[("xct",), ("XCT",), ("cb",)], writes=[("xc", k)])
        a.phase()
        r2.close()
        r1x.close()
        XC = [("xc", k) for k in range(KC)]

        r4 = contextlib.ExitStack()
        xs_sb = sb("xs_sb", [128, 4 * T], F32, r4)
        q_stream_weight(a, win_d, w_sb, KC, 512, stage, "pw", "pww")
        sgroups = [(m, n0, nn) for m in range(4) for (n0, nn) in NT]
        q_mm(a, "psmm", sgroups,
             lambda g, k: w_sb[:, k * 512 + g[0] * 128:k * 512 + (g[0] + 1) * 128],
             lambda g, k: xc[:, k * T + g[1]:k * T + g[1] + g[2]],
             KC, PS[0:2],
             lambda a_, g, ps, pk: a_.op(
                 "vector",
                 (lambda e, g=g, ps=ps: e.tensor_copy(
                     xs_sb[:, g[0] * T + g[1]:g[0] * T + g[1] + g[2]], ps[:, :g[2]])),
                 reads=[pk], writes=[("xs", g[0], g[1])]),
             lambda g: XC + [("pww", k) for k in range(KC)],
             lambda g, ps: ps[:, :g[2]])
        a.phase()

        dbc = sb("dbc", [128, T], F32, r4)
        da = sb("da", [128, 4 * T], F32, r4)
        hloc = bc_sb
        for i, (n0, nn) in enumerate(NT):
            a.op("tensor", (lambda e, n0=n0, nn=nn: e.matmul(
                PS[4][:, :nn], u["ones_row32"][:, :], drw[:1, n0:n0 + nn],
                start=True, stop=True)),
                reads=[("dl",), "c"], writes=[("PSM", id(PS[4]))])
            a.op("vector", (lambda e, n0=n0, nn=nn: e.tensor_copy(
                dbc[:, n0:n0 + nn], PS[4][:, :nn])),
                reads=[("PSM", id(PS[4]))], writes=[("dbc", i)])
        DBC = [("dbc", i) for i in range(len(NT))]
        xsk = lambda sc: [("xs", sc, n0) for (n0, _) in NT]
        for sc in range(4):
            a.op("vector", (lambda e, sc=sc: e.tensor_tensor(
                xs_sb[:, sc * T:(sc + 1) * T], xs_sb[:, sc * T:(sc + 1) * T],
                bc_sb[:, sc * T:(sc + 1) * T], OP.mult)),
                reads=xsk(sc) + [("bc", sc, n0) for (n0, _) in NT],
                writes=[("bu", sc)])
            a.op("vector", (lambda e, sc=sc: e.tensor_tensor(
                xs_sb[:, sc * T:(sc + 1) * T], xs_sb[:, sc * T:(sc + 1) * T],
                dbc[:, :], OP.mult)),
                reads=[("bu", sc)] + DBC, writes=[("bu", sc)])
            a.op("vector", (lambda e, sc=sc: e.tensor_scalar(
                da[:, sc * T:(sc + 1) * T], dbc[:, :], acol[:, sc:sc + 1], None,
                OP.mult)),
                reads=DBC + [("ac",)], writes=[("da0", sc)])
            a.op("scalar", (lambda e, sc=sc: e.activation(
                da[:, sc * T:(sc + 1) * T], da[:, sc * T:(sc + 1) * T], AF.Exp)),
                reads=[("da0", sc)], writes=[("da", sc)])
            a.op("vector", (lambda e, sc=sc: e.tensor_tensor_scan(
                hloc[:, sc * T:(sc + 1) * T], da[:, sc * T:(sc + 1) * T],
                xs_sb[:, sc * T:(sc + 1) * T], 0.0, OP.mult, OP.add)),
                reads=[("da", sc), ("bu", sc)],
                writes=[("hl", sc)] + [("bc", sc, n0) for (n0, _) in NT])
            a.op("vector", (lambda e, sc=sc: e.tensor_copy(
                edst[:, sc:sc + 1], hloc[:, sc * T + T - 1:sc * T + T])),
                reads=[("hl", sc)], writes=[("ed", sc)])
            a.op("vector", (lambda e, sc=sc: e.tensor_tensor(
                edst[:, 4 + sc:5 + sc], acol[:, sc:sc + 1], sdcol[:, 0:1], OP.mult)),
                reads=[("ac",), ("sdc",)], writes=[("ed0", sc)])
            a.op("scalar", (lambda e, sc=sc: e.activation(
                edst[:, 4 + sc:5 + sc], edst[:, 4 + sc:5 + sc], AF.Exp)),
                reads=[("ed0", sc)], writes=[("ed", 4 + sc)])
        a.dma("sync", ed_i[:, :], edst[:, :],
              reads=[("ed", j) for j in range(8)], writes=[("ed_i",)])
        a.cc("AllGather", OP.bypass, ed_i, ed_g, [("ed_i",)], "ed_g")
        for r in range(8):
            a.dma("sync", edl[:, 8 * r:8 * (r + 1)], ed_g[128 * r:128 * (r + 1), :],
                  reads=["ed_g"], writes=[("edl", r)])
        edlk = [("edl", r) for r in range(8)]
        a.op("vector", lambda e: e.memset(carry[:, :], 0.0), writes=[("cy",)])
        for r in range(8):
            for sc in range(4):
                a.op("vector", (lambda e, r=r, sc=sc: e.scalar_tensor_tensor(
                    cfold[:, 0:1], edl[:, 8 * r + 4 + sc:8 * r + 5 + sc],
                    mcz[:, 8 + r:9 + r], mcz[:, 16 + r:17 + r], OP.mult, OP.add)),
                    reads=edlk + [("pm", r), ("ipm", r), ("CF",)],
                    writes=[("cfd", r, sc), ("CF",)])
                a.op("vector", (lambda e, r=r, sc=sc: e.tensor_scalar(
                    cfold[:, 1:2], edl[:, 8 * r + sc:8 * r + 1 + sc],
                    mcz[:, 8 + r:9 + r], None, OP.mult)),
                    reads=edlk + [("pm", r), ("CF",)], writes=[("cfe", r, sc), ("CF",)])
                a.op("vector", (lambda e, r=r, sc=sc: e.scalar_tensor_tensor(
                    carry[:, sc:sc + 1], carry[:, sc:sc + 1], cfold[:, 0:1],
                    cfold[:, 1:2], OP.mult, OP.add)),
                    reads=[("cfd", r, sc), ("cfe", r, sc), ("cy",), ("CF",)],
                    writes=[("cy",)])
        for sc in range(4):
            a.op("vector", (lambda e, sc=sc: e.tensor_tensor_scan(
                hloc[:, sc * T:(sc + 1) * T], da[:, sc * T:(sc + 1) * T],
                xs_sb[:, sc * T:(sc + 1) * T], carry[:, sc:sc + 1],
                OP.mult, OP.add)),
                reads=[("da", sc), ("bu", sc), ("cy",), ("ed", sc)],
                writes=[("hl2", sc)])
            a.op("vector", (lambda e, sc=sc: e.tensor_tensor(
                hloc[:, sc * T:(sc + 1) * T], hloc[:, sc * T:(sc + 1) * T],
                bc_sb[:, (4 + sc) * T:(5 + sc) * T], OP.mult)),
                reads=[("hl2", sc)] + [("bc", 4 + sc, n0) for (n0, _) in NT],
                writes=[("hmod", sc)])
            a.op("vector", (lambda e, sc=sc: e.tensor_copy(
                hf16[:, sc * T:(sc + 1) * T], hloc[:, sc * T:(sc + 1) * T])),
                reads=[("hmod", sc)], writes=[("hf", sc)])
        a.phase()
        r4.close()

        r5 = contextlib.ExitStack()
        xstg = sb("xstg", [128, 4 * 512], F32, r5)
        dtt = sb("dtt", [128, 512], F32, r5)
        q_stream_weight(a, wout_d, w_sb, 4, D, stage, "po", "pow")
        HF = [("hf", sc) for sc in range(4)]
        ogroups = [(m, n0, nn) for m in range(KC) for (n0, nn) in NT]
        resid = mk_resid_evac(a, xstg, NT, "rp", xT, x2p_d, None, "x2p")

        def o_evac(a_, g, ps, pk):
            m, n0, nn = g
            a_.op("vector", (lambda e, m=m, n0=n0, nn=nn: e.tensor_scalar(
                dtt[:, :nn], xc[:, m * T + n0:m * T + n0 + nn],
                dd_sb[:, m:m + 1], None, OP.mult)),
                reads=[("xc", m), ("dd",), ("DTT",)], writes=[("dt",), ("DTT",)])
            a_.op("vector", (lambda e, nn=nn, ps=ps: e.tensor_tensor(
                ps[:, :nn], ps[:, :nn], dtt[:, :nn], OP.add)),
                reads=[pk, ("dt",), ("DTT",)], writes=[pk, ("DTT",)])
            resid(a_, g, ps, pk)
        q_mm(a, "pomm", ogroups,
             lambda g, k: w_sb[:, k * D + g[0] * 128:k * D + (g[0] + 1) * 128],
             lambda g, k: hf16[:, k * T + g[1]:k * T + g[1] + g[2]],
             4, PS[0:2], o_evac,
             lambda g: HF + [("pow", k) for k in range(4)],
             lambda g, ps: ps[:, :g[2]])
        _drain_dmas(a)
        a.phase()
        r5.close()
        r1.close()
        rP.close()
        r3 = contextlib.ExitStack()
        return _l2_ffn(nc, es, a, u, sb, PS, PR, locals(), r3, None, None)
    return nc


# version 08
# speedup vs baseline: 1.0929x; 1.0004x over previous
import contextlib
import numpy as np
import ml_dtypes
import concourse.bass as bass
import concourse.mybir as mybir
from concourse.bass_utils import run_bass_kernel_spmd
from concourse.aot_env import allow_ici_all_to_all

F32 = mybir.dt.float32
BF16 = mybir.dt.bfloat16
I32 = mybir.dt.int32
F16 = mybir.dt.float16
AF = mybir.ActivationFunctionType
OP = mybir.AluOpType
AX = mybir.AxisListType

NCORES = 8
D = 2048
KC = D // 128
ENGINES = ("sync", "vector", "scalar", "tensor", "gpsimd")


class _Op:
    __slots__ = ("eng", "fn", "kind", "deps", "phase", "idx", "needs_inc",
                 "sem", "count", "tag")

    def __init__(self, eng, fn, kind, phase, idx, tag=""):
        self.eng, self.fn, self.kind = eng, fn, kind
        self.phase, self.idx = phase, idx
        self.deps = []
        self.needs_inc = False
        self.sem = None
        self.count = None
        self.tag = tag


class Auto:

    def __init__(self, nc, es):
        self.nc = nc
        self.es = es
        self.phase_no = 0
        self.pending = {e: [] for e in ENGINES}
        self.last_write = {}
        self.readers = {}
        self.sems = {}
        self.counts = {}
        self.rot = {}
        self.out_dma_keys = []

    def _sem(self, name):
        if name not in self.sems:
            self.sems[name] = self.es.enter_context(self.nc.semaphore(name))
            self.counts[name] = 0
        return self.sems[name]

    def op(self, eng, fn, reads=(), writes=(), kind="c", tag=""):
        rec = _Op(eng, fn, kind, self.phase_no, len(self.pending[eng]), tag)
        deps = []
        for k in list(reads) + list(writes):
            w = self.last_write.get(k)
            if w is not None:
                deps.append(w)
        for k in writes:
            deps.extend(self.readers.get(k, ()))
        seen = set()
        rec.deps = [d for d in deps if not (id(d) in seen or seen.add(id(d)))]
        for k in writes:
            self.last_write[k] = rec
            self.readers[k] = []
        for k in reads:
            self.readers.setdefault(k, []).append(rec)
        self.pending[eng].append(rec)
        return rec

    def dma(self, eng, out, in_, reads=(), writes=(), tag=""):
        return self.op(eng, (lambda e, out=out, in_=in_: e.dma_start(out=out, in_=in_)),
                       reads=reads, writes=writes, kind="dma", tag=tag)

    def cc(self, kind_s, alu, src_dram, dst_dram, rkey, wkey):
        def fn(e):
            with allow_ici_all_to_all():
                return e.collective_compute(
                    kind_s, alu, replica_groups=[list(range(NCORES))],
                    ins=[src_dram.ap().opt()], outs=[dst_dram.ap().opt()])
        rk = rkey if isinstance(rkey, list) else [rkey]
        return self.op("gpsimd", fn, reads=rk, writes=[wkey], kind="cc")

    def phase(self, name=""):
        cur = self.phase_no
        for eng in ENGINES:
            for rec in self.pending[eng]:
                for d in rec.deps:
                    if d.kind == "c" and d.eng == eng and d.phase == cur:
                        pass
                    elif d.kind == "c" and d.phase < cur:
                        pass
                    else:
                        d.needs_inc = True
        for eng in ENGINES:
            for rec in self.pending[eng]:
                if rec.kind == "dma":
                    r = self.rot.get(eng, 0)
                    self.rot[eng] = r + 1
                    rec.sem = f"D_{eng}_{r % 8}"
                    rec.needs_inc = True
                    self._sem(rec.sem)
                    self.counts[rec.sem] += 16
                    rec.count = self.counts[rec.sem]
                elif rec.kind == "cc":
                    r = self.rot.get("cc", 0)
                    self.rot["cc"] = r + 1
                    rec.sem = f"CC{r % 4}"
                    rec.needs_inc = True
                    self._sem(rec.sem)
                    self.counts[rec.sem] += 1
                    rec.count = self.counts[rec.sem]
                elif rec.needs_inc:
                    rec.sem = "P_" + eng
                    self._sem(rec.sem)
                    self.counts[rec.sem] += 1
                    rec.count = self.counts[rec.sem]
        scope = contextlib.ExitStack()
        scope.enter_context(self.nc.named_scope(f"ph{cur:03d}{name}"))
        with self.nc.Block() as block:
            for eng in ENGINES:
                recs = self.pending[eng]
                if not recs:
                    continue
                dec = getattr(block, eng)

                def body(e, recs=recs, eng=eng):
                    watermark = {}
                    last_drain = -1
                    for rec in recs:
                        need_drain = False
                        waits = {}
                        for d in rec.deps:
                            if d.kind == "c" and d.phase < cur:
                                continue
                            if d.kind == "c" and d.eng == eng:
                                if d.idx > last_drain and eng != "tensor":
                                    need_drain = True
                                elif d.idx > last_drain and eng == "tensor":
                                    pass
                                continue
                            if d.sem is None:
                                continue
                            waits[d.sem] = max(waits.get(d.sem, 0), d.count)
                        if rec.kind == "dma" and rec.count > 16:
                            waits[rec.sem] = max(waits.get(rec.sem, 0), rec.count - 16)
                        elif rec.kind == "cc" and rec.count > 1:
                            waits[rec.sem] = max(waits.get(rec.sem, 0), rec.count - 1)
                        if need_drain:
                            e.drain()
                            last_drain = rec.idx - 1
                        for s, c in waits.items():
                            if watermark.get(s, -1) < c:
                                e.wait_ge(self.sems[s], c)
                                watermark[s] = c
                        ins = rec.fn(e)
                        if rec.needs_inc:
                            if rec.kind == "dma":
                                ins.then_inc(self.sems[rec.sem], 16)
                            else:
                                ins.then_inc(self.sems[rec.sem], 1)

                dec(body)
        scope.close()
        self.pending = {e: [] for e in ENGINES}
        self.phase_no += 1

    def finish(self, eng="sync"):
        def fn(e):
            for s, c in sorted(self.counts.items()):
                if c > 0:
                    e.wait_ge(self.sems[s], c)
            return e.nop()
        self.op(eng, fn, reads=(), writes=(), kind="c")


def ceil_div(x, y):
    return (x + y - 1) // y


def nspans(T, W=512):
    return [(n0, min(W, T - n0)) for n0 in range(0, T, W)]


def q_load_act(a, dram_ap, sb, T, key, kin=KC, src_key=None):
    for k in range(kin):
        a.dma("sync", sb[:, k * T:(k + 1) * T], dram_ap[128 * k:128 * (k + 1), :],
              reads=[src_key] if src_key else (), writes=[(key, k)])


def q_store_act(a, sb, dram_ap, T, key, dst_key, kin=KC, eng="sync"):
    for k in range(kin):
        a.dma(eng, dram_ap[128 * k:128 * (k + 1), :], sb[:, k * T:(k + 1) * T],
              reads=[(key, k)], writes=[(dst_key, k)])


def q_stream_weight(a, w_dram, w_sb, kin, M, stage, key, wkey, quant=None,
                    src_key=None):
    stgk = lambda sl_: ("STG", id(stage), sl_)
    for k in range(kin):
        sl = stage[:, (k % 2) * M:(k % 2 + 1) * M]
        a.dma("sync", sl, w_dram[128 * k:128 * (k + 1), :],
              reads=([src_key] if src_key else []) + [stgk(k % 2)],
              writes=[(key, "stg", k % 2), stgk(k % 2)])
        dst = w_sb[:, k * M:(k + 1) * M]
        if quant is None:
            a.op("vector", (lambda e, sl=sl, dst=dst: e.tensor_copy(dst, sl)),
                 reads=[(key, "stg", k % 2), stgk(k % 2)], writes=[(wkey, k)])
        else:
            quant(a, sl, dst, k, (key, "stg", k % 2), (wkey, k))
            a.op("vector", lambda e: e.nop(), reads=[(wkey, k)], writes=[stgk(k % 2)])


def q_mm(a, tag, groups, lhs_fn, rhs_fn, kin, psums, evac, rkeys_fn, out_ap_fn):
    np_ = len(psums)
    for gi, g in enumerate(groups):
        ps = psums[gi % np_]
        pskey = ("PSM", id(ps))
        rk = list(rkeys_fn(g))
        for k in range(kin):
            def fn(e, g=g, k=k, ps=ps):
                return e.matmul(out_ap_fn(g, ps), lhs_fn(g, k), rhs_fn(g, k),
                                start=(k == 0), stop=(k == kin - 1))
            a.op("tensor", fn, reads=rk, writes=[pskey], tag=tag)
        evac(a, g, ps, pskey)


class Rounder:
    def __init__(self, a, ctx_sbuf, name, width):
        self.i1 = ctx_sbuf(name + "_ri", [128, width], I32)
        self.f1 = ctx_sbuf(name + "_rf1", [128, width], F32)
        self.f2 = ctx_sbuf(name + "_rf2", [128, width], F32)
        self.name = name

    def round(self, a, z_ap, out_ap, n, zkey, okey):
        i1, f1, f2 = self.i1[:, :n], self.f1[:, :n], self.f2[:, :n]
        nm = self.name
        a.op("vector", lambda e: e.tensor_scalar(f1, z_ap, 8192.5, None, OP.add),
             reads=[zkey], writes=[(nm, "f1")])
        a.op("vector", lambda e: e.tensor_copy(i1, f1),
             reads=[(nm, "f1")], writes=[(nm, "i1")])
        a.op("vector", lambda e: e.tensor_copy(f2, i1),
             reads=[(nm, "i1")], writes=[(nm, "f2")])
        a.op("vector", lambda e: e.scalar_tensor_tensor(f1, f2, -1.0, f1, OP.mult, OP.add),
             reads=[(nm, "f1"), (nm, "f2")], writes=[(nm, "f1")])
        a.op("vector", lambda e: e.tensor_scalar(f1, f1, -0.5, None, OP.add),
             reads=[(nm, "f1")], writes=[(nm, "f1")])
        a.op("vector", lambda e: e.tensor_copy(i1, f1),
             reads=[(nm, "f1")], writes=[(nm, "i1")])
        a.op("vector", lambda e: e.tensor_copy(f1, i1),
             reads=[(nm, "i1")], writes=[(nm, "f1")])
        a.op("vector", lambda e: e.tensor_tensor(f2, f2, f1, OP.add),
             reads=[(nm, "f1"), (nm, "f2")], writes=[(nm, "f2")])
        a.op("vector", lambda e: e.tensor_scalar(out_ap, f2, -8192.0, None, OP.add),
             reads=[(nm, "f2")], writes=[okey])


_S = 8192
_DEINT = np.concatenate([np.arange(0, 128, 2), np.arange(1, 128, 2)])
_PERM = [c for p in range(8) for c in (p, p + 8)]


def _host_consts():
    return {
        "c_ones_col": np.ones((128, 1), dtype=ml_dtypes.bfloat16),
        "c_ones_row": np.ones((1, 128), dtype=ml_dtypes.bfloat16),
        "c_id32": np.eye(128, dtype=np.float32),
        "c_ones_row32": np.ones((1, 128), dtype=np.float32),
        "c_idbf": np.eye(128, dtype=ml_dtypes.bfloat16),
    }


def _host_cs_cm(S):
    th = (1.0 / 10000.0 ** (np.arange(64, dtype=np.float32) / np.float32(64))).astype(np.float32)
    t = np.arange(S, dtype=np.float32)
    ang = np.outer(th, t)
    cs = np.concatenate([np.cos(ang), np.sin(ang)], axis=0).astype(np.float16)
    cm = np.zeros((128, 4 * 512), dtype=np.float32)
    qq = np.arange(128)[:, None]
    kk = np.arange(512)[None, :]
    for j in range(4):
        cm[:, j * 512:(j + 1) * 512] = np.where(kk <= qq + 128 * j, 0.0, -1e9)
    return cs, cm.astype(ml_dtypes.bfloat16)


def _launch(nc, in_maps, out_name):
    r = run_bass_kernel_spmd(nc, in_maps, core_ids=list(range(NCORES)))
    return np.concatenate([np.asarray(r.results[i][out_name])
                           for i in range(NCORES)], axis=1)


def kernel(**inputs):
    S, T = _S, _S // NCORES
    g = {k: np.asarray(v) for k, v in inputs.items()}
    c = _host_consts()
    cs, cm = _host_cs_cm(S)
    xT = np.ascontiguousarray(g["L0_attn_x"][0].T)

    base = dict(c)
    base.update(b0_wq=g["L0_attn_Wq"], b0_wk=g["L0_attn_Wk"], b0_wv=g["L0_attn_Wv"],
                b0_wo=g["L0_attn_Wo"],
                b0_wf=np.ascontiguousarray(g["L0_ffn_W"].T),
                b0_gamma_pk=np.ascontiguousarray(g["L0_ffn_gamma"].reshape(16, 128).T))

    colperm = np.concatenate([np.arange(128 * h, 128 * (h + 1)) for h in _PERM])
    wfcT = np.ascontiguousarray(g["L1_ffn_w_fc"].T)
    wpjT = np.ascontiguousarray(g["L1_ffn_w_proj"].T)
    slopes = (2.0 ** (-8.0 * np.arange(1, 17, dtype=np.float32) / np.float32(16))
              ).astype(np.float32)
    base.update(b1_wq=np.ascontiguousarray(g["L1_attn_Wq"][:, colperm]),
                b1_wk=g["L1_attn_Wk"], b1_wv=g["L1_attn_Wv"],
                b1_wo=np.ascontiguousarray(g["L1_attn_Wo"][colperm, :]),
                b1_cs=cs, b1_cmask=cm)

    M = np.ascontiguousarray(g["L2_attn_W_qkv"].T)
    wq2 = M[:, :2048].copy(); wk2 = M[:, 2048:4096].copy(); wv2 = M[:, 4096:].copy()
    for h in range(16):
        wq2[:, 128 * h:128 * (h + 1)] = wq2[:, 128 * h + _DEINT]
        wk2[:, 128 * h:128 * (h + 1)] = wk2[:, 128 * h + _DEINT]
    gT2 = np.ascontiguousarray(g["L2_ffn_gate_idx"].T)
    uT2 = np.ascontiguousarray(g["L2_ffn_up_idx"].T)
    dT2 = np.ascontiguousarray(g["L2_ffn_down_idx"].T)
    base.update(b2_wq=wq2, b2_wk=wk2, b2_wv=wv2,
                b2_wo=np.ascontiguousarray(g["L2_attn_W_o"].T),
                b2_lut=g["L2_ffn_lut"].reshape(1, 16), b2_cs=cs, b2_cmask=cm,
                b2_slopes=np.zeros((1, 2), dtype=np.float32))

    M3 = np.ascontiguousarray(g["L3_attn_Wxproj"].T)
    cw = g["L3_attn_conv_w"]
    gT3 = np.ascontiguousarray(g["L3_ffn_gate_idx"].T)
    uT3 = np.ascontiguousarray(g["L3_ffn_up_idx"].T)
    dT3 = np.ascontiguousarray(g["L3_ffn_down_idx"].T)
    base.update(b3_wbc=np.ascontiguousarray(M3[:, :1024]),
                b3_wz=np.ascontiguousarray(M3[:, 1024:1025]),
                b3_win=np.ascontiguousarray(g["L3_attn_Winproj"].T),
                b3_wout=np.ascontiguousarray(g["L3_attn_Woutproj"].T),
                b3_convw=np.ascontiguousarray(np.transpose(
                    cw[:, 0, :].reshape(16, 128, 4), (1, 0, 2)).reshape(128, 64)),
                b3_convb=np.ascontiguousarray(g["L3_attn_conv_b"].reshape(16, 128).T),
                b3_alog=np.ascontiguousarray(g["L3_attn_A_log"].reshape(4, 128).T),
                b3_dcoef=np.ascontiguousarray(g["L3_attn_D"].reshape(16, 128).T),
                b3_lut=g["L3_ffn_lut"].reshape(1, 16))

    maps = []
    for ci in range(NCORES):
        msk = np.zeros((1, 16), dtype=np.float32)
        if ci > 0:
            msk[0, ci - 1] = 1.0
        msk[0, 8:8 + ci] = 1.0
        m = dict(base, b3_msk=msk,
                 b0_xT=np.ascontiguousarray(xT[:, ci * T:(ci + 1) * T]),
                 b1_wfcs=np.ascontiguousarray(wfcT[:, ci * 1024:(ci + 1) * 1024]),
                 b1_wpjs=np.ascontiguousarray(wpjT[:, ci * 256:(ci + 1) * 256]),
                 b1_slopes=np.array([[slopes[ci], slopes[ci + 8]]], dtype=np.float32),
                 b2_gidx=np.ascontiguousarray(gT2[256 * ci:256 * (ci + 1), :]),
                 b2_uidx=np.ascontiguousarray(uT2[256 * ci:256 * (ci + 1), :]),
                 b2_didx=np.ascontiguousarray(dT2[:, 256 * ci:256 * (ci + 1)]),
                 b3_gidx=np.ascontiguousarray(gT3[256 * ci:256 * (ci + 1), :]),
                 b3_uidx=np.ascontiguousarray(uT3[256 * ci:256 * (ci + 1), :]),
                 b3_didx=np.ascontiguousarray(dT3[:, 256 * ci:256 * (ci + 1)]))
        maps.append(m)

    x4T = _launch(build_fused(S), maps, "x4T")
    return np.ascontiguousarray(x4T.T)[None].astype(np.float32)


def q_bcast_row(a, u, row_ap, rowkey, out_sb, outkey, T, tmptag):
    rowbf = u["rowbf"]
    a.op("vector", lambda e: e.tensor_copy(rowbf[:, :T], row_ap),
         reads=[rowkey], writes=[(tmptag, "rowbf")])
    for i, (n0, nn) in enumerate(nspans(T)):
        ps = u["psb"][i % 2]
        def mm(e, n0=n0, nn=nn, ps=ps):
            return e.matmul(ps[:, :nn], u["ones_row"][:, :], rowbf[:, n0:n0 + nn],
                            start=True, stop=True)
        a.op("tensor", mm, reads=[(tmptag, "rowbf"), "c"], writes=[("PSM", id(ps))])
        def ev(e, n0=n0, nn=nn, ps=ps):
            return e.tensor_copy(out_sb[:, n0:n0 + nn], ps[:, :nn])
        a.op("vector", ev, reads=[("PSM", id(ps))], writes=[(outkey, i)])
    a.op("vector", lambda e: e.nop(), reads=[(outkey, i) for i in range(len(nspans(T)))],
         writes=[outkey])


def q_layernorm(a, u, T, x_sb, xkey, xn, xnkey, eps=1e-5, keep32=False):
    NT = nspans(T)
    xbf, sqbf, rowbuf = u["xbf"], u["sqbf"], u["rowbuf"]
    mu_b, rs_b = u["mu_b"], u["rs_b"]
    for k in range(KC):
        sl = (k % 2) * T
        a.op("vector", (lambda e, k=k, sl=sl: e.tensor_copy(
            xbf[:, sl:sl + T], x_sb[:, k * T:(k + 1) * T])),
            reads=[(xkey, k)], writes=[("lnxbf", k % 2)])
        a.op("scalar", (lambda e, k=k, sl=sl: e.activation(
            sqbf[:, sl:sl + T], xbf[:, sl:sl + T], AF.Square)),
            reads=[("lnxbf", k % 2)], writes=[("lnsq", k % 2)])
        for i, (n0, nn) in enumerate(NT):
            a.op("tensor", (lambda e, k=k, sl=sl, n0=n0, nn=nn, i=i: e.matmul(
                u["pstat"][i][:1, :nn], u["ones_col"][:, :],
                xbf[:, sl + n0:sl + n0 + nn],
                start=(k == 0), stop=(k == KC - 1))),
                reads=[("lnxbf", k % 2), "c"], writes=[("PSM", id(u["pstat"][i]))], tag="ln")
            a.op("tensor", (lambda e, k=k, sl=sl, n0=n0, nn=nn, i=i: e.matmul(
                u["pstat"][len(NT) + i][:1, :nn], u["ones_col"][:, :],
                sqbf[:, sl + n0:sl + n0 + nn],
                start=(k == 0), stop=(k == KC - 1))),
                reads=[("lnsq", k % 2), "c"],
                writes=[("PSM", id(u["pstat"][len(NT) + i]))], tag="ln")
    for i, (n0, nn) in enumerate(NT):
        a.op("vector", (lambda e, i=i, n0=n0, nn=nn: e.tensor_scalar(
            rowbuf[:1, n0:n0 + nn], u["pstat"][i][:1, :nn], 1.0 / D, None, OP.mult)),
            reads=[("PSM", id(u["pstat"][i]))], writes=[("lnrow", "mu", i)])
        a.op("vector", (lambda e, i=i, n0=n0, nn=nn: e.tensor_scalar(
            rowbuf[:1, T + n0:T + n0 + nn], u["pstat"][len(NT) + i][:1, :nn],
            1.0 / D, None, OP.mult)),
            reads=[("PSM", id(u["pstat"][len(NT) + i]))], writes=[("lnrow", "va", i)])
    rmu = [("lnrow", "mu", i) for i in range(len(NT))]
    rva = [("lnrow", "va", i) for i in range(len(NT))]
    a.op("vector", lambda e: e.tensor_tensor(rowbuf[:1, 2 * T:3 * T],
                                             rowbuf[:1, 0:T], rowbuf[:1, 0:T], OP.mult),
         reads=rmu, writes=[("lnrow", "mu2")])
    a.op("vector", lambda e: e.tensor_tensor(rowbuf[:1, T:2 * T],
                                             rowbuf[:1, T:2 * T], rowbuf[:1, 2 * T:3 * T],
                                             OP.subtract),
         reads=rva + [("lnrow", "mu2")], writes=[("lnrow", "var")])
    a.op("vector", lambda e: e.tensor_scalar(rowbuf[:1, T:2 * T], rowbuf[:1, T:2 * T],
                                             eps, None, OP.add),
         reads=[("lnrow", "var")], writes=[("lnrow", "var")])
    a.op("scalar", lambda e: e.activation(rowbuf[:1, T:2 * T], rowbuf[:1, T:2 * T],
                                          AF.Sqrt),
         reads=[("lnrow", "var")], writes=[("lnrow", "var")])
    a.op("vector", lambda e: e.reciprocal(rowbuf[:1, T:2 * T], rowbuf[:1, T:2 * T]),
         reads=[("lnrow", "var")], writes=[("lnrow", "rstd")])
    q_bcast_row(a, u, rowbuf[:1, 0:T], rmu[-1], mu_b, "mu_b", T, "bmu")
    q_bcast_row(a, u, rowbuf[:1, T:2 * T], ("lnrow", "rstd"), rs_b, "rs_b", T, "brs")
    for k in range(KC):
        a.op("vector", (lambda e, k=k: e.tensor_tensor(
            x_sb[:, k * T:(k + 1) * T], x_sb[:, k * T:(k + 1) * T], mu_b[:, :T],
            OP.subtract)), reads=[(xkey, k), "mu_b"], writes=[(xkey, k)])
        a.op("vector", (lambda e, k=k: e.tensor_tensor(
            xn[:, k * T:(k + 1) * T], x_sb[:, k * T:(k + 1) * T], rs_b[:, :T],
            OP.mult)), reads=[(xkey, k), "rs_b"], writes=[(xnkey, k)])
        if keep32:
            a.op("vector", (lambda e, k=k: e.tensor_tensor(
                x_sb[:, k * T:(k + 1) * T], x_sb[:, k * T:(k + 1) * T], rs_b[:, :T],
                OP.mult)), reads=[(xkey, k), "rs_b", (xnkey, k)], writes=[(xkey, k), (xnkey, "32", k)])


def mk_utils(a, nc, es, T):
    sb = lambda n, shp, dt: es.enter_context(nc.sbuf_tensor(n, shp, dt))
    NT = nspans(T)
    u = {
        "ones_col": sb("u_ones_col", [128, 1], BF16),
        "ones_row": sb("u_ones_row", [1, 128], BF16),
        "id32": sb("u_id32", [128, 128], F32),
        "ones_row32": sb("u_ones_row32", [1, 128], F32),
        "idbf": sb("u_idbf", [128, 128], BF16),
        "rowbuf": sb("u_rowbuf", [1, max(3 * T, 2 * T + D)], F32),
        "rowbf": sb("u_rowbf", [1, max(T, D)], BF16),
        "mu_b": sb("u_mu_b", [128, T], F32),
        "rs_b": sb("u_rs_b", [128, T], F32),
    }
    return u


def q_load_consts(a, u, cdram):
    for name in ("ones_col", "ones_row", "id32", "idbf", "ones_row32"):
        a.dma("sync", u[name][:, :], cdram[name], writes=[("c", name)])
    a.op("sync", lambda e: e.nop(),
         reads=[("c", n) for n in ("ones_col", "ones_row", "id32", "idbf", "ones_row32")],
         writes=["c"])


def _drain_dmas(a):
    def fn(e):
        snap = {s2: c for s2, c in a.counts.items()
                if c > 0 and (s2.startswith("D_") or s2.startswith("CC"))}
        for s2, c in sorted(snap.items()):
            e.wait_ge(a.sems[s2], c)
        return e.nop()
    for eng in ENGINES:
        a.op(eng, fn)


class _NCPfx:

    def __init__(self, nc, pfx):
        self._nc = nc
        self._pfx = pfx

    def sbuf_tensor(self, name, *a, **k):
        return self._nc.sbuf_tensor(self._pfx + name, *a, **k)

    def dram_tensor(self, name, *a, **k):
        return self._nc.dram_tensor(self._pfx + name, *a, **k)

    def psum_tensor(self, name, *a, **k):
        return self._nc.psum_tensor(self._pfx + name, *a, **k)

    def semaphore(self, name, *a, **k):
        return self._nc.semaphore(self._pfx + name, *a, **k)

    def __getattr__(self, x):
        return getattr(self._nc, x)


def _mk_cdram(nc):
    return {
        "ones_col": nc.dram_tensor("c_ones_col", [128, 1], BF16, kind="ExternalInput").ap(),
        "ones_row": nc.dram_tensor("c_ones_row", [1, 128], BF16, kind="ExternalInput").ap(),
        "id32": nc.dram_tensor("c_id32", [128, 128], F32, kind="ExternalInput").ap(),
        "idbf": nc.dram_tensor("c_idbf", [128, 128], BF16, kind="ExternalInput").ap(),
        "ones_row32": nc.dram_tensor("c_ones_row32", [1, 128], F32, kind="ExternalInput").ap(),
    }


def build_fused(S):
    T = S // NCORES
    nc = bass.Bass("TRN2", target_bir_lowering=False)
    es0 = contextlib.ExitStack()
    with es0:
        a = Auto(nc, es0)
        cdram = _mk_cdram(nc)
        xin = nc.dram_tensor("b0_xT", [D, T], F32, kind="ExternalInput").ap()
        h1 = nc.dram_tensor("h1", [D, T], F32).ap()
        h2 = nc.dram_tensor("h2", [D, T], F32).ap()
        h3 = nc.dram_tensor("h3", [D, T], F32).ap()
        x4T = nc.dram_tensor("x4T", [D, T], F32, kind="ExternalOutput").ap()
        build_l0(S, dict(nc=nc, pfx="b0_", a=a, cdram=cdram, x_in=xin, x_out=h1))
        build_l1(S, dict(nc=nc, pfx="b1_", a=a, cdram=cdram, x_in=h1, x_out=h2))
        build_l2(S, dict(nc=nc, pfx="b2_", a=a, cdram=cdram, x_in=h2, x_out=h3))
        build_l3(S, dict(nc=nc, pfx="b3_", a=a, cdram=cdram, x_in=h3, x_out=x4T))
    return nc


def build_l0(S, ctx=None):
    T = S // NCORES
    TC = T // 128
    NT = nspans(T)
    ND = nspans(D)
    nc = _NCPfx(ctx["nc"], ctx["pfx"]) if ctx else bass.Bass("TRN2", target_bir_lowering=False)
    xT = ctx["x_in"] if ctx else nc.dram_tensor("xT", [D, T], F32, kind="ExternalInput").ap()
    wq_d = nc.dram_tensor("wq", [D, D], F32, kind="ExternalInput").ap()
    wk_d = nc.dram_tensor("wk", [D, D], F32, kind="ExternalInput").ap()
    wv_d = nc.dram_tensor("wv", [D, D], F32, kind="ExternalInput").ap()
    wo_d = nc.dram_tensor("wo", [D, D], F32, kind="ExternalInput").ap()
    wf_d = nc.dram_tensor("wf", [D, D], F32, kind="ExternalInput").ap()
    gamma_d = nc.dram_tensor("gamma_pk", [128, KC], F32, kind="ExternalInput").ap()
    cdram = ctx["cdram"] if ctx else _mk_cdram(nc)
    x1T = ctx["x_out"] if ctx else nc.dram_tensor("x1T", [D, T], F32, kind="ExternalOutput").ap()
    v_dram = nc.dram_tensor("v_d", [D, T], BF16)
    k_dram = nc.dram_tensor("k_d", [T, D], BF16)
    scp_dram = nc.dram_tensor("sc_part", [D, D], F32)
    scf_dram = nc.dram_tensor("sc_full", [D, D], F32)

    es = contextlib.ExitStack()
    with es:
        a = ctx["a"] if ctx else Auto(nc, es)
        sb = lambda n, shp, dt, st=es, side=None: st.enter_context(nc.sbuf_tensor(n, shp, dt, side=side))
        u = mk_utils(a, nc, es, T)
        PS = [es.enter_context(nc.psum_tensor(f"PS{i}", [128, 512], F32)) for i in range(4)]
        PR = [es.enter_context(nc.psum_tensor(f"PR{i}", [1, 512], F32)) for i in range(4)]
        u["pstat"] = PR
        u["psb"] = PS[2:4]
        xn = sb("xn", [128, KC * T], BF16)
        stage = sb("stage", [128, 2 * D], F32)
        gam = sb("gam", [128, KC], F32)

        r0 = contextlib.ExitStack()
        x_sb = sb("x_sb", [128, KC * T], F32, r0)
        u["xbf"] = sb("xbf", [128, 2 * T], BF16, r0)
        u["sqbf"] = sb("sqbf", [128, 2 * T], BF16, r0)
        q_load_consts(a, u, cdram)
        a.dma("sync", gam[:, :], gamma_d, writes=["gam"])
        q_load_act(a, xT, x_sb, T, "x")
        q_layernorm(a, u, T, x_sb, "x", xn, "xn")
        _drain_dmas(a)
        a.phase()
        r0.close()
        XN = [("xn", k) for k in range(KC)]

        r1 = contextlib.ExitStack()
        q_sb = sb("q_sb", [128, TC * D], BF16, r1)
        kstg = sb("kstg", [128, 2 * D], BF16, r1)
        vstg = sb("vstg", [128, 2 * T], BF16, r1)
        scstg = sb("scstg", [128, 2 * D], F32, r1)
        kstrip = sb("kstrip", [128, 2 * TC * 128], BF16, r1)
        r1b = contextlib.ExitStack()
        w_sb = sb("w_sb", [128, KC * D], BF16, r1b)

        def tok_proj(wdram, dst_sb, dstkey, tag):
            q_stream_weight(a, wdram, w_sb, KC, D, stage, tag, tag + "w")
            wk_keys = [(tag + "w", k) for k in range(KC)]
            groups = [(t, n0, nn) for t in range(TC) for (n0, nn) in ND]
            q_mm(a, tag + "mm", groups,
                 lambda g, k: xn[:, k * T + g[0] * 128:k * T + (g[0] + 1) * 128],
                 lambda g, k: w_sb[:, k * D + g[1]:k * D + g[1] + g[2]],
                 KC, PS[0:2],
                 lambda a_, g, ps, pk: a_.op(
                     "vector",
                     (lambda e, g=g, ps=ps: e.tensor_copy(
                         dst_sb[:, g[0] * D + g[1]:g[0] * D + g[1] + g[2]],
                         ps[:, :g[2]])),
                     reads=[pk], writes=[(dstkey, g[0], g[1])]),
                 lambda g: XN + wk_keys,
                 lambda g, ps: ps[:, :g[2]])

        tok_proj(wq_d, q_sb, "q", "pq")
        a.phase()

        q_stream_weight(a, wk_d, w_sb, KC, D, stage, "pk", "pkw")
        pkw_keys = [("pkw", k) for k in range(KC)]
        kgroups = [(t, n0, nn) for t in range(TC) for (n0, nn) in ND]

        def k_evac(a_, g, ps, pk):
            t, n0, nn = g
            slot = t % 2
            a_.op("vector", (lambda e: e.tensor_copy(
                kstg[:, slot * D + n0:slot * D + n0 + nn], ps[:, :nn])),
                reads=[pk], writes=[("kstg", slot, n0)])
            if n0 + nn >= D:
                a_.dma("scalar", k_dram[t * 128:(t + 1) * 128, :],
                       kstg[:, slot * D:slot * D + D],
                       reads=[("kstg", slot, x0) for (x0, _) in ND],
                       writes=[("k_d", t)])
        q_mm(a, "pkmm", kgroups,
             lambda g, k: xn[:, k * T + g[0] * 128:k * T + (g[0] + 1) * 128],
             lambda g, k: w_sb[:, k * D + g[1]:k * D + g[1] + g[2]],
             KC, PS[0:2], k_evac, lambda g: XN + pkw_keys,
             lambda g, ps: ps[:, :g[2]])
        a.phase()

        q_stream_weight(a, wv_d, w_sb, KC, D, stage, "pv", "pvw")
        pvw_keys = [("pvw", k) for k in range(KC)]
        vgroups = [(m, n0, nn) for m in range(KC) for (n0, nn) in NT]

        def v_evac(a_, g, ps, pk):
            m, n0, nn = g
            slot = m % 2
            a_.op("vector", (lambda e: e.tensor_copy(
                vstg[:, slot * T + n0:slot * T + n0 + nn], ps[:, :nn])),
                reads=[pk], writes=[("vstg", slot, n0)])
            if n0 + nn >= T:
                a_.dma("scalar", v_dram[128 * m:128 * (m + 1), :],
                       vstg[:, slot * T:slot * T + T],
                       reads=[("vstg", slot, x0) for (x0, _) in NT],
                       writes=[("v_d", m)])
        q_mm(a, "pvmm", vgroups,
             lambda g, k: w_sb[:, k * D + g[0] * 128:k * D + (g[0] + 1) * 128],
             lambda g, k: xn[:, k * T + g[1]:k * T + g[1] + g[2]],
             KC, PS[0:2], v_evac, lambda g: XN + pvw_keys,
             lambda g, ps: ps[:, :g[2]])
        _drain_dmas(a)
        a.phase()

        r1b.close()
        q_keys = [("q", t, e) for t in range(TC) for (e, _) in ND]
        groups = [(f, e0, nn) for f in range(KC) for (e0, nn) in ND]

        def sc_evac(a_, g, ps, pk):
            f, e0, nn = g
            slot = f % 2
            a_.op("vector", (lambda e: e.tensor_scalar(
                scstg[:, slot * D + e0:slot * D + e0 + nn], ps[:, :nn],
                0.125, None, OP.mult)),
                reads=[pk], writes=[("scstg", slot, e0)])
            if e0 + nn >= D:
                a_.dma("scalar", scp_dram[128 * f:128 * (f + 1), :],
                       scstg[:, slot * D:slot * D + D],
                       reads=[("scstg", slot, ee0) for (ee0, _) in ND],
                       writes=[("scp", f)])
        for f in range(KC):
            slot = f % 2
            for t in range(TC):
                a.dma("sync",
                      kstrip[:, slot * TC * 128 + t * 128:slot * TC * 128 + (t + 1) * 128],
                      k_dram[t * 128:(t + 1) * 128, f * 128:(f + 1) * 128],
                      reads=[("k_d", t)] + ([("scmm_used", f - 2)] if f >= 2 else []),
                      writes=[("kstrip", slot, t)])
            for ie, (e0, nn) in enumerate(ND):
                ps = PS[(f * len(ND) + ie) % 2]
                pskey = ("PSM", id(ps))
                for t in range(TC):
                    def fn(e, f=f, slot=slot, t=t, e0=e0, nn=nn, ps=ps):
                        return e.matmul(ps[:, :nn],
                                        kstrip[:, slot * TC * 128 + t * 128:slot * TC * 128 + (t + 1) * 128],
                                        q_sb[:, t * D + e0:t * D + e0 + nn],
                                        start=(t == 0), stop=(t == TC - 1))
                    a.op("tensor", fn,
                         reads=[("kstrip", slot, t), ("q", t, e0)],
                         writes=[pskey] + ([("scmm_used", f)] if ie == len(ND) - 1 and t == TC - 1 else []))
                sc_evac(a, (f, e0, nn), ps, pskey)
        a.cc("AllReduce", OP.add, scp_dram, scf_dram,
             [("scp", f) for f in range(KC)], "scf")
        _drain_dmas(a)
        a.phase()
        r1.close()

        r2 = contextlib.ExitStack()
        p_sbT = sb("p_sbT", [128, KC * D], BF16, r2)
        maxcol = sb("maxcol", [128, 2], F32, r2)
        negm = sb("negm", [128, 1], F32, r2)
        linv_col = sb("linv_col", [128, KC], F32, r2)
        r2s = contextlib.ExitStack()
        scstg2 = sb("scstg2", [128, 2 * D], F32, r2s)
        for f in range(KC):
            rd = ["scf"] + ([("mxacc",)] if f >= 2 else [])
            a.dma("sync", scstg2[:, (f % 2) * D:(f % 2 + 1) * D],
                  scf_dram[128 * f:128 * (f + 1), :], reads=rd, writes=[("scld", f)])
            a.op("vector", (lambda e, f=f: e.reduce_max(
                maxcol[:, 1:2], scstg2[:, (f % 2) * D:(f % 2 + 1) * D], axis=AX.X)),
                reads=[("scld", f)], writes=[("mx1",)])
            if f == 0:
                a.op("vector", lambda e: e.tensor_copy(maxcol[:, 0:1], maxcol[:, 1:2]),
                     reads=[("mx1",)], writes=[("mxacc",)])
            else:
                a.op("vector", (lambda e: e.tensor_tensor(
                    maxcol[:, 0:1], maxcol[:, 0:1], maxcol[:, 1:2], OP.max)),
                    reads=[("mx1",), ("mxacc",)], writes=[("mxacc",)])
        pt = PS[3]
        a.op("tensor", lambda e: e.transpose(pt[:1, :128], maxcol[:, 0:1], u["id32"][:, :]),
             reads=[("mxacc",), "c"], writes=[("PSM", id(PS[3]))])
        a.op("vector", lambda e: e.reduce_max(u["rowbuf"][:1, 0:1], pt[:1, :128], axis=AX.X),
             reads=[("PSM", id(PS[3]))], writes=[("mrow",)])
        a.op("vector", lambda e: e.tensor_copy(u["rowbf"][:1, 0:1], u["rowbuf"][:1, 0:1]),
             reads=[("mrow",)], writes=[("mbf",)])
        a.op("tensor", lambda e: e.matmul(pt[:, 0:1], u["ones_row"][:, :],
                                          u["rowbf"][:1, 0:1], start=True, stop=True),
             reads=[("mbf",), "c", ("mrow",)], writes=[("PSM", id(PS[3]))])
        a.op("vector", lambda e: e.tensor_scalar(negm[:, :], pt[:, 0:1], -1.0, None, OP.mult),
             reads=[("PSM", id(PS[3]))], writes=[("negm",)])
        _drain_dmas(a)
        a.phase()

        for f in range(KC):
            rd = [("pexp", f - 2)] if f >= 2 else ["scf"]
            a.dma("sync", scstg2[:, (f % 2) * D:(f % 2 + 1) * D],
                  scf_dram[128 * f:128 * (f + 1), :], reads=rd, writes=[("scl2", f)])
            a.op("scalar", (lambda e, f=f: e.activation(
                p_sbT[:, f * D:(f + 1) * D], scstg2[:, (f % 2) * D:(f % 2 + 1) * D],
                AF.Exp, bias=negm[:, :])),
                reads=[("scl2", f), ("negm",)], writes=[("pexp", f)])
            for i, (e0, nn) in enumerate(ND):
                a.op("tensor", (lambda e, f=f, e0=e0, nn=nn, i=i: e.matmul(
                    PR[i][:, :nn], u["ones_col"][:, :],
                    p_sbT[:, f * D + e0:f * D + e0 + nn],
                    start=(f == 0), stop=(f == KC - 1))),
                    reads=[("pexp", f), "c"], writes=[("PSM", id(PR[i]))])
        LP = 2 * T
        for i, (e0, nn) in enumerate(ND):
            a.op("vector", (lambda e, i=i, e0=e0, nn=nn: e.tensor_copy(
                u["rowbuf"][:1, LP + e0:LP + e0 + nn], PR[i][:, :nn])),
                reads=[("PSM", id(PR[i]))], writes=[("lrow", i)])
        a.op("vector", lambda e: e.reciprocal(u["rowbuf"][:1, LP:LP + D],
                                              u["rowbuf"][:1, LP:LP + D]),
             reads=[("lrow", i) for i in range(len(ND))], writes=[("linvrow",)])
        for j in range(KC):
            a.op("tensor", (lambda e, j=j: e.transpose(
                pt[:, j:j + 1],
                u["rowbuf"][:1, LP + 128 * j:LP + 128 * (j + 1)], u["id32"][:1, :1])),
                reads=[("linvrow",), "c"], writes=[("PSM", id(PS[3])), ("ltr", j)], tag="ltr")
        a.op("vector", lambda e: e.tensor_copy(linv_col[:, :], pt[:, :KC]),
             reads=[("PSM", id(PS[3]))] + [("ltr", j) for j in range(KC)], writes=[("linv",)])
        a.phase()

        r2s.close()
        r2b = contextlib.ExitStack()
        v_sb = sb("v_sb", [128, KC * T], BF16, r2)
        attn_sb = sb("attn_sb", [128, KC * T], BF16, r2b, side="right")
        for k in range(KC):
            a.dma("sync", v_sb[:, k * T:(k + 1) * T], v_dram[128 * k:128 * (k + 1), :],
                  reads=[("v_d", k)], writes=[("vre", k)])
        pexp_keys = [("pexp", k) for k in range(KC)]
        vre_keys = [("vre", k) for k in range(KC)]
        groups = [(m, n0, nn) for m in range(KC) for (n0, nn) in NT]
        q_mm(a, "avmm", groups,
             lambda g, k: p_sbT[:, k * D + g[0] * 128:k * D + (g[0] + 1) * 128],
             lambda g, k: v_sb[:, k * T + g[1]:k * T + g[1] + g[2]],
             KC, PS[0:2],
             lambda a_, g, ps, pk: a_.op(
                 "vector",
                 (lambda e, g=g, ps=ps: e.tensor_scalar(
                     attn_sb[:, g[0] * T + g[1]:g[0] * T + g[1] + g[2]],
                     ps[:, :g[2]], linv_col[:, g[0]:g[0] + 1], None, OP.mult)),
                 reads=[pk, ("linv",)], writes=[("attn", g[0], g[1])]),
             lambda g: pexp_keys + vre_keys,
             lambda g, ps: ps[:, :g[2]])
        _drain_dmas(a)
        a.phase()
        r2.close()

        x1p_dram = nc.dram_tensor("x1p", [D, T], F32)
        r3 = contextlib.ExitStack()
        w2_sb = sb("w2_sb", [128, KC * D], BF16, r3)
        xstg = sb("xstg", [128, 4 * 512], F32, r3)
        q_stream_weight(a, wo_d, w2_sb, KC, D, stage, "po", "pow")
        pow_keys = [("pow", k) for k in range(KC)]
        attn_keys = [("attn", m, n0) for m in range(KC) for (n0, _) in NT]

        def resid_evac(tag, srcdram, dstdram):
            def cb(a_, g, ps, pk):
                m, n0, nn = g
                gi = (m * len(NT)) + [x[0] for x in NT].index(n0)
                sl_in = xstg[:, (gi % 2) * 512:(gi % 2) * 512 + nn]
                sl_out = xstg[:, (2 + gi % 2) * 512:(2 + gi % 2) * 512 + nn]
                a_.dma("scalar", sl_in, srcdram[128 * m:128 * (m + 1), n0:n0 + nn],
                       reads=[(tag, "xin", gi % 2)], writes=[(tag, "xld", gi % 2)])
                a_.op("vector", (lambda e: e.tensor_tensor(sl_out, sl_in, ps[:, :nn],
                                                           OP.add)),
                      reads=[pk, (tag, "xld", gi % 2), (tag, "xst", gi % 2)],
                      writes=[(tag, "xin", gi % 2), (tag, "xout", gi % 2)])
                a_.dma("scalar", dstdram[128 * m:128 * (m + 1), n0:n0 + nn], sl_out,
                       reads=[(tag, "xout", gi % 2)],
                       writes=[(tag, "xst", gi % 2), (tag, "done", m, n0)])
            return cb
        q_mm(a, "pomm", groups,
             lambda g, k: w2_sb[:, k * D + g[0] * 128:k * D + (g[0] + 1) * 128],
             lambda g, k: attn_sb[:, k * T + g[1]:k * T + g[1] + g[2]],
             KC, PS[0:2], resid_evac("rp", xT, x1p_dram),
             lambda g: attn_keys + pow_keys,
             lambda g, ps: ps[:, :g[2]])
        _drain_dmas(a)
        a.phase()
        r2b.close()

        r4 = contextlib.ExitStack()
        xq = sb("xq", [128, KC * T], BF16, r4)
        sqs = sb("sqs", [128, 2 * T], BF16, r4)
        WT = max(T, D // 2)
        macc = sb("macc", [128, WT], F32, r4)
        zstg = sb("zstg", [128, WT], F32, r4)
        amax_col = sb("amax_col", [128, TC], F32, r4)
        wsc_col = sb("wsc_col", [128, 1], F32, r4)
        rnd = Rounder(a, lambda n, shp, dt: sb(n, shp, dt, r4), "rd", T)
        minc = rnd.f1

        for k in range(KC):
            a.op("scalar", (lambda e, k=k: e.activation(
                sqs[:, (k % 2) * T:(k % 2 + 1) * T], xn[:, k * T:(k + 1) * T],
                AF.Square)),
                reads=[("xn", k)],
                writes=[("rsqslot", k % 2)])
            for i, (n0, nn) in enumerate(NT):
                a.op("tensor", (lambda e, k=k, i=i, n0=n0, nn=nn: e.matmul(
                    PR[i][:, :nn], u["ones_col"][:, :],
                    sqs[:, (k % 2) * T + n0:(k % 2) * T + n0 + nn],
                    start=(k == 0), stop=(k == KC - 1))),
                    reads=[("rsqslot", k % 2), "c"], writes=[("PSM", id(PR[i])), ("rmsps", i)])
        for i, (n0, nn) in enumerate(NT):
            a.op("vector", (lambda e, i=i, n0=n0, nn=nn: e.tensor_scalar(
                u["rowbuf"][:1, n0:n0 + nn], PR[i][:, :nn], 1.0 / D, None, OP.mult)),
                reads=[("PSM", id(PR[i])), ("rmsps", i)], writes=[("rmsrow", i)])
        a.op("vector", lambda e: e.tensor_scalar(u["rowbuf"][:1, 0:T], u["rowbuf"][:1, 0:T],
                                                 1e-6, None, OP.add),
             reads=[("rmsrow", i) for i in range(len(NT))], writes=[("rmsrow",)])
        a.op("scalar", lambda e: e.activation(u["rowbuf"][:1, 0:T], u["rowbuf"][:1, 0:T],
                                              AF.Sqrt),
             reads=[("rmsrow",)], writes=[("rmsrow",)])
        a.op("vector", lambda e: e.reciprocal(u["rowbuf"][:1, 0:T], u["rowbuf"][:1, 0:T]),
             reads=[("rmsrow",)], writes=[("rmsrow",)])
        q_bcast_row(a, u, u["rowbuf"][:1, 0:T], ("rmsrow",), u["rs_b"], "rmsb", T, "brm")
        for k in range(KC):
            a.op("vector", (lambda e, k=k: e.tensor_scalar(
                zstg[:, :T], xn[:, k * T:(k + 1) * T], gam[:, k:k + 1], None, OP.mult)),
                reads=[("xn", k), "gam", ("ZM",)], writes=[("ZM",)])
            a.op("vector", (lambda e, k=k: e.tensor_tensor(
                zstg[:, :T], zstg[:, :T], u["rs_b"][:, :T], OP.mult)),
                reads=[("ZM",), "rmsb"], writes=[("ZM",)])
            if k == 0:
                a.op("vector", lambda e: e.tensor_copy(macc[:, :T], zstg[:, :T]),
                     reads=[("ZM",)], writes=[("macc",)])
                a.op("vector", lambda e: e.tensor_scalar(minc[:, :T], zstg[:, :T],
                                                         -1.0, None, OP.mult),
                     reads=[("ZM",)], writes=[("mincc",), ("rd", "f1")])
            else:
                a.op("vector", (lambda e: e.tensor_tensor(
                    macc[:, :T], macc[:, :T], zstg[:, :T], OP.max)),
                    reads=[("macc",), ("ZM",)], writes=[("macc",)])
                a.op("vector", (lambda e: e.scalar_tensor_tensor(
                    minc[:, :T], zstg[:, :T], -1.0, minc[:, :T], OP.mult, OP.max)),
                    reads=[("mincc",), ("ZM",)], writes=[("mincc",), ("rd", "f1")])
        a.op("vector", lambda e: e.tensor_tensor(macc[:, :T], macc[:, :T], minc[:, :T],
                                                 OP.max),
             reads=[("macc",), ("mincc",)], writes=[("macc",)])
        for t in range(TC):
            a.op("tensor", (lambda e, t=t: e.transpose(
                PS[3][:, :128], macc[:, t * 128:(t + 1) * 128], u["id32"][:, :])),
                reads=[("macc",), "c"],
                writes=[("PSM", id(PS[3]))])
            a.op("vector", (lambda e, t=t: e.reduce_max(
                amax_col[:, t:t + 1], PS[3][:, :128], axis=AX.X)),
                reads=[("PSM", id(PS[3]))], writes=[("amx", t)])
        for t in range(TC):
            a.op("tensor", (lambda e, t=t: e.transpose(
                PS[3][:1, :128], amax_col[:, t:t + 1], u["id32"][:, :])),
                reads=[("amx", t)],
                writes=[("PSM", id(PS[3]))])
            a.op("vector", (lambda e, t=t: e.tensor_copy(
                u["rowbuf"][:1, T + t * 128:T + (t + 1) * 128], PS[3][:1, :128])),
                reads=[("PSM", id(PS[3]))], writes=[("arow", t)])
        a.op("vector", lambda e: e.tensor_scalar(u["rowbuf"][:1, T:2 * T],
                                                 u["rowbuf"][:1, T:2 * T], 1e-5, 1.0 / 127.0,
                                                 OP.add, OP.mult),
             reads=[("arow", t) for t in range(TC)], writes=[("aprow",)])
        a.op("vector", lambda e: e.reciprocal(u["rowbuf"][:1, 0:T], u["rowbuf"][:1, T:2 * T]),
             reads=[("aprow",)], writes=[("ascrow",)])
        q_bcast_row(a, u, u["rowbuf"][:1, 0:T], ("ascrow",), u["mu_b"], "ascb", T, "bas")
        for k in range(KC):
            a.op("vector", (lambda e, k=k: e.tensor_scalar(
                zstg[:, :T], xn[:, k * T:(k + 1) * T], gam[:, k:k + 1], None, OP.mult)),
                reads=[("xn", k), "gam", ("ZM",)], writes=[("ZM",)])
            a.op("vector", (lambda e: e.tensor_tensor(
                zstg[:, :T], zstg[:, :T], u["rs_b"][:, :T], OP.mult)),
                reads=[("ZM",), "rmsb"], writes=[("ZM",)])
            a.op("vector", (lambda e: e.tensor_tensor(
                zstg[:, :T], zstg[:, :T], u["mu_b"][:, :T], OP.mult)),
                reads=[("ZM",), "ascb"], writes=[("ZM",)])
            rnd.round(a, zstg[:, :T], xq[:, k * T:(k + 1) * T], T, ("ZM",), ("xq", k))
        _drain_dmas(a)
        a.phase()

        wacc = sb("wacc", [128, 2], F32, r4)
        for k in range(KC):
            a.dma("sync", stage[:, (k % 2) * D:(k % 2 + 1) * D],
                  wf_d[128 * k:128 * (k + 1), :],
                  reads=[("wabs", k - 2)] if k >= 2 else (), writes=[("wfld", k)])
            a.op("vector", (lambda e, k=k: e.reduce_sum(
                wacc[:, 1:2], stage[:, (k % 2) * D:(k % 2 + 1) * D], axis=AX.X,
                apply_absolute_value=True)),
                reads=[("wfld", k)], writes=[("wabs1",), ("wabs", k)])
            if k == 0:
                a.op("vector", lambda e: e.tensor_copy(wacc[:, 0:1], wacc[:, 1:2]),
                     reads=[("wabs1",)], writes=[("wacc",)])
            else:
                a.op("vector", lambda e: e.tensor_tensor(wacc[:, 0:1], wacc[:, 0:1],
                                                         wacc[:, 1:2], OP.add),
                     reads=[("wabs1",), ("wacc",)], writes=[("wacc",)])
        a.op("tensor", lambda e: e.transpose(PS[3][:1, :128], wacc[:, 0:1], u["id32"][:, :]),
             reads=[("wacc",), "c"], writes=[("PSM", id(PS[3]))])
        a.op("vector", lambda e: e.reduce_sum(u["rowbuf"][:1, 0:1], PS[3][:1, :128],
                                              axis=AX.X),
             reads=[("PSM", id(PS[3]))], writes=[("wsum",)])
        a.op("vector", lambda e: e.tensor_scalar(u["rowbuf"][:1, 0:1], u["rowbuf"][:1, 0:1],
                                                 1.0 / (D * D), 1e-5, OP.mult, OP.add),
             reads=[("wsum",)], writes=[("wsum",)])
        a.op("vector", lambda e: e.reciprocal(u["rowbuf"][:1, 0:1], u["rowbuf"][:1, 0:1]),
             reads=[("wsum",)], writes=[("wsum",)])
        a.op("vector", lambda e: e.tensor_copy(u["rowbf"][:1, 0:1], u["rowbuf"][:1, 0:1]),
             reads=[("wsum",)], writes=[("wsumbf",)])
        a.op("tensor", lambda e: e.matmul(PS[3][:, 0:1], u["ones_row"][:, :],
                                          u["rowbf"][:1, 0:1], start=True, stop=True),
             reads=[("wsumbf",), "c"], writes=[("PSM", id(PS[3]))])
        a.op("vector", lambda e: e.tensor_copy(wsc_col[:, :], PS[3][:, 0:1]),
             reads=[("PSM", id(PS[3]))], writes=[("wsc",)])
        _drain_dmas(a)
        a.phase()

        def tern_quant(a_, src, dst, k, rkey, wkey):
            for h in range(2):
                hs = slice(h * (D // 2), (h + 1) * (D // 2))
                W = D // 2
                a_.op("vector", (lambda e, src=src, hs=hs: e.tensor_scalar(
                    zstg[:, :W], src[:, hs], wsc_col[:, 0:1], None, OP.mult)),
                    reads=[rkey, ("wsc",), ("ZM",), ("MC",)], writes=[("ZM",)])
                a_.op("vector", (lambda e, hs=hs: e.tensor_scalar(
                    macc[:, :W], zstg[:, :W], 0.5, None, OP.is_ge)),
                    reads=[("ZM",)], writes=[("MC",)])
                a_.op("vector", (lambda e, hs=hs: e.tensor_scalar(
                    zstg[:, :W], zstg[:, :W], -0.5, None, OP.is_le)),
                    reads=[("ZM",)], writes=[("ZM",)])
                a_.op("vector", (lambda e, dst=dst, hs=hs: e.tensor_tensor(
                    dst[:, hs], macc[:, :W], zstg[:, :W], OP.subtract)),
                    reads=[("ZM",), ("MC",)], writes=[wkey if h else (wkey, "h0")])
        q_stream_weight(a, wf_d, w2_sb, KC, D, stage, "wf", "wfq", quant=tern_quant)
        wfq_keys = [("wfq", k) for k in range(KC)]
        xq_keys = [("xq", k) for k in range(KC)]
        groups = [(m, n0, nn) for m in range(KC) for (n0, nn) in NT]

        q_bcast_row(a, u, u["rowbuf"][:1, T:2 * T], ("aprow",), u["rs_b"], "sob", T, "bso")
        winv = sb("winv", [128, 1], F32, r4)
        a.op("vector", lambda e: e.reciprocal(winv[:, :], wsc_col[:, :]),
             reads=[("wsc",)], writes=[("winv",)])

        def ffn_evac(a_, g, ps, pk):
            m, n0, nn = g
            gi = (m * len(NT)) + [x[0] for x in NT].index(n0)
            sl_in = xstg[:, (gi % 2) * 512:(gi % 2) * 512 + nn]
            sl_out = xstg[:, (2 + gi % 2) * 512:(2 + gi % 2) * 512 + nn]
            a_.dma("scalar", sl_in, x1p_dram[128 * m:128 * (m + 1), n0:n0 + nn],
                   reads=[("rp", "done", m, n0), ("ff", "xin", gi % 2)],
                   writes=[("ff", "xld", gi % 2)])
            a_.op("vector", (lambda e: e.tensor_scalar(
                ps[:, :nn], ps[:, :nn], winv[:, 0:1], None, OP.mult)),
                reads=[pk, ("winv",)], writes=[pk])
            a_.op("vector", (lambda e: e.tensor_tensor(
                ps[:, :nn], ps[:, :nn], u["rs_b"][:, n0:n0 + nn], OP.mult)),
                reads=[pk, "sob"], writes=[pk])
            a_.op("vector", (lambda e: e.tensor_tensor(sl_out, sl_in, ps[:, :nn],
                                                       OP.add)),
                  reads=[pk, ("ff", "xld", gi % 2), ("ff", "xst", gi % 2)],
                  writes=[("ff", "xin", gi % 2), ("ff", "xout", gi % 2)])
            a_.dma("scalar", x1T[128 * m:128 * (m + 1), n0:n0 + nn], sl_out,
                   reads=[("ff", "xout", gi % 2)],
                   writes=[("ff", "xst", gi % 2), "OUT"])
        q_mm(a, "ffmm", groups,
             lambda g, k: w2_sb[:, k * D + g[0] * 128:k * D + (g[0] + 1) * 128],
             lambda g, k: xq[:, k * T + g[1]:k * T + g[1] + g[2]],
             KC, PS[0:2], ffn_evac,
             lambda g: xq_keys + wfq_keys,
             lambda g, ps: ps[:, :g[2]])
        a.finish()
        a.phase()
        r4.close()
        r3.close()
    return nc


LN2 = 0.6931471805599453


class Fp8Quant:
    NI = 2
    MAGIC = 12582912.0

    def __init__(self, a, sbf, name, W):
        self.nm = name
        n = self.NI
        self.t = sbf(name + "_t", [128, n * W], F32)
        self.eb = sbf(name + "_eb", [128, n * W], I32)
        self.ef = sbf(name + "_ef", [128, n * W], F32)
        self.e2 = sbf(name + "_e2", [128, n * W], F32)
        self.qi = sbf(name + "_qi", [128, n * W], F32)
        self.W = W

    def emit(self, a, src_ap, dst_ap, n, skey, dkey, sinv_col, sinv_key):
        self.emit_jobs(a, [(src_ap, dst_ap, n, skey, dkey, sinv_col, sinv_key)])

    def emit_jobs(self, a, jobs):
        nm = self.nm
        W = self.W
        assert len(jobs) <= self.NI

        def bufs(j, n):
            o = j * W
            return (self.t[:, o:o + n], self.eb[:, o:o + n], self.ef[:, o:o + n],
                    self.e2[:, o:o + n], self.qi[:, o:o + n])

        for j, (src_ap, dst_ap, n, skey, dkey, sinv_col, sinv_key) in enumerate(jobs):
            t, eb, ef, e2, qi = bufs(j, n)
            a.op("vector", (lambda e, t=t, src_ap=src_ap, sinv_col=sinv_col:
                            e.tensor_scalar(t, src_ap, sinv_col, None, OP.mult)),
                 reads=[skey, sinv_key, (nm, "t", j)], writes=[(nm, "t", j)])
        for j, (src_ap, dst_ap, n, skey, dkey, sinv_col, sinv_key) in enumerate(jobs):
            t, eb, ef, e2, qi = bufs(j, n)
            a.op("vector", (lambda e, t=t, eb=eb: e.tensor_scalar(
                eb, t.bitcast(I32), 23, 255,
                OP.logical_shift_right, OP.bitwise_and)),
                 reads=[(nm, "t", j)], writes=[(nm, "eb", j)])
        for j, (src_ap, dst_ap, n, skey, dkey, sinv_col, sinv_key) in enumerate(jobs):
            t, eb, ef, e2, qi = bufs(j, n)
            a.op("vector", (lambda e, eb=eb: e.tensor_scalar(
                eb, eb, 121, None, OP.max)),
                 reads=[(nm, "eb", j)], writes=[(nm, "efc", j)])
        for j, (src_ap, dst_ap, n, skey, dkey, sinv_col, sinv_key) in enumerate(jobs):
            t, eb, ef, e2, qi = bufs(j, n)
            a.op("vector", (lambda e, eb=eb, qi=qi: e.tensor_scalar(
                qi.bitcast(I32), eb, -1, 257, OP.mult, OP.add)),
                 reads=[(nm, "efc", j)], writes=[(nm, "qa", j)])
        for j, (src_ap, dst_ap, n, skey, dkey, sinv_col, sinv_key) in enumerate(jobs):
            t, eb, ef, e2, qi = bufs(j, n)
            a.op("vector", (lambda e, qi=qi: e.tensor_scalar(
                qi.bitcast(I32), qi.bitcast(I32), 23, None,
                OP.logical_shift_left)),
                 reads=[(nm, "qa", j)], writes=[(nm, "qi", j)])
        for j, (src_ap, dst_ap, n, skey, dkey, sinv_col, sinv_key) in enumerate(jobs):
            t, eb, ef, e2, qi = bufs(j, n)
            a.op("vector", (lambda e, t=t, qi=qi: e.tensor_tensor(t, t, qi, OP.mult)),
                 reads=[(nm, "t", j), (nm, "qi", j)], writes=[(nm, "t", j)])
        for j, (src_ap, dst_ap, n, skey, dkey, sinv_col, sinv_key) in enumerate(jobs):
            t, eb, ef, e2, qi = bufs(j, n)
            a.op("vector", (lambda e, t=t: e.tensor_scalar(
                t, t, self.MAGIC, None, OP.add)),
                 reads=[(nm, "t", j)], writes=[(nm, "t", j)])
        for j, (src_ap, dst_ap, n, skey, dkey, sinv_col, sinv_key) in enumerate(jobs):
            t, eb, ef, e2, qi = bufs(j, n)
            a.op("vector", (lambda e, t=t: e.tensor_scalar(
                t, t, -self.MAGIC, None, OP.add)),
                 reads=[(nm, "t", j)], writes=[(nm, "t", j)])
        for j, (src_ap, dst_ap, n, skey, dkey, sinv_col, sinv_key) in enumerate(jobs):
            t, eb, ef, e2, qi = bufs(j, n)
            a.op("vector", (lambda e, eb=eb, e2=e2: e.tensor_scalar(
                e2.bitcast(I32), eb, 1, -3, OP.mult, OP.add)),
                 reads=[(nm, "efc", j)], writes=[(nm, "ea", j)])
        for j, (src_ap, dst_ap, n, skey, dkey, sinv_col, sinv_key) in enumerate(jobs):
            t, eb, ef, e2, qi = bufs(j, n)
            a.op("vector", (lambda e, e2=e2: e.tensor_scalar(
                e2.bitcast(I32), e2.bitcast(I32), 23, None,
                OP.logical_shift_left)),
                 reads=[(nm, "ea", j)], writes=[(nm, "e2", j)])
        for j, (src_ap, dst_ap, n, skey, dkey, sinv_col, sinv_key) in enumerate(jobs):
            t, eb, ef, e2, qi = bufs(j, n)
            dk = dkey if isinstance(dkey, list) else [dkey]
            a.op("vector", (lambda e, t=t, e2=e2, dst_ap=dst_ap:
                            e.tensor_tensor(dst_ap, t, e2, OP.mult)),
                 reads=[(nm, "t", j), (nm, "e2", j)], writes=dk)


def q_scalar_bcast(a, u, src_row_ap, srckey, dst_col, dstkey, ps, neg=False):
    a.op("tensor", lambda e: e.matmul(ps[:, 0:1], u["ones_row32"][:, :],
                                      src_row_ap, start=True, stop=True),
         reads=[srckey, "c"], writes=[("PSM", id(ps))])
    a.op("vector", lambda e: e.tensor_scalar(dst_col, ps[:, 0:1],
                                             -1.0 if neg else 1.0, None, OP.mult),
         reads=[("PSM", id(ps))], writes=[dstkey])


def mk_resid_evac(a, xstg, NT, tag, srcdram, dstdram, srckey_fn, out_key):
    def cb(a_, g, ps, pk):
        m, n0, nn = g
        gi = (m * len(NT)) + [x[0] for x in NT].index(n0)
        sl_in = xstg[:, (gi % 2) * 512:(gi % 2) * 512 + nn]
        sl_out = xstg[:, (2 + gi % 2) * 512:(2 + gi % 2) * 512 + nn]
        rd = [(tag, "xin", gi % 2)]
        sk = srckey_fn(m, n0) if srckey_fn else None
        if sk:
            rd.append(sk)
        a_.dma("scalar", sl_in, srcdram[128 * m:128 * (m + 1), n0:n0 + nn],
               reads=rd, writes=[(tag, "xld", gi % 2)])
        a_.op("vector", (lambda e: e.tensor_tensor(sl_out, sl_in, ps[:, :nn], OP.add)),
              reads=[pk, (tag, "xld", gi % 2), (tag, "xst", gi % 2)],
              writes=[(tag, "xin", gi % 2), (tag, "xout", gi % 2)])
        a_.dma("scalar", dstdram[128 * m:128 * (m + 1), n0:n0 + nn], sl_out,
               reads=[(tag, "xout", gi % 2)],
               writes=[(tag, "xst", gi % 2), (out_key, m, n0)])
    return cb


def build_l1(S, ctx=None):
    T = S // NCORES
    TC = T // 128
    NB = S // 128
    NT = nspans(T)
    ND = nspans(D)
    FD = 4 * D
    WSL = FD // NCORES
    PSL = D // NCORES
    nc = _NCPfx(ctx["nc"], ctx["pfx"]) if ctx else bass.Bass("TRN2", target_bir_lowering=False)
    xT = ctx["x_in"] if ctx else nc.dram_tensor("xT", [D, T], F32, kind="ExternalInput").ap()
    wq_d = nc.dram_tensor("wq", [D, D], F32, kind="ExternalInput").ap()
    wk_d = nc.dram_tensor("wk", [D, 512], F32, kind="ExternalInput").ap()
    wv_d = nc.dram_tensor("wv", [D, 512], F32, kind="ExternalInput").ap()
    wo_d = nc.dram_tensor("wo", [D, D], F32, kind="ExternalInput").ap()
    wfcs_d = nc.dram_tensor("wfcs", [D, WSL], F32, kind="ExternalInput").ap()
    wpjs_d = nc.dram_tensor("wpjs", [FD, PSL], F32, kind="ExternalInput").ap()
    cs_d = nc.dram_tensor("cs", [128, S], F16, kind="ExternalInput").ap()
    slopes_d = nc.dram_tensor("slopes", [1, 2], F32, kind="ExternalInput").ap()
    cmask_d = nc.dram_tensor("cmask", [128, 4 * 512], BF16, kind="ExternalInput").ap()
    cdram = ctx["cdram"] if ctx else _mk_cdram(nc)
    x2T = ctx["x_out"] if ctx else nc.dram_tensor("x2T", [D, T], F32, kind="ExternalOutput").ap()
    qa_i = nc.dram_tensor("qa_i", [D, T], F16)
    qa_o = nc.dram_tensor("qa_o", [D, T], F16)
    ka_i = nc.dram_tensor("ka_i", [D, T], F16)
    ka_o = nc.dram_tensor("ka_o", [D, T], F16)
    va_i = nc.dram_tensor("va_i", [S, 256], F16)
    va_o = nc.dram_tensor("va_o", [S, 256], F16)
    ya_i = nc.dram_tensor("ya_i", [D, T], F16)
    ldbg = nc.dram_tensor("ldbg", [2 * (S // 128) + 2, 512], F32)
    ya_o = nc.dram_tensor("ya_o", [D, T], F16)
    mx_i = nc.dram_tensor("mx_i", [1, 4], F32)
    mx_o = nc.dram_tensor("mx_o", [1, 4], F32)
    hx_i = nc.dram_tensor("hx_i", [1, 1], F32)
    hx_o = nc.dram_tensor("hx_o", [1, 1], F32)
    w8fc_p = nc.dram_tensor("w8fc_p", [D, WSL], BF16)
    w8fc_g = nc.dram_tensor("w8fc_g", [NCORES * D, WSL], BF16)
    w8pj_p = nc.dram_tensor("w8pj_p", [FD, PSL], BF16)
    w8pj_g = nc.dram_tensor("w8pj_g", [NCORES * FD, PSL], BF16)
    h_d = nc.dram_tensor("h_d", [FD, T], F32)
    x2p_d = nc.dram_tensor("x2p", [D, T], F32)
    xn_d = nc.dram_tensor("xn_d", [D, T], F16)
    xn32_d = nc.dram_tensor("xn32_d", [D, T], F32)

    es = contextlib.ExitStack()
    with es:
        a = ctx["a"] if ctx else Auto(nc, es)
        sb = lambda n, shp, dt, st=es, side=None: st.enter_context(
            nc.sbuf_tensor(n, shp, dt, side=side))
        u = mk_utils(a, nc, es, T)
        PS = [es.enter_context(nc.psum_tensor(f"PS{i}", [128, 512], F32)) for i in range(5)]
        PR = [es.enter_context(nc.psum_tensor("PR0", [1, 512], F32))]
        PSC = es.enter_context(nc.psum_tensor("PSC", [128, 512], BF16))
        PSC16 = es.enter_context(nc.psum_tensor("PSC16", [128, 128], F16))
        u["pstat"] = [PS[0], PS[1], PS[2], PR[0]]
        u["psb"] = PS[3:5]
        u["idf16"] = sb("u_idf16", [128, 128], F16)
        stage = sb("stage", [128, 2 * D], F32)
        iota_f = sb("iota_f", [128, 1], F32)
        iota_i = sb("iota_i", [128, 1], I32)
        slope_col = sb("slope_col", [128, 2], F32)
        scl = sb("scl", [128, 8], F32)
        srow = sb("srow", [1, 24], F32)
        r0x = contextlib.ExitStack()
        xn = sb("xn", [128, KC * T], F16, r0x)

        r0 = contextlib.ExitStack()
        x_sb = sb("x_sb", [128, KC * T], F32, r0)
        u["xbf"] = sb("xbf", [128, 2 * T], BF16, r0)
        u["sqbf"] = sb("sqbf", [128, 2 * T], BF16, r0)
        q_load_consts(a, u, cdram)
        a.op("vector", lambda e: e.tensor_copy(u["idf16"][:, :], u["id32"][:, :]),
             reads=[("c", "id32")], writes=[("c", "idf16")])
        a.op("vector", lambda e: e.memset(srow[:1, :], 0.0), writes=[("srow0",)])
        a.op("gpsimd", lambda e: e.iota(iota_i[:, :], [[1, 1]], base=0, channel_multiplier=1),
             writes=[("iota_i",)])
        a.op("vector", lambda e: e.tensor_copy(iota_f[:, :], iota_i[:, :]),
             reads=[("iota_i",)], writes=[("iota",)])
        q_load_act(a, xT, x_sb, T, "x")
        q_layernorm(a, u, T, x_sb, "x", xn, "xn", keep32=True)
        for k in range(KC):
            a.dma("sync", xn32_d[128 * k:128 * (k + 1), :], x_sb[:, k * T:(k + 1) * T],
                  reads=[(("xn", "32", k))], writes=[("xn32_d", k)])
        _drain_dmas(a)
        a.phase()
        r0.close()
        XN = [("xn", k) for k in range(KC)]

        r1 = contextlib.ExitStack()
        w_sb = sb("w_sb", [128, KC * D], F16, r1)
        qstg = sb("qstg", [128, 2 * T], F16, r1)
        q_stream_weight(a, wq_d, w_sb, KC, D, stage, "pq", "pqw")
        pqw_keys = [("pqw", k) for k in range(KC)]
        qgroups = [(m, n0, nn) for m in range(KC) for (n0, nn) in NT]

        def q_evac(a_, g, ps, pk):
            m, n0, nn = g
            slot = m % 2
            a_.op("vector", (lambda e: e.tensor_copy(
                qstg[:, slot * T + n0:slot * T + n0 + nn], ps[:, :nn])),
                reads=[pk], writes=[("qstg", slot, n0)])
            if n0 + nn >= T:
                a_.dma("scalar", qa_i[128 * m:128 * (m + 1), :],
                       qstg[:, slot * T:slot * T + T],
                       reads=[("qstg", slot, x0) for (x0, _) in NT],
                       writes=[("qa_i", m)])
        q_mm(a, "pqmm", qgroups,
             lambda g, k: w_sb[:, k * D + g[0] * 128:k * D + (g[0] + 1) * 128],
             lambda g, k: xn[:, k * T + g[1]:k * T + g[1] + g[2]],
             KC, PS[0:2], q_evac, lambda g: XN + pqw_keys,
             lambda g, ps: ps[:, :g[2]])
        for k in range(KC):
            a.op("vector", (lambda e, k=k: e.reduce_max(
                scl[:, 1:2], xn[:, k * T:(k + 1) * T], axis=AX.X,
                apply_absolute_value=True)),
                reads=[("xn", k)], writes=[("xm1",)])
            if k == 0:
                a.op("vector", lambda e: e.tensor_copy(scl[:, 0:1], scl[:, 1:2]),
                     reads=[("xm1",)], writes=[("xmacc",)])
            else:
                a.op("vector", lambda e: e.tensor_tensor(scl[:, 0:1], scl[:, 0:1],
                                                         scl[:, 1:2], OP.max),
                     reads=[("xm1",), ("xmacc",)], writes=[("xmacc",)])
        a.op("tensor", lambda e: e.transpose(PS[2][:1, :128], scl[:, 0:1], u["id32"][:, :]),
             reads=[("xmacc",), "c"], writes=[("PSM", id(PS[2]))])
        a.op("vector", lambda e: e.reduce_max(srow[:1, 0:1], PS[2][:1, :128], axis=AX.X),
             reads=[("PSM", id(PS[2])), ("srow0",)], writes=[("sxl",)])
        a.phase()

        kv_sb = sb("kv_sb", [128, 4 * T], F16, r1)
        vt_sb = sb("vt_sb", [128, TC * 512], F16, r1)
        q_stream_weight(a, wk_d, w_sb, KC, 512, stage, "pk", "pkw")
        kgroups = [(m, n0, nn) for m in range(4) for (n0, nn) in NT]
        q_mm(a, "pkmm", kgroups,
             lambda g, k: w_sb[:, k * 512 + g[0] * 128:k * 512 + (g[0] + 1) * 128],
             lambda g, k: xn[:, k * T + g[1]:k * T + g[1] + g[2]],
             KC, PS[0:2],
             lambda a_, g, ps, pk: a_.op(
                 "vector",
                 (lambda e, g=g, ps=ps: e.tensor_copy(
                     kv_sb[:, g[0] * T + g[1]:g[0] * T + g[1] + g[2]], ps[:, :g[2]])),
                 reads=[pk], writes=[("kf", g[0], g[1])]),
             lambda g: XN + [("pkw", k) for k in range(KC)],
             lambda g, ps: ps[:, :g[2]])
        q_stream_weight(a, wv_d, w_sb, KC, 512, stage, "pv", "pvw")
        vgroups = [(t, n0, nn) for t in range(TC) for (n0, nn) in nspans(512)]
        q_mm(a, "pvmm", vgroups,
             lambda g, k: xn[:, k * T + g[0] * 128:k * T + (g[0] + 1) * 128],
             lambda g, k: w_sb[:, k * 512 + g[1]:k * 512 + g[1] + g[2]],
             KC, PS[0:2],
             lambda a_, g, ps, pk: a_.op(
                 "vector",
                 (lambda e, g=g, ps=ps: e.tensor_copy(
                     vt_sb[:, g[0] * 512 + g[1]:g[0] * 512 + g[1] + g[2]], ps[:, :g[2]])),
                 reads=[pk], writes=[("vt", g[0], g[1])]),
             lambda g: XN + [("pvw", k) for k in range(KC)],
             lambda g, ps: ps[:, :g[2]])
        kf_all = [("kf", g, n0) for g in range(4) for (n0, _) in NT]
        vt_all = [("vt", t, n0) for t in range(TC) for (n0, _) in nspans(512)]
        for j in range(NCORES):
            for sgi, grp in enumerate((j // 4, 2 + j // 4)):
                a.dma("scalar", ka_i[256 * j + 128 * sgi:256 * j + 128 * (sgi + 1), :],
                      kv_sb[:, grp * T:(grp + 1) * T],
                      reads=kf_all, writes=[("ka_i", j, sgi)])
        for j in range(NCORES):
            for sgi, grp in enumerate((j // 4, 2 + j // 4)):
                a.dma("sync",
                      va_i[j * T:(j + 1) * T,
                           128 * sgi:128 * (sgi + 1)].rearrange(
                          "(t p) c -> p t c", t=TC, p=128),
                      vt_sb[:, :].rearrange("p (t d) -> p t d", t=TC, d=512)[
                          :, :, grp * 128:(grp + 1) * 128],
                      reads=vt_all, writes=[("va_i", j, 0, sgi)])
        def slice_absmax(wdram, kin, M, tagn, out_idx):
            for k in range(kin):
                a.dma("sync", stage[:, (k % 2) * M:(k % 2) * M + M],
                      wdram[128 * k:128 * (k + 1), :],
                      reads=([(tagn, "rd", k - 2)] if k >= 2 else []) + [("STG", id(stage), k % 2)],
                      writes=[(tagn, "ld", k % 2), ("STG", id(stage), k % 2)])
                a.op("vector", (lambda e, k=k: e.reduce_max(
                    scl[:, 3:4], stage[:, (k % 2) * M:(k % 2) * M + M], axis=AX.X,
                    apply_absolute_value=True)),
                    reads=[(tagn, "ld", k % 2), ("STG", id(stage), k % 2)],
                    writes=[(tagn, "m1"), (tagn, "rd", k)])
                if k == 0:
                    a.op("vector", lambda e: e.tensor_copy(scl[:, 2:3], scl[:, 3:4]),
                         reads=[(tagn, "m1")], writes=[(tagn, "acc")])
                else:
                    a.op("vector", lambda e: e.tensor_tensor(scl[:, 2:3], scl[:, 2:3],
                                                             scl[:, 3:4], OP.max),
                         reads=[(tagn, "m1"), (tagn, "acc")], writes=[(tagn, "acc")])
            a.op("tensor", lambda e: e.transpose(PS[2][:1, :128], scl[:, 2:3],
                                                 u["id32"][:, :]),
                 reads=[(tagn, "acc"), "c"], writes=[("PSM", id(PS[2]))])
            a.op("vector", (lambda e: e.reduce_max(
                srow[:1, out_idx:out_idx + 1], PS[2][:1, :128], axis=AX.X)),
                reads=[("PSM", id(PS[2]))], writes=[(tagn, "gmax")])
        slice_absmax(wfcs_d, KC, WSL, "wfm", 1)
        slice_absmax(wpjs_d, FD // 128, PSL, "wpm", 2)
        a.dma("scalar", mx_i[:1, 0:4], srow[:1, 0:4],
              reads=[("sxl",), ("wfm", "gmax"), ("wpm", "gmax"), ("srow0",)], writes=[("mx_i",)])
        a.cc("AllReduce", OP.max, mx_i, mx_o, [("mx_i",)], "mx_o")
        a.cc("AllToAll", OP.bypass, qa_i, qa_o, [("qa_i", m) for m in range(KC)], "qa_o")
        a.cc("AllToAll", OP.bypass, ka_i, ka_o,
             [("ka_i", j, sgi) for j in range(NCORES) for sgi in range(2)], "ka_o")
        a.cc("AllToAll", OP.bypass, va_i, va_o,
             [("va_i", j, 0, sgi) for j in range(NCORES)
              for sgi in range(2)], "va_o")
        q_store_act(a, xn, xn_d, T, "xn", "xn_d", eng="sync")
        _drain_dmas(a)
        a.phase()
        r1.close()
        r0x.close()
        return _l1_attn(nc, es, a, u, sb, PS, PR, locals())
    return nc


def _l1_attn(nc, es, a, u, sb, PS, PR, L):
    PSC = L["PSC"]
    PSC16 = L["PSC16"]
    S, T, TC, NB = L["S"], L["T"], L["TC"], L["NB"]
    NT, ND = L["NT"], L["ND"]
    xn, stage = L["xn"], L["stage"]
    iota_f, slope_col, scl, srow = L["iota_f"], L["slope_col"], L["scl"], L["srow"]
    qa_o, ka_o, va_o, ya_i, ya_o = L["qa_o"], L["ka_o"], L["va_o"], L["ya_i"], L["ya_o"]
    ldbg = L["ldbg"]
    cs_d, slopes_d, cmask_d = L["cs_d"], L["slopes_d"], L["cmask_d"]
    wo_d, x2T, x2p_d, xT = L["wo_d"], L["x2T"], L["x2p_d"], L["xT"]
    mx_o = L.get("mx_o")

    r2 = contextlib.ExitStack()
    qh = sb("qh", [128, 2 * S], F16, r2)
    kh = sb("kh", [128, 2 * S], F16, r2)
    vh = sb("vh", [128, 2 * S], F16, r2)
    cs_sb = sb("cs_sb", [128, S], F16, r2)
    cmask_sb = sb("cmask_sb", [128, 4 * 512], BF16, r2)
    sqt = sb("sqt", [128, 1024], F16, r2)
    bias0 = sb("bias0", [128, 2], F32, r2)
    a.dma("sync", cs_sb[:, :], cs_d, writes=[("cs",)])
    a.dma("sync", cmask_sb[:, :], cmask_d, writes=[("cm",)])
    a.dma("sync", stage[:1, 0:2].bitcast(F32) if False else stage[:1, 0:2], slopes_d,
          writes=[("slp_r",)])
    for r in range(NCORES):
        for h in range(2):
            a.dma("sync", qh[:, h * S + r * T:h * S + (r + 1) * T],
                  qa_o[256 * r + 128 * h:256 * r + 128 * (h + 1), :],
                  reads=["qa_o"], writes=[("qh", h, r)])
            a.dma("sync", kh[:, h * S + r * T:h * S + (r + 1) * T],
                  ka_o[256 * r + 128 * h:256 * r + 128 * (h + 1), :],
                  reads=["ka_o"], writes=[("kh", h, r)])
    for h in range(2):
        a.dma("scalar", vh[:, h * S:(h + 1) * S].rearrange(
                  "p (b c) -> p b c", b=NB, c=128),
              va_o[:, 128 * h:128 * (h + 1)].rearrange(
                  "(b p) c -> p b c", b=NB, p=128),
              reads=["va_o"], writes=[("vh", h, 0)])
    for h in range(2):
        q_scalar_bcast(a, u, stage[:1, h:h + 1], ("slp_r",), slope_col[:, h:h + 1],
                       ("slope", h), PS[2])

    def rope(nm, arena, h, qkeys):
        base = h * S
        A = lambda r0, r1: arena[r0:r1, base:base + S]
        RT = ("RT",)
        a.dma("sync", rtmp[0:64, 0:S], A(64, 128),
              reads=qkeys + [RT], writes=[(nm, "sw1"), RT])
        a.dma("sync", rtmp[64:128, 0:S], A(0, 64),
              reads=qkeys + [RT], writes=[(nm, "sw2"), RT])
        a.op("vector", lambda e: e.tensor_tensor(A(0, 64), A(0, 64),
                                                 cs_sb[0:64, 0:S], OP.mult),
             reads=qkeys + [("cs",), (nm, "sw2")], writes=[(nm, "q1")])
        a.op("vector", lambda e: e.tensor_tensor(rtmp[0:64, 0:S], rtmp[0:64, 0:S],
                                                 cs2_sb[0:64, 0:S], OP.mult),
             reads=[(nm, "sw1"), ("cs2",), RT], writes=[(nm, "p1"), RT])
        a.op("vector", lambda e: e.tensor_tensor(A(0, 64), A(0, 64),
                                                 rtmp[0:64, 0:S], OP.subtract),
             reads=[(nm, "q1"), (nm, "p1"), RT], writes=[(nm, "q1")])
        a.op("vector", lambda e: e.tensor_tensor(A(64, 128), A(64, 128),
                                                 cs2_sb[64:128, 0:S], OP.mult),
             reads=qkeys + [("cs2",), (nm, "sw1")], writes=[(nm, "q2")])
        a.op("vector", lambda e: e.tensor_tensor(rtmp[64:128, 0:S], rtmp[64:128, 0:S],
                                                 cs_sb[64:128, 0:S], OP.mult),
             reads=[(nm, "sw2"), ("cs",), RT], writes=[(nm, "p2"), RT])
        a.op("vector", lambda e: e.tensor_tensor(A(64, 128), A(64, 128),
                                                 rtmp[64:128, 0:S], OP.add),
             reads=[(nm, "q2"), (nm, "p2"), RT], writes=[(nm, "done")])

    NBQ = S // 128
    mcol = sb("mcol", [128, 2 * NBQ], F32, r2)
    nrow = sb("nrow", [1, 512], F32, r2)
    r2r = contextlib.ExitStack()
    rtmp = sb("rtmp", [128, S], F16, r2r)
    cs2_sb = sb("cs2_sb", [128, S], F16, r2r)
    a.dma("sync", cs2_sb[0:64, 0:S], cs_sb[64:128, 0:S],
          reads=[("cs",)], writes=[("cs2",)])
    a.dma("sync", cs2_sb[64:128, 0:S], cs_sb[0:64, 0:S],
          reads=[("cs",)], writes=[("cs2",)])
    for h in range(2):
        qk = [("qh", h, r) for r in range(NCORES)]
        rope(f"rq{h}", qh, h, qk)
        a.op("vector", (lambda e, h=h: e.tensor_scalar(
            qh[:, h * S:(h + 1) * S], qh[:, h * S:(h + 1) * S],
            0.08838834764831845, None, OP.mult)),
            reads=[(f"rq{h}", "q1"), (f"rq{h}", "done")], writes=[(f"rq{h}", "done")])
        kk = [("kh", h, r) for r in range(NCORES)]
        rope(f"rk{h}", kh, h, kk)

    def kmaxnorm(nm, h, dkeys):
        for i, (n0, nn) in enumerate(nspans(S, 1024)):
            a.op("scalar", (lambda e, n0=n0, nn=nn: e.activation(
                sqt[:, 0:nn], kh[:, h * S + n0:h * S + n0 + nn], AF.Square)),
                reads=dkeys + [("SQT",)], writes=[(nm, "sq", i), ("SQT",)])
            for j, (m0, mm) in enumerate(nspans(nn)):
                a.op("tensor", (lambda e, m0=m0, mm=mm: e.matmul(
                    PR[0][:1, :mm], u["ones_col"][:, :], sqt[:, m0:m0 + mm],
                    start=True, stop=True)),
                    reads=[(nm, "sq", i), "c", ("SQT",)], writes=[("PSM", id(PR[0]))])
                a.op("vector", (lambda e: e.reduce_max(
                    scl[:1, 5:6], PR[0][:1, :mm], axis=AX.X)),
                    reads=[("PSM", id(PR[0])), ("SCL5",)],
                    writes=[(nm, "m1"), ("SCL5",)])
                if i == 0 and j == 0:
                    a.op("vector", lambda e: e.tensor_copy(srow[:1, 5:6], scl[:1, 5:6]),
                         reads=[(nm, "m1"), ("srow0",), ("SCL5",)], writes=[(nm, "acc")])
                else:
                    a.op("vector", (lambda e: e.tensor_tensor(
                        srow[:1, 5:6], srow[:1, 5:6], scl[:1, 5:6], OP.max)),
                        reads=[(nm, "m1"), (nm, "acc"), ("SCL5",)], writes=[(nm, "acc")])

    for h in range(2):
        kmaxnorm(f"nk{h}", h, [(f"rk{h}", "done"), (f"rk{h}", "q1")])
        a.op("scalar", (lambda e, h=h: e.activation(srow[:1, 6 + h:7 + h],
                                                    srow[:1, 5:6], AF.Sqrt)),
             reads=[(f"nk{h}", "acc"), ("srow0",)], writes=[("kmax", h)])
        nm = f"nq{h}"
        qdk = [(f"rq{h}", "done")]
        for i, (n0, nn) in enumerate(nspans(S)):
            a.op("scalar", (lambda e, n0=n0, nn=nn: e.activation(
                sqt[:, 0:nn], qh[:, h * S + n0:h * S + n0 + nn], AF.Square)),
                reads=qdk + [("SQT",)], writes=[(nm, "sq", i), ("SQT",)])
            a.op("tensor", (lambda e, nn=nn: e.matmul(
                PR[0][:1, :nn], u["ones_col"][:, :], sqt[:, 0:nn],
                start=True, stop=True)),
                reads=[(nm, "sq", i), "c", ("SQT",)], writes=[("PSM", id(PR[0]))])
            a.op("vector", (lambda e, nn=nn: e.tensor_copy(nrow[:1, :nn], PR[0][:1, :nn])),
                 reads=[("PSM", id(PR[0])), ("NRW",)], writes=[(nm, "r", i), ("NRW",)])
            a.op("scalar", (lambda e, nn=nn: e.activation(nrow[:1, :nn], nrow[:1, :nn],
                                                          AF.Sqrt)),
                 reads=[(nm, "r", i), ("NRW",)], writes=[(nm, "r", i), ("NRW",)])
            a.op("vector", (lambda e, h=h, nn=nn: e.tensor_scalar(
                nrow[:1, :nn], nrow[:1, :nn], srow[:1, 6 + h:7 + h], 1.0,
                OP.mult, OP.add)),
                reads=[(nm, "r", i), ("kmax", h), ("NRW",)], writes=[(nm, "r", i), ("NRW",)])
            for j in range(nn // 128):
                t = (n0 + j * 128) // 128
                a.op("tensor", (lambda e, j=j: e.transpose(
                    PS[4][:, 0:1], nrow[:1, 128 * j:128 * (j + 1)], u["id32"][:1, :1])),
                    reads=[(nm, "r", i), ("NRW",), "c"], writes=[("PSM", id(PS[4]))])
                a.op("vector", (lambda e, t=t, h=h: e.tensor_scalar(
                    mcol[:, h * NBQ + t:h * NBQ + t + 1], PS[4][:, 0:1], -1.0, None,
                    OP.mult)),
                    reads=[("PSM", id(PS[4]))], writes=[("mc", h, t)])
    a.op("vector", lambda e: e.tensor_scalar(slope_col[:, 0:2], slope_col[:, 0:2],
                                             -1.0, None, OP.mult),
         reads=[("slope", 0), ("slope", 1)], writes=[("negsl",)])
    a.phase()
    r2r.close()

    ramp = sb("ramp", [128, 2 * 5 * 512], F32, r2)
    pt_sb = sb("pt", [128, 2 * 512], BF16, r2)
    ptt = sb("ptt", [128, 512], BF16, r2)
    lacc = sb("lacc", [128, 24], F32, r2)
    ystg = sb("ystg", [128, 2 * 128], F16, r2)
    yout = sb("yout", [128, 2 * 128], F16, r2)
    irow = sb("irow", [1, 512], F32, r2)
    irbf = sb("irbf", [1, 512], BF16, r2)
    bcol = sb("bcol", [128, 4], F32, r2)
    basc = sb("basc", [128, 2], F32, r2)
    vh_keys = [[("vh", h, 0)] for h in range(2)]
    NO_RAMP = bool(L.get("no_alibi"))
    a.op("gpsimd", lambda e: e.iota(irow[:1, :].bitcast(I32), [[1, 512]], base=0,
                                    channel_multiplier=0),
         writes=[("irowi",)])
    a.op("vector", lambda e: e.tensor_copy(irow[:1, :], irow[:1, :].bitcast(I32)),
         reads=[("irowi",)], writes=[("irbf",)])
    a.op("tensor", lambda e: e.matmul(PS[4][:, :512], u["ones_row32"][:, :], irow[:1, :],
                                      start=True, stop=True),
         reads=[("irbf",), "c"], writes=[("PSM", id(PS[4]))])
    for h in range(2 if not NO_RAMP else 0):
        a.op("vector", (lambda e, h=h: e.tensor_scalar(
            sqt[:, 0:512], PS[4][:, :512], slope_col[:, h:h + 1], None, OP.mult)),
            reads=[("PSM", id(PS[4])), ("negsl",), ("SQT",)],
            writes=[("rmp0", h), ("SQT",)])
        a.op("vector", (lambda e, h=h: e.tensor_scalar(
            scl[:, 6:7], slope_col[:, h:h + 1], 511.0, None, OP.mult)),
            reads=[("negsl",), ("SCL6",)], writes=[("r511", h), ("SCL6",)])
        a.op("vector", (lambda e, h=h: e.tensor_scalar(
            ramp[:, h * 5 * 512:h * 5 * 512 + 512], sqt[:, 0:512], -1.0,
            scl[:, 6:7], OP.mult, OP.add)),
            reads=[("rmp0", h), ("r511", h), ("SQT",), ("SCL6",)], writes=[("ramp", h, 0)])
        for v in range(4):
            a.op("vector", (lambda e, h=h, v=v: e.tensor_tensor(
                ramp[:, (h * 5 + 1 + v) * 512:(h * 5 + 2 + v) * 512],
                ramp[:, h * 5 * 512:h * 5 * 512 + 512],
                cmask_sb[:, v * 512:(v + 1) * 512], OP.add)),
                reads=[("ramp", h, 0), ("cm",)], writes=[("ramp", h, 1 + v)])

    def flash(h):
        nm = f"fl{h}"
        qdone = (f"rq{h}", "done")
        kdone = [(f"rk{h}", "done"), (f"rk{h}", "q1")]
        noal = bool(L.get("no_alibi"))
        WINB = L.get("winA", 6) if h == 0 else NB + 9
        steps = []
        for t in range(NBQ):
            q0 = t * 128
            kb_hi = q0 // 512
            kb_lo = max(0, kb_hi - WINB)
            for bi, b in enumerate(range(kb_lo, kb_hi + 1)):
                k0 = b * 512
                kw = 512 if k0 + 512 <= q0 else (q0 + 128 - k0)
                var = 0 if k0 + 512 <= q0 else 1 + ((q0 - k0) // 128)
                steps.append(dict(t=t, q0=q0, k0=k0, kw=kw, var=var, bi=bi,
                                  first=(bi == 0), last=(b == kb_hi)))
        n = len(steps)

        def eS(g):
            st = steps[g]
            psx = PS[g % 2]
            a.op("tensor", (lambda e, h=h, q0=st["q0"], k0=st["k0"], kw=st["kw"],
                            psx=psx: e.matmul(
                psx[:, :kw], qh[:, h * S + q0:h * S + q0 + 128],
                kh[:, h * S + k0:h * S + k0 + kw], start=True, stop=True)),
                 reads=[qdone] + kdone, writes=[("PSM", id(psx))], tag=nm)

        def eBase(g):
            st = steps[g]
            if noal or not st["first"]:
                return
            t = st["t"]
            sl = t % 2
            a.op("gpsimd", (lambda e, h=h, sl=sl: e.tensor_scalar(
                basc[:, sl:sl + 1], iota_f[:, :], slope_col[:, h:h + 1],
                None, OP.mult)),
                reads=[("iota",), ("negsl",), ("bau", sl)], writes=[("ba0", sl)])
            a.op("gpsimd", (lambda e, h=h, t=t, sl=sl: e.tensor_tensor(
                basc[:, sl:sl + 1], basc[:, sl:sl + 1],
                mcol[:, h * NBQ + t:h * NBQ + t + 1], OP.add)),
                reads=[("ba0", sl), ("mc", h, t)], writes=[("bas", sl)])

        def ePre(g):
            st = steps[g]
            t, kw, var = st["t"], st["kw"], st["var"]
            psx = PS[g % 2]
            pskey = ("PSM", id(psx))
            if noal:
                if var > 0:
                    a.op("vector", (lambda e, var=var, kw=kw, psx=psx: e.tensor_tensor(
                        psx[:, :kw], psx[:, :kw],
                        cmask_sb[:, (var - 1) * 512:(var - 1) * 512 + kw], OP.add)),
                        reads=[pskey, ("cm",)], writes=[pskey])
                bias_ap = mcol[:, h * NBQ + t:h * NBQ + t + 1]
                bias_keys = [("mc", h, t)]
            else:
                a.op("vector", (lambda e, h=h, var=var, kw=kw, psx=psx: e.tensor_tensor(
                    psx[:, :kw], psx[:, :kw],
                    ramp[:, (h * 5 + var) * 512:(h * 5 + var) * 512 + kw], OP.add)),
                    reads=[pskey, ("ramp", h, var)], writes=[pskey])
                bci = g % 4
                cst = float(st["q0"] - st["k0"] - 511)
                sl = t % 2
                a.op("vector", (lambda e, h=h, cst=cst, bci=bci, sl=sl:
                                e.scalar_tensor_tensor(
                    bcol[:, bci:bci + 1], slope_col[:, h:h + 1], cst,
                    basc[:, sl:sl + 1], OP.mult, OP.add)),
                    reads=[("negsl",), ("bas", sl), ("bcu", bci)],
                    writes=[("bc", bci), ("bau", sl)])
                bias_ap = bcol[:, bci:bci + 1]
                bias_keys = [("bc", bci)]
            bi = st["bi"]
            a.op("scalar", (lambda e, kw=kw, psx=psx, bias_ap=bias_ap, g=g, bi=bi:
                            e.activation(
                pt_sb[:, (g % 2) * 512:(g % 2) * 512 + kw], psx[:, :kw], AF.Exp,
                bias=bias_ap, accum_out=lacc[:, 2 + bi:3 + bi])),
                 reads=[pskey, ("la1", bi)] + bias_keys,
                 writes=[("pt", g % 2), ("la1", bi)]
                        + ([] if noal else [("bcu", g % 4)]))

        def ePV(g):
            st = steps[g]
            t, k0, kw = st["t"], st["k0"], st["kw"]
            po = PS[2 + (t % 2)]
            pok = ("PSM", id(po))
            ncc = ceil_div(kw, 128)
            for cc in range(ncc):
                a.op("tensor", (lambda e, g=g, cc=cc: e.transpose(
                    PSC[:, cc * 128:cc * 128 + 128],
                    pt_sb[:, (g % 2) * 512 + cc * 128:(g % 2) * 512 + cc * 128 + 128],
                    u["idbf"][:, :])),
                    reads=[("pt", g % 2), "c"], writes=[("psct", cc)], tag=nm)
            a.op("scalar", (lambda e, kw=kw: e.activation(
                ptt[:, 0:kw], PSC[:, 0:kw], AF.Copy)),
                reads=[("psct", cc2) for cc2 in range(ncc)]
                      + [("ptt", cc2) for cc2 in range(4)],
                writes=[("pttv",)])
            for cc in range(ncc):
                a.op("tensor", (lambda e, h=h, k0=k0, cc=cc, po=po,
                                st_=(st["first"] and cc == 0),
                                sp=(st["last"] and cc == ncc - 1):
                                e.matmul(po[:, :128],
                                         ptt[:, cc * 128:cc * 128 + 128],
                                         vh[:, h * S + (k0 + cc * 128):
                                            h * S + (k0 + cc * 128) + 128],
                                         start=st_, stop=sp)),
                     reads=[("pttv",)] + vh_keys[h],
                     writes=[pok, ("ptt", cc)], tag=nm)
            if not st["last"]:
                return
            q0 = st["q0"]
            nblk = st["bi"] + 1
            a.op("vector", (lambda e, nblk=nblk: e.reduce_sum(
                lacc[:, 0:1], lacc[:, 2:2 + nblk], axis=AX.X)),
                reads=[("la1", b2) for b2 in range(nblk)], writes=[("la",)])
            a.op("vector", lambda e: e.reciprocal(lacc[:, 0:1], lacc[:, 0:1]),
                 reads=[("la",)], writes=[("la",)])
            yslot = t % 2
            a.op("vector", (lambda e, po=po, yslot=yslot: e.tensor_scalar(
                ystg[:, yslot * 128:(yslot + 1) * 128], po[:, :128], lacc[:, 0:1],
                None, OP.mult)),
                reads=[pok, ("la",), ("yts", yslot)], writes=[("ys", yslot)])
            a.op("tensor", (lambda e, yslot=yslot: e.transpose(
                PSC16[:, :128], ystg[:, yslot * 128:(yslot + 1) * 128], u["idf16"][:, :])),
                reads=[("ys", yslot), "c", ("yoc", yslot)],
                writes=[("PSM", id(PSC16))])
            a.op("vector", (lambda e, yslot=yslot: e.tensor_copy(
                yout[:, yslot * 128:(yslot + 1) * 128], PSC16[:, :128])),
                reads=[("PSM", id(PSC16))], writes=[("yo", yslot), ("yts", yslot)])
            r = q0 // T
            c0 = q0 % T
            a.dma("scalar", ya_i[256 * r + 128 * h:256 * r + 128 * (h + 1), c0:c0 + 128],
                  yout[:, yslot * 128:(yslot + 1) * 128],
                  reads=[("yo", yslot)], writes=[("yoc", yslot), ("ya_i", h, t)])

        eS(0)
        eBase(0)
        ePre(0)
        if n > 1:
            eS(1)
        for g in range(n):
            ePV(g)
            if g + 2 < n:
                eS(g + 2)
            if g + 1 < n:
                eBase(g + 1)
                ePre(g + 1)
    flash(0)
    flash(1)
    ya_keys = [("ya_i", hh, t) for hh in range(2) for t in range(NBQ)]
    a.cc("AllToAll", OP.bypass, ya_i, ya_o, ya_keys, "ya_o")
    _drain_dmas(a)
    a.phase()
    r2.close()

    r3 = contextlib.ExitStack()
    xstg = sb("xstg", [128, 4 * 512], F32, r3)
    r3b = contextlib.ExitStack()
    w_sb = sb("wo_sb", [128, KC * D], F16, r3b)
    y_sb = sb("y_sb", [128, KC * T], F16, r3b)
    q_load_act(a, ya_o, y_sb, T, "yf", src_key="ya_o")
    q_stream_weight(a, wo_d, w_sb, KC, D, stage, "po", "pow")
    NTl = NT
    groups = [(m, n0, nn) for m in range(KC) for (n0, nn) in NTl]
    q_mm(a, "pomm", groups,
         lambda g, k: w_sb[:, k * D + g[0] * 128:k * D + (g[0] + 1) * 128],
         lambda g, k: y_sb[:, k * T + g[1]:k * T + g[1] + g[2]],
         KC, PS[0:2],
         mk_resid_evac(a, xstg, NTl, "rp", xT, x2p_d, None, "x2p"),
         lambda g: [("yf", k) for k in range(KC)] + [("pow", k) for k in range(KC)],
         lambda g, ps: ps[:, :g[2]])
    _drain_dmas(a)
    a.phase()
    r3b.close()
    return L.get("ffn_fn", _l1_ffn)(nc, es, a, u, sb, PS, PR, L, r3, w_sb, xstg)


def _l1_ffn(nc, es, a, u, sb, PS, PR, L, r3, w_sb, xstg):
    S, T, TC = L["S"], L["T"], L["TC"]
    NT, ND = L["NT"], L["ND"]
    FD, WSL, PSL = L["FD"], L["WSL"], L["PSL"]
    stage, srow = L["stage"], L["srow"]
    xn_d = L["xn32_d"]
    wfcs_d, wpjs_d = L["wfcs_d"], L["wpjs_d"]
    w8fc_p, w8fc_g, w8pj_p, w8pj_g = L["w8fc_p"], L["w8fc_g"], L["w8pj_p"], L["w8pj_g"]
    h_d, x2p_d, x2T = L["h_d"], L["x2p_d"], L["x2T"]
    mx_o, hx_i, hx_o = L["mx_o"], L["hx_i"], L["hx_o"]

    r4 = contextlib.ExitStack()
    sc2 = sb("sc2", [128, 8], F32, r4)
    fq = Fp8Quant(a, (lambda n, shp, dt: sb(n, shp, dt, r4)), "fq", 512)
    w8stg = sb("w8stg", [128, 2 * WSL], BF16, r4)
    hstg = sb("hstg", [128, 2 * 512], F32, r4)
    wstrip = sb("wstrip", [128, 2 * 16 * 128], BF16, r4)
    a.dma("sync", srow[:1, 12:16], mx_o[:1, :], reads=["mx_o"], writes=[("mxr",)])
    for j, nmk in ((12, "sx"), (13, "swf"), (14, "swp")):
        a.op("vector", (lambda e, j=j: e.tensor_scalar(
            srow[:1, j:j + 1], srow[:1, j:j + 1], 1.0 / 448.0, 1e-6, OP.mult, OP.max)),
            reads=[("mxr",)], writes=[(nmk, "s")])
    a.op("vector", lambda e: e.tensor_tensor(srow[:1, 11:12], srow[:1, 12:13],
                                             srow[:1, 13:14], OP.mult),
         reads=[("sx", "s"), ("swf", "s"), ("srow0",)], writes=[("sxw", "s")])
    for j, nmk in ((12, "sx"), (13, "swf"), (14, "swp")):
        a.op("vector", (lambda e, j=j: e.reciprocal(srow[:1, j + 4:j + 5], srow[:1, j:j + 1])),
             reads=[(nmk, "s")], writes=[(nmk, "i")])
    q_scalar_bcast(a, u, srow[:1, 16:17], ("sx", "i"), sc2[:, 0:1], ("sc2", 0), PS[2])
    q_scalar_bcast(a, u, srow[:1, 17:18], ("swf", "i"), sc2[:, 1:2], ("sc2", 1), PS[2])
    q_scalar_bcast(a, u, srow[:1, 18:19], ("swp", "i"), sc2[:, 2:3], ("sc2", 2), PS[2])
    q_scalar_bcast(a, u, srow[:1, 11:12], ("sxw", "s"), sc2[:, 3:4], ("sc2", 3), PS[2])

    r4b = contextlib.ExitStack()
    x8 = sb("x8", [128, KC * T], BF16, r4b)
    xnld = sb("xnld", [128, 2 * 512], F32, r4b)
    allj = [(k, n0, nn) for k in range(KC) for (n0, nn) in NT]
    pend = []
    for ji, (k, n0, nn) in enumerate(allj):
        sl = (ji % 2) * 512
        a.dma("sync", xnld[:, sl:sl + nn], xn_d[128 * k:128 * (k + 1), n0:n0 + nn],
              reads=[("xn32_d", k)] + ([("x8", "emq", ji - 2)] if ji >= 2 else []),
              writes=[("xnl", ji % 2)])
        pend.append((ji, k, n0,
                     (xnld[:, sl:sl + nn], x8[:, k * T + n0:k * T + n0 + nn], nn,
                      ("xnl", ji % 2), [("x8", k, n0), ("x8", "emq", ji)],
                      sc2[:, 0:1], ("sc2", 0))))
        if len(pend) == 2 or ji == len(allj) - 1:
            fq.emit_jobs(a, [p[3] for p in pend])
            pend = []
    def wquant(wdram, kin, M, wp, tagn, sidx):
        jobs = [(k, n0, nn) for k in range(kin) for (n0, nn) in nspans(M)]
        pend = []
        for idx, (k, n0, nn) in enumerate(jobs):
            if n0 == 0:
                a.dma("sync", stage[:, (k % 2) * M:(k % 2) * M + M],
                      wdram[128 * k:128 * (k + 1), :],
                      reads=([(tagn, "q", k - 2)] if k >= 2 else [])
                            + [("STG", id(stage), k % 2)],
                      writes=[(tagn, "ld", k % 2), ("STG", id(stage), k % 2)])
            pend.append((k, n0,
                         (stage[:, (k % 2) * M + n0:(k % 2) * M + n0 + nn],
                          w8stg[:, (k % 2) * M + n0:(k % 2) * M + n0 + nn], nn,
                          (tagn, "ld", k % 2),
                          [(tagn, "q8", k % 2, n0), ("W8S",),
                           ("STG", id(stage), k % 2)],
                          sc2[:, sidx:sidx + 1], ("sc2", sidx))))
            if len(pend) == 2 or idx == len(jobs) - 1:
                fq.emit_jobs(a, [p[2] for p in pend])
                for k2 in sorted(set(p[0] for p in pend)):
                    if max(n03 for (k3, n03, _) in pend if k3 == k2) \
                            == nspans(M)[-1][0]:
                        a.dma("scalar", wp[128 * k2:128 * (k2 + 1), :],
                              w8stg[:, (k2 % 2) * M:(k2 % 2) * M + M],
                              reads=[(tagn, "q8", k2 % 2, n03)
                                     for (n03, _) in nspans(M)] + [("W8S",)],
                              writes=[(tagn, "q", k2), (tagn, "st", k2)])
                pend = []
    wquant(wfcs_d, KC, WSL, w8fc_p, "qfc", 1)
    wquant(wpjs_d, FD // 128, PSL, w8pj_p, "qpj", 2)
    a.cc("AllGather", OP.bypass, w8fc_p, w8fc_g,
         [("qfc", "st", k) for k in range(KC)], "w8fc_g")
    a.cc("AllGather", OP.bypass, w8pj_p, w8pj_g,
         [("qpj", "st", k) for k in range(FD // 128)], "w8pj_g")
    _drain_dmas(a)
    a.phase()

    a.op("vector", lambda e: e.memset(sc2[:, 5:6], 0.0), writes=[("hmacc",)])
    for m in range(FD // 128):
        r = m // (WSL // 128)
        cm = (m % (WSL // 128)) * 128
        slot = m % 2
        a.dma("sync",
              wstrip[:, slot * 2048:(slot + 1) * 2048].rearrange(
                  "p (k c) -> p k c", k=KC, c=128),
              w8fc_g[D * r:D * (r + 1), cm:cm + 128].rearrange(
                  "(k p) c -> p k c", k=KC, p=128),
              reads=["w8fc_g"] + ([("fcg", m - 2)] if m >= 2 else []),
              writes=[("ws", slot)])
        for i, (n0, nn) in enumerate(NT):
            ps = PS[i % 2]
            pk = ("PSM", id(ps))
            for k in range(KC):
                a.op("tensor", (lambda e, slot=slot, k=k, n0=n0, nn=nn, ps=ps: e.matmul(
                    ps[:, :nn],
                    wstrip[:, slot * 2048 + k * 128:slot * 2048 + (k + 1) * 128],
                    x8[:, k * T + n0:k * T + n0 + nn],
                    start=(k == 0), stop=(k == KC - 1))),
                    reads=[("ws", slot), ("x8", k, n0)],
                    writes=[pk] + ([("fcg", m)] if k == KC - 1 and i == len(NT) - 1 else []))
            hs = hstg[:, i * 512:i * 512 + nn]
            a.op("vector", (lambda e, hs=hs, ps=ps, nn=nn: e.tensor_scalar(
                hs, ps[:, :nn], sc2[:, 3:4], 0.0, OP.mult, OP.max)),
                reads=[pk, ("sc2", 3), ("hst", i)], writes=[("h0", i)])
            a.op("scalar", (lambda e, hs=hs: e.activation(hs, hs, AF.Square)),
                 reads=[("h0", i)], writes=[("h0", i)])
            a.op("vector", (lambda e, hs=hs: e.reduce_max(sc2[:, 4:5], hs, axis=AX.X)),
                 reads=[("h0", i)], writes=[("hm1",)])
            a.op("vector", lambda e: e.tensor_tensor(sc2[:, 5:6], sc2[:, 5:6],
                                                     sc2[:, 4:5], OP.max),
                 reads=[("hm1",), ("hmacc",)], writes=[("hmacc",)])
        a.dma("scalar", h_d[128 * m:128 * (m + 1), 0:T], hstg[:, 0:T],
              reads=[("h0", i2) for i2 in range(len(NT))],
              writes=[("hst", i2) for i2 in range(len(NT))]
                     + [("h_d", m, n0) for (n0, _) in NT])
    a.op("tensor", lambda e: e.transpose(PS[2][:1, :128], sc2[:, 5:6], u["id32"][:, :]),
         reads=[("hmacc",), "c"], writes=[("PSM", id(PS[2]))])
    a.op("vector", lambda e: e.reduce_max(srow[:1, 10:11], PS[2][:1, :128], axis=AX.X),
         reads=[("PSM", id(PS[2])), ("srow0",)], writes=[("hxl",)])
    a.dma("scalar", hx_i[:1, :], srow[:1, 10:11], reads=[("hxl",)], writes=[("hx_i",)])
    a.cc("AllReduce", OP.max, hx_i, hx_o, [("hx_i",)], "hx_o")
    _drain_dmas(a)
    a.phase()
    r4b.close()

    r4c = contextlib.ExitStack()
    h8 = sb("h8", [128, (FD // 128) * 512], BF16, r4c)
    wpstrip = sb("wpstrip", [128, 2 * (FD // 128) * 128], BF16, r4c)
    hload = sb("hload", [128, 2 * 512], F32, r4c)
    a.dma("sync", srow[:1, 10:11], hx_o[:1, :], reads=["hx_o"], writes=[("hxg",)])
    a.op("vector", lambda e: e.tensor_scalar(srow[:1, 10:11], srow[:1, 10:11],
                                             1.0 / 448.0, 1e-6, OP.mult, OP.max),
         reads=[("hxg",)], writes=[("sh", "s")])
    a.op("vector", lambda e: e.tensor_tensor(srow[:1, 9:10], srow[:1, 10:11],
                                             srow[:1, 14:15], OP.mult),
         reads=[("sh", "s"), ("swp", "s"), ("srow0",)], writes=[("shw", "s")])
    a.op("vector", lambda e: e.reciprocal(srow[:1, 19:20], srow[:1, 10:11]),
         reads=[("sh", "s")], writes=[("sh", "i")])
    q_scalar_bcast(a, u, srow[:1, 19:20], ("sh", "i"), sc2[:, 6:7], ("sc2", 6), PS[2])
    q_scalar_bcast(a, u, srow[:1, 9:10], ("shw", "s"), sc2[:, 7:8], ("sc2", 7), PS[2])

    NKF = FD // 128
    for i, (n0, nn) in enumerate(NT):
        pend = []
        for k in range(NKF):
            a.dma("sync", hload[:, (k % 2) * 512:(k % 2) * 512 + nn],
                  h_d[128 * k:128 * (k + 1), n0:n0 + nn],
                  reads=[("h_d", k, n0)] + ([("h8q", i, k - 2)] if k >= 2 else []),
                  writes=[("hld", k % 2)])
            pend.append((hload[:, (k % 2) * 512:(k % 2) * 512 + nn],
                         h8[:, k * 512:k * 512 + nn], nn,
                         ("hld", k % 2), ("h8q", i, k), sc2[:, 6:7], ("sc2", 6)))
            if len(pend) == 2 or k == NKF - 1:
                fq.emit_jobs(a, pend)
                pend = []
        for m in range(KC):
            r = (m * 128) // PSL
            cm = (m * 128) % PSL
            slot = m % 2
            prev = ("pjg", i, m - 2) if m > 1 else (("pjg", i - 1, KC - 2) if i > 0 else None)
            a.dma("sync",
                  wpstrip[:, slot * NKF * 128:(slot + 1) * NKF * 128].rearrange(
                      "p (k c) -> p k c", k=NKF, c=128),
                  w8pj_g[FD * r:FD * (r + 1), cm:cm + 128].rearrange(
                      "(k p) c -> p k c", k=NKF, p=128),
                  reads=["w8pj_g"] + ([prev] if prev else []),
                  writes=[("wp", slot)])
            ps = PS[m % 2]
            pk = ("PSM", id(ps))
            for k in range(NKF):
                a.op("tensor", (lambda e, k=k, n0=n0, nn=nn, ps=ps, slot=slot:
                                e.matmul(
                    ps[:, :nn], wpstrip[:, slot * NKF * 128 + k * 128:
                                        slot * NKF * 128 + (k + 1) * 128],
                    h8[:, k * 512:k * 512 + nn],
                    start=(k == 0), stop=(k == NKF - 1))),
                    reads=[("wp", slot), ("h8q", i, k)],
                    writes=[pk] + ([("pjg", i, m)] if k == NKF - 1 else []))
            a.op("vector", (lambda e, ps=ps, nn=nn: e.tensor_scalar(
                ps[:, :nn], ps[:, :nn], sc2[:, 7:8], None, OP.mult)),
                reads=[pk, ("sc2", 7)], writes=[pk])
            mk_resid_evac(a, xstg, [(n0, nn)], "ff", x2p_d, x2T,
                          (lambda mm, nn0: ("x2p", mm, nn0)), "x2T")(
                a, (m, n0, nn), ps, pk)
    a.finish()
    a.phase()
    r4c.close()
    r4.close()
    r3.close()
    return nc


def build_l2(S, ctx=None):
    T = S // NCORES
    TC = T // 128
    NB = S // 128
    NT = nspans(T)
    ND = nspans(D)
    FI = 5504
    NKI = FI // 128
    NSI = nspans(FI)
    nc = _NCPfx(ctx["nc"], ctx["pfx"]) if ctx else bass.Bass("TRN2", target_bir_lowering=False)
    xT = ctx["x_in"] if ctx else nc.dram_tensor("xT", [D, T], F32, kind="ExternalInput").ap()
    wq_d = nc.dram_tensor("wq", [D, D], F32, kind="ExternalInput").ap()
    wk_d = nc.dram_tensor("wk", [D, D], F32, kind="ExternalInput").ap()
    wv_d = nc.dram_tensor("wv", [D, D], F32, kind="ExternalInput").ap()
    wo_d = nc.dram_tensor("wo", [D, D], F32, kind="ExternalInput").ap()
    gidx_d = nc.dram_tensor("gidx", [256, FI], I32, kind="ExternalInput").ap()
    uidx_d = nc.dram_tensor("uidx", [256, FI], I32, kind="ExternalInput").ap()
    didx_d = nc.dram_tensor("didx", [FI, 256], I32, kind="ExternalInput").ap()
    lut_d = nc.dram_tensor("lut", [1, 16], F32, kind="ExternalInput").ap()
    cs_d = nc.dram_tensor("cs", [128, S], F16, kind="ExternalInput").ap()
    slopes_d = nc.dram_tensor("slopes", [1, 2], F32, kind="ExternalInput").ap()
    cmask_d = nc.dram_tensor("cmask", [128, 4 * 512], BF16, kind="ExternalInput").ap()
    cdram = ctx["cdram"] if ctx else _mk_cdram(nc)
    x2T = ctx["x_out"] if ctx else nc.dram_tensor("x2T", [D, T], F32, kind="ExternalOutput").ap()
    qa_i = nc.dram_tensor("qa_i", [D, T], F16)
    qa_o = nc.dram_tensor("qa_o", [D, T], F16)
    ka_i = nc.dram_tensor("ka_i", [D, T], F16)
    ka_o = nc.dram_tensor("ka_o", [D, T], F16)
    va_i = nc.dram_tensor("va_i", [S, 256], F16)
    va_o = nc.dram_tensor("va_o", [S, 256], F16)
    ya_i = nc.dram_tensor("ya_i", [D, T], F16)
    ldbg = nc.dram_tensor("ldbg", [2 * (S // 128) + 2, 512], F32)
    ya_o = nc.dram_tensor("ya_o", [D, T], F16)
    w8g_p = nc.dram_tensor("w8g_p", [256, FI], BF16)
    w8g_g = nc.dram_tensor("w8g_g", [D, FI], BF16)
    w8u_p = nc.dram_tensor("w8u_p", [256, FI], BF16)
    w8u_g = nc.dram_tensor("w8u_g", [D, FI], BF16)
    w8d_p = nc.dram_tensor("w8d_p", [FI, 256], BF16)
    w8d_g = nc.dram_tensor("w8d_g", [NCORES * FI, 256], BF16)
    h_d = nc.dram_tensor("h_d", [FI, T], BF16)
    x2p_d = nc.dram_tensor("x2p", [D, T], F32)
    xn_d = nc.dram_tensor("xn_d", [D, T], F16)
    winA = NB + 9
    no_alibi = True
    ffn_fn = _l2_ffn

    es = contextlib.ExitStack()
    with es:
        a = ctx["a"] if ctx else Auto(nc, es)
        sb = lambda n, shp, dt, st=es, side=None: st.enter_context(
            nc.sbuf_tensor(n, shp, dt, side=side))
        u = mk_utils(a, nc, es, T)
        PS = [es.enter_context(nc.psum_tensor(f"PS{i}", [128, 512], F32)) for i in range(5)]
        PR = [es.enter_context(nc.psum_tensor("PR0", [1, 512], F32))]
        PSC = es.enter_context(nc.psum_tensor("PSC", [128, 512], BF16))
        PSC16 = es.enter_context(nc.psum_tensor("PSC16", [128, 128], F16))
        u["pstat"] = [PS[0], PS[1], PS[2], PR[0]]
        u["psb"] = PS[3:5]
        u["idf16"] = sb("u_idf16", [128, 128], F16)
        stage = sb("stage", [128, 2 * D], F32)
        iota_f = sb("iota_f", [128, 1], F32)
        iota_i = sb("iota_i", [128, 1], I32)
        slope_col = sb("slope_col", [128, 2], F32)
        scl = sb("scl", [128, 8], F32)
        srow = sb("srow", [1, 24], F32)
        r0x = contextlib.ExitStack()
        xn = sb("xn", [128, KC * T], F16, r0x)

        r0 = contextlib.ExitStack()
        x_sb = sb("x_sb", [128, KC * T], F32, r0)
        u["xbf"] = sb("xbf", [128, 2 * T], BF16, r0)
        u["sqbf"] = sb("sqbf", [128, 2 * T], BF16, r0)
        q_load_consts(a, u, cdram)
        a.op("vector", lambda e: e.tensor_copy(u["idf16"][:, :], u["id32"][:, :]),
             reads=[("c", "id32")], writes=[("c", "idf16")])
        a.op("vector", lambda e: e.memset(srow[:1, :], 0.0), writes=[("srow0",)])
        a.op("gpsimd", lambda e: e.iota(iota_i[:, :], [[1, 1]], base=0, channel_multiplier=1),
             writes=[("iota_i",)])
        a.op("vector", lambda e: e.tensor_copy(iota_f[:, :], iota_i[:, :]),
             reads=[("iota_i",)], writes=[("iota",)])
        q_load_act(a, xT, x_sb, T, "x")
        q_layernorm(a, u, T, x_sb, "x", xn, "xn")
        _drain_dmas(a)
        a.phase()
        r0.close()
        XN = [("xn", k) for k in range(KC)]

        r1 = contextlib.ExitStack()
        w_sb = sb("w_sb", [128, KC * D], F16, r1)
        r1q = contextlib.ExitStack()
        qstg = sb("qstg", [128, 2 * T], F16, r1q)
        q_stream_weight(a, wq_d, w_sb, KC, D, stage, "pq", "pqw")
        pqw_keys = [("pqw", k) for k in range(KC)]
        qgroups = [(m, n0, nn) for m in range(KC) for (n0, nn) in NT]

        def q_evac(a_, g, ps, pk):
            m, n0, nn = g
            slot = m % 2
            a_.op("vector", (lambda e: e.tensor_copy(
                qstg[:, slot * T + n0:slot * T + n0 + nn], ps[:, :nn])),
                reads=[pk], writes=[("qstg", slot, n0)])
            if n0 + nn >= T:
                a_.dma("scalar", qa_i[128 * m:128 * (m + 1), :],
                       qstg[:, slot * T:slot * T + T],
                       reads=[("qstg", slot, x0) for (x0, _) in NT],
                       writes=[("qa_i", m)])
        q_mm(a, "pqmm", qgroups,
             lambda g, k: w_sb[:, k * D + g[0] * 128:k * D + (g[0] + 1) * 128],
             lambda g, k: xn[:, k * T + g[1]:k * T + g[1] + g[2]],
             KC, PS[0:2], q_evac, lambda g: XN + pqw_keys,
             lambda g, ps: ps[:, :g[2]])
        _drain_dmas(a)
        a.phase()
        r1q.close()

        kv_sb = sb("kv_sb", [128, KC * T], F16, r1)
        vt_sb = sb("vt_sb", [128, TC * D], F16, r1)
        q_stream_weight(a, wk_d, w_sb, KC, D, stage, "pk", "pkw")
        kgroups = [(m, n0, nn) for m in range(KC) for (n0, nn) in NT]
        q_mm(a, "pkmm", kgroups,
             lambda g, k: w_sb[:, k * D + g[0] * 128:k * D + (g[0] + 1) * 128],
             lambda g, k: xn[:, k * T + g[1]:k * T + g[1] + g[2]],
             KC, PS[0:2],
             lambda a_, g, ps, pk: a_.op(
                 "vector",
                 (lambda e, g=g, ps=ps: e.tensor_copy(
                     kv_sb[:, g[0] * T + g[1]:g[0] * T + g[1] + g[2]], ps[:, :g[2]])),
                 reads=[pk], writes=[("kf", g[0], g[1])]),
             lambda g: XN + [("pkw", k) for k in range(KC)],
             lambda g, ps: ps[:, :g[2]])
        q_stream_weight(a, wv_d, w_sb, KC, D, stage, "pv", "pvw")
        vgroups = [(t, n0, nn) for t in range(TC) for (n0, nn) in ND]
        q_mm(a, "pvmm", vgroups,
             lambda g, k: xn[:, k * T + g[0] * 128:k * T + (g[0] + 1) * 128],
             lambda g, k: w_sb[:, k * D + g[1]:k * D + g[1] + g[2]],
             KC, PS[0:2],
             lambda a_, g, ps, pk: a_.op(
                 "vector",
                 (lambda e, g=g, ps=ps: e.tensor_copy(
                     vt_sb[:, g[0] * D + g[1]:g[0] * D + g[1] + g[2]], ps[:, :g[2]])),
                 reads=[pk], writes=[("vt", g[0], g[1])]),
             lambda g: XN + [("pvw", k) for k in range(KC)],
             lambda g, ps: ps[:, :g[2]])
        kf_all = [("kf", g, n0) for g in range(KC) for (n0, _) in NT]
        vt_all = [("vt", t, n0) for t in range(TC) for (n0, _) in ND]
        for j in range(NCORES):
            for sgi in range(2):
                grp = 2 * j + sgi
                a.dma("scalar", ka_i[256 * j + 128 * sgi:256 * j + 128 * (sgi + 1), :],
                      kv_sb[:, grp * T:(grp + 1) * T],
                      reads=kf_all, writes=[("ka_i", j, sgi)])
        for j in range(NCORES):
            for sgi in range(2):
                grp = 2 * j + sgi
                a.dma("sync",
                      va_i[j * T:(j + 1) * T,
                           128 * sgi:128 * (sgi + 1)].rearrange(
                          "(t p) c -> p t c", t=TC, p=128),
                      vt_sb[:, :].rearrange("p (t d) -> p t d", t=TC, d=D)[
                          :, :, grp * 128:(grp + 1) * 128],
                      reads=vt_all, writes=[("va_i", j, 0, sgi)])
        a.cc("AllToAll", OP.bypass, qa_i, qa_o, [("qa_i", m) for m in range(KC)], "qa_o")
        a.cc("AllToAll", OP.bypass, ka_i, ka_o,
             [("ka_i", j, sgi) for j in range(NCORES) for sgi in range(2)], "ka_o")
        a.cc("AllToAll", OP.bypass, va_i, va_o,
             [("va_i", j, 0, sgi) for j in range(NCORES)
              for sgi in range(2)], "va_o")
        q_store_act(a, xn, xn_d, T, "xn", "xn_d", eng="sync")
        _drain_dmas(a)
        a.phase()
        r1.close()
        r0x.close()
        return _l1_attn(nc, es, a, u, sb, PS, PR, locals())
    return nc


def _l2_ffn(nc, es, a, u, sb, PS, PR, L, r3, w_sb, xstg):
    S, T, TC = L["S"], L["T"], L["TC"]
    NT = L["NT"]
    FI, NKI, NSI = L["FI"], L["NKI"], L["NSI"]
    stage, srow, scl = L["stage"], L["srow"], L["scl"]
    xn_d = L["xn_d"]
    gidx_d, uidx_d, didx_d, lut_d = L["gidx_d"], L["uidx_d"], L["didx_d"], L["lut_d"]
    w8g_p, w8g_g = L["w8g_p"], L["w8g_g"]
    w8u_p, w8u_g = L["w8u_p"], L["w8u_g"]
    w8d_p, w8d_g = L["w8d_p"], L["w8d_g"]
    h_d, x2p_d, x2T = L["h_d"], L["x2p_d"], L["x2T"]
    if xstg is None:
        xstg = sb("xstg2", [128, 4 * 512], F32, r3)

    r4 = contextlib.ExitStack()
    idxs = sb("idxs", [128, 2 * 512], I32, r4)
    fidx = sb("fidx", [128, 2 * 512], F32, r4)
    wt16 = sb("wt16", [128, 32 * 512], BF16, r4)
    wacc = sb("wacc", [128, 2 * 512], BF16, r4)
    lutc = sb("lutc", [128, 16], F32, r4)
    a.dma("sync", srow[:1, 20:24].bitcast(F32) if False else stage[:1, 0:16], lut_d,
          reads=[("STG", id(stage), 0)], writes=[("lutr",), ("STG", id(stage), 0)])
    for v in range(16):
        q_scalar_bcast(a, u, stage[:1, v:v + 1], ("lutr",), lutc[:, v:v + 1],
                       ("lutc", v), PS[2])
    lutk = [("lutc", v) for v in range(16)]

    def matz(tag, src_d, dst_d, tiles):
        pendt = []

        def flush():
            for (i, r0, c0, w) in pendt:
                slot = i % 2
                a.op("vector", (lambda e, slot=slot, w=w: e.tensor_copy(
                    fidx[:, slot * 512:slot * 512 + w],
                    idxs[:, slot * 512:slot * 512 + w])),
                    reads=[(tag, "ld", slot), ("FID", slot), ("IDX", slot)],
                    writes=[(tag, "f", slot), ("FID", slot)])
            for v in range(16):
                for (i, r0, c0, w) in pendt:
                    slot = i % 2
                    a.op("vector", (lambda e, v=v, w=w, slot=slot: e.tensor_scalar(
                        wt16[:, (v * 2 + slot) * 512:(v * 2 + slot) * 512 + w],
                        fidx[:, slot * 512:slot * 512 + w], float(v),
                        lutc[:, v:v + 1], OP.is_equal, OP.mult)),
                        reads=[(tag, "f", slot), ("FID", slot), ("W16", v, slot)]
                              + lutk,
                        writes=[(tag, "v", v, slot), ("W16", v, slot)])
            for lvl, stepn in ((1, 8), (2, 4), (3, 2)):
                for j in range(stepn):
                    for (i, r0, c0, w) in pendt:
                        slot = i % 2
                        a.op("vector", (lambda e, j=j, stepn=stepn, w=w, slot=slot:
                                        e.tensor_tensor(
                            wt16[:, (j * 2 + slot) * 512:(j * 2 + slot) * 512 + w],
                            wt16[:, (j * 2 + slot) * 512:(j * 2 + slot) * 512 + w],
                            wt16[:, ((j + stepn) * 2 + slot) * 512:
                                 ((j + stepn) * 2 + slot) * 512 + w], OP.add)),
                            reads=[(tag, "v", j, slot), (tag, "v", j + stepn, slot),
                                   ("W16", j, slot), ("W16", j + stepn, slot)],
                            writes=[(tag, "v", j, slot), ("W16", j, slot)])
            for (i, r0, c0, w) in pendt:
                slot = i % 2
                a.op("vector", (lambda e, slot=slot, w=w: e.tensor_tensor(
                    wacc[:, slot * 512:slot * 512 + w],
                    wt16[:, slot * 512:slot * 512 + w],
                    wt16[:, (2 + slot) * 512:(2 + slot) * 512 + w], OP.add)),
                    reads=[(tag, "v", 0, slot), (tag, "v", 1, slot),
                           (tag, "wa", slot),
                           ("W16", 0, slot), ("W16", 1, slot), ("WAC", slot)],
                    writes=[(tag, "acc", slot), ("WAC", slot)])
            for (i, r0, c0, w) in pendt:
                slot = i % 2
                a.dma("scalar", dst_d[r0:r0 + 128, c0:c0 + w],
                      wacc[:, slot * 512:slot * 512 + w],
                      reads=[(tag, "acc", slot), ("WAC", slot)],
                      writes=[(tag, "st", i), (tag, "wa", slot)])

        for i, (r0, c0, w) in enumerate(tiles):
            slot = i % 2
            a.dma("sync", idxs[:, slot * 512:slot * 512 + w],
                  src_d[r0:r0 + 128, c0:c0 + w],
                  reads=([(tag, "st", i - 2)] if i >= 2 else []),
                  writes=[(tag, "ld", slot), ("IDX", slot)])
            pendt.append((i, r0, c0, w))
            if len(pendt) == 2 or i == len(tiles) - 1:
                flush()
                pendt = []
        return [(tag, "st", i) for i in range(len(tiles))]

    gtiles = [(128 * r, c0, w) for r in range(2) for (c0, w) in NSI]
    dtiles = [(128 * r, 0, 256) for r in range(NKI)]
    gk = matz("mg", gidx_d, w8g_p, gtiles)
    uk = matz("mu", uidx_d, w8u_p, gtiles)
    dk = matz("md", didx_d, w8d_p, dtiles)
    a.cc("AllGather", OP.bypass, w8g_p, w8g_g, gk, "w8g_g")
    a.cc("AllGather", OP.bypass, w8u_p, w8u_g, uk, "w8u_g")
    a.cc("AllGather", OP.bypass, w8d_p, w8d_g, dk, "w8d_g")
    _drain_dmas(a)
    a.phase()
    r4.close()

    import os as _os
    if _os.environ.get("KDBG_NOFFN"):
        a.finish()
        a.phase()
        r3.close()
        return nc
    r5 = contextlib.ExitStack()
    xnl = sb("xnl", [128, KC * T], F16, r5)
    gustrip = sb("gustrip", [128, 2 * 2 * KC * 128], BF16, r5)
    hsil = sb("hsil", [128, 2 * 512], F32, r5)
    hbf = sb("hbf", [128, 2 * 512], BF16, r5)
    q_load_act(a, xn_d, xnl, T, "xl", src_key="xn_d")
    XL = [("xl", k) for k in range(KC)]
    GW = 2 * KC * 128
    for ic in range(NKI):
        slot = ic % 2
        for wi, wg in enumerate((w8g_g, w8u_g)):
            a.dma("sync",
                  gustrip[:, slot * GW + wi * KC * 128:
                          slot * GW + (wi + 1) * KC * 128].rearrange(
                      "p (k c) -> p k c", k=KC, c=128),
                  wg[0:D, 128 * ic:128 * ic + 128].rearrange(
                      "(k p) c -> p k c", k=KC, p=128),
                  reads=(["w8g_g", "w8u_g"] + ([("gu", ic - 2)] if ic >= 2 else [])),
                  writes=[("gw", slot, wi)])
        for i, (n0, nn) in enumerate(NT):
            pg = PS[2 * slot]
            pu = PS[2 * slot + 1]
            pgk, puk = ("PSM", id(pg)), ("PSM", id(pu))
            for wi, (ps, pk) in enumerate(((pg, pgk), (pu, puk))):
                for k in range(KC):
                    a.op("tensor", (lambda e, slot=slot, wi=wi, k=k, n0=n0, nn=nn,
                                    ps=ps: e.matmul(
                        ps[:, :nn],
                        gustrip[:, slot * GW + (wi * KC + k) * 128:
                                slot * GW + (wi * KC + k) * 128 + 128],
                        xnl[:, k * T + n0:k * T + n0 + nn],
                        start=(k == 0), stop=(k == KC - 1))),
                        reads=[("gw", slot, wi)] + XL,
                        writes=[pk] + ([("gu", ic)] if k == KC - 1 and wi == 1
                                       and i == len(NT) - 1 else []))
            a.op("scalar", (lambda e, i=i, nn=nn, pg=pg: e.activation(
                hsil[:, i * 512:i * 512 + nn], pg[:, :nn], AF.Silu)),
                reads=[pgk, ("HSL", i)], writes=[("sil", i), ("HSL", i)])
            a.op("vector", (lambda e, i=i, nn=nn, pu=pu: e.tensor_tensor(
                hbf[:, i * 512:i * 512 + nn], hsil[:, i * 512:i * 512 + nn],
                pu[:, :nn], OP.mult)),
                reads=[("sil", i), puk, ("HBF", i), ("HSL", i)],
                writes=[("hb", i), ("HBF", i)])
            a.dma("scalar", h_d[128 * ic:128 * (ic + 1), n0:n0 + nn],
                  hbf[:, i * 512:i * 512 + nn],
                  reads=[("hb", i)], writes=[("h_d", ic, n0), ("HBF", i)])
    _drain_dmas(a)
    a.phase()
    r5.close()

    r6 = contextlib.ExitStack()
    hres = sb("hres", [128, NKI * 512], BF16, r6)
    dstrip = sb("dstrip", [128, 2 * NKI * 128], BF16, r6)
    WSP = NT[0][1]
    for i, (n0, nn) in enumerate(NT):
        a.dma("sync",
              hres[:, 0:NKI * WSP].rearrange("p (k c) -> p k c", k=NKI, c=WSP),
              h_d[0:FI, n0:n0 + nn].rearrange("(k p) c -> p k c", k=NKI, p=128),
              reads=[("h_d", k, n0) for k in range(NKI)]
                    + ([("dmm", i - 1, m2) for m2 in (KC - 1,)] if i >= 1 else []),
              writes=[("hr", i)])
        for m in range(KC):
            r = m // 2
            cm = (m % 2) * 128
            slot = m % 2
            prev = ("dmm", i, m - 2) if m >= 2 else (("dmm", i - 1, KC - 2) if i >= 1 else None)
            a.dma("sync",
                  dstrip[:, slot * NKI * 128:(slot + 1) * NKI * 128].rearrange(
                      "p (k c) -> p k c", k=NKI, c=128),
                  w8d_g[FI * r:FI * (r + 1), cm:cm + 128].rearrange(
                      "(k p) c -> p k c", k=NKI, p=128),
                  reads=["w8d_g"] + ([prev] if prev else []),
                  writes=[("dw", slot)])
            ps = PS[m % 2]
            pk = ("PSM", id(ps))
            for k in range(NKI):
                a.op("tensor", (lambda e, slot=slot, k=k, n0=n0, nn=nn, ps=ps: e.matmul(
                    ps[:, :nn],
                    dstrip[:, slot * NKI * 128 + k * 128:slot * NKI * 128 + (k + 1) * 128],
                    hres[:, k * WSP:k * WSP + nn],
                    start=(k == 0), stop=(k == NKI - 1))),
                    reads=[("dw", slot), ("hr", i)],
                    writes=[pk] + ([("dmm", i, m)] if k == NKI - 1 else []))
            mk_resid_evac(a, xstg, [(n0, nn)], "ff", x2p_d, x2T,
                          (lambda mm, nn0: ("x2p", mm, nn0)), "x2T")(
                a, (m, n0, nn), ps, pk)
    a.finish()
    a.phase()
    r6.close()
    r3.close()
    return nc


def build_l3(S, ctx=None):
    T = S // NCORES
    TC = T // 128
    NT = nspans(T)
    ND = nspans(D)
    FI = 5504
    NKI = FI // 128
    NSI = nspans(FI)
    nc = _NCPfx(ctx["nc"], ctx["pfx"]) if ctx else bass.Bass("TRN2", target_bir_lowering=False)
    xT = ctx["x_in"] if ctx else nc.dram_tensor("xT", [D, T], F32, kind="ExternalInput").ap()
    wbc_d = nc.dram_tensor("wbc", [D, 1024], F32, kind="ExternalInput").ap()
    wz_d = nc.dram_tensor("wz", [D, 1], F32, kind="ExternalInput").ap()
    win_d = nc.dram_tensor("win", [D, 512], F32, kind="ExternalInput").ap()
    wout_d = nc.dram_tensor("wout", [512, D], F32, kind="ExternalInput").ap()
    convw_d = nc.dram_tensor("convw", [128, 64], F32, kind="ExternalInput").ap()
    convb_d = nc.dram_tensor("convb", [128, 16], F32, kind="ExternalInput").ap()
    alog_d = nc.dram_tensor("alog", [128, 4], F32, kind="ExternalInput").ap()
    dcoef_d = nc.dram_tensor("dcoef", [128, 16], F32, kind="ExternalInput").ap()
    msk_d = nc.dram_tensor("msk", [1, 16], F32, kind="ExternalInput").ap()
    gidx_d = nc.dram_tensor("gidx", [256, FI], I32, kind="ExternalInput").ap()
    uidx_d = nc.dram_tensor("uidx", [256, FI], I32, kind="ExternalInput").ap()
    didx_d = nc.dram_tensor("didx", [FI, 256], I32, kind="ExternalInput").ap()
    lut_d = nc.dram_tensor("lut", [1, 16], F32, kind="ExternalInput").ap()
    cdram = ctx["cdram"] if ctx else _mk_cdram(nc)
    x2T = ctx["x_out"] if ctx else nc.dram_tensor("x2T", [D, T], F32, kind="ExternalOutput").ap()
    hal_i = nc.dram_tensor("hal_i", [D, 3], F16)
    hal_g = nc.dram_tensor("hal_g", [NCORES * D, 3], F16)
    ed_i = nc.dram_tensor("ed_i", [128, 8], F32)
    ed_g = nc.dram_tensor("ed_g", [NCORES * 128, 8], F32)
    w8g_p = nc.dram_tensor("w8g_p", [256, FI], BF16)
    w8g_g = nc.dram_tensor("w8g_g", [D, FI], BF16)
    w8u_p = nc.dram_tensor("w8u_p", [256, FI], BF16)
    w8u_g = nc.dram_tensor("w8u_g", [D, FI], BF16)
    w8d_p = nc.dram_tensor("w8d_p", [FI, 256], BF16)
    w8d_g = nc.dram_tensor("w8d_g", [NCORES * FI, 256], BF16)
    h_d = nc.dram_tensor("h_d", [FI, T], BF16)
    x2p_d = nc.dram_tensor("x2p", [D, T], F32)
    xn_d = nc.dram_tensor("xn_d", [D, T], F16)

    es = contextlib.ExitStack()
    with es:
        a = ctx["a"] if ctx else Auto(nc, es)
        sb = lambda n, shp, dt, st=es, side=None: st.enter_context(
            nc.sbuf_tensor(n, shp, dt, side=side))
        u = mk_utils(a, nc, es, T)
        PS = [es.enter_context(nc.psum_tensor(f"PS{i}", [128, 512], F32)) for i in range(5)]
        PR = [es.enter_context(nc.psum_tensor("PR0", [1, 512], F32))]
        u["pstat"] = [PS[0], PS[1], PS[2], PR[0]]
        u["psb"] = PS[3:5]
        stage = sb("stage", [128, 2 * D], F32)
        scl = sb("scl", [128, 8], F32)
        srow = sb("srow", [1, 24], F32)
        convw_sb = sb("convw_sb", [128, 64], F32)
        convb_sb = sb("convb_sb", [128, 16], F32)
        alog_sb = sb("alog_sb", [128, 4], F32)
        acol = sb("acol", [128, 4], F32)
        dd_sb = sb("dd_sb", [128, 16], F32)
        mrow = sb("mrow", [1, 16], F32)
        mcz = sb("mcz", [128, 24], F32)
        sdcol = sb("sdcol", [128, 1], F32)
        drw = sb("drw", [1, 1024], F32)
        drw2 = sb("drw2", [1, 1024], F32)
        drw3 = sb("drw3", [1, 1024], F32)
        edst = sb("edst", [128, 8], F32)
        edl = sb("edl", [128, 64], F32)
        carry = sb("carry", [128, 4], F32)
        cfold = sb("cfold", [128, 2], F32)
        rP = contextlib.ExitStack()
        xc = sb("xc", [128, KC * T], F16, rP)
        hf16 = sb("hf16", [128, 4 * T], F16, rP)

        r0 = contextlib.ExitStack()
        x_sb = sb("x_sb", [128, KC * T], F32, r0)
        u["xbf"] = sb("xbf", [128, 2 * T], BF16, r0)
        u["sqbf"] = sb("sqbf", [128, 2 * T], BF16, r0)
        xn = sb("xn0", [128, KC * T], F16, r0)
        q_load_consts(a, u, cdram)
        a.op("vector", lambda e: e.memset(srow[:1, :], 0.0), writes=[("srow0",)])
        for dst, src, key in ((convw_sb, convw_d, "cw"), (convb_sb, convb_d, "cb"),
                              (alog_sb, alog_d, "al"), (dd_sb, dcoef_d, "dd"),
                              (mrow, msk_d, "mr")):
            a.dma("sync", dst[:dst.shape[0], :], src, writes=[(key,)])
        a.op("scalar", lambda e: e.activation(acol[:, :], alog_sb[:, :], AF.Exp),
             reads=[("al",)], writes=[("ac0",)])
        a.op("vector", lambda e: e.tensor_scalar(acol[:, :], acol[:, :], -1.0, None,
                                                 OP.mult),
             reads=[("ac0",)], writes=[("ac",)])
        for r in range(8):
            q_scalar_bcast(a, u, mrow[:1, r:r + 1], ("mr",), mcz[:, r:r + 1],
                           ("hm", r), PS[2])
            q_scalar_bcast(a, u, mrow[:1, 8 + r:9 + r], ("mr",), mcz[:, 8 + r:9 + r],
                           ("pm", r), PS[2])
            a.op("vector", (lambda e, r=r: e.tensor_scalar(
                mcz[:, 16 + r:17 + r], mcz[:, 8 + r:9 + r], -1.0, 1.0,
                OP.mult, OP.add)),
                reads=[("pm", r)], writes=[("ipm", r)])
        q_load_act(a, xT, x_sb, T, "x")
        q_layernorm(a, u, T, x_sb, "x", xn, "xn")
        for k in range(KC):
            a.dma("sync", hal_i[128 * k:128 * (k + 1), :],
                  xn[:, k * T + T - 3:k * T + T],
                  reads=[("xn", k)], writes=[("hal_i", k)])
        a.cc("AllGather", OP.bypass, hal_i, hal_g,
             [("hal_i", k) for k in range(KC)], "hal_g")
        q_store_act(a, xn, xn_d, T, "xn", "xn_d", eng="scalar")
        _drain_dmas(a)
        a.phase()
        r0.close()

        r1 = contextlib.ExitStack()
        bc_sb = sb("bc_sb", [128, 8 * T], F32, r1)
        w_sb = sb("w_sb", [128, KC * 1024], F16, r1)
        wzc = sb("wzc", [128, 16], F32, r1)
        wzc16 = sb("wzc16", [128, 16], F16, r1)
        r1x = contextlib.ExitStack()
        xnl = sb("xnl3", [128, KC * T], F16, r1x)
        q_load_act(a, xn_d, xnl, T, "xl", src_key="xn_d")
        XN = [("xl", k) for k in range(KC)]
        q_stream_weight(a, wbc_d, w_sb, KC, 1024, stage, "pb", "pbw")
        bgroups = [(m, n0, nn) for m in range(8) for (n0, nn) in NT]
        q_mm(a, "pbmm", bgroups,
             lambda g, k: w_sb[:, k * 1024 + g[0] * 128:k * 1024 + (g[0] + 1) * 128],
             lambda g, k: xnl[:, k * T + g[1]:k * T + g[1] + g[2]],
             KC, PS[0:2],
             lambda a_, g, ps, pk: a_.op(
                 "vector",
                 (lambda e, g=g, ps=ps: e.tensor_copy(
                     bc_sb[:, g[0] * T + g[1]:g[0] * T + g[1] + g[2]], ps[:, :g[2]])),
                 reads=[pk], writes=[("bc", g[0], g[1])]),
             lambda g: XN + [("pbw", k) for k in range(KC)],
             lambda g, ps: ps[:, :g[2]])
        for k in range(KC):
            a.dma("sync", wzc[:, k:k + 1], wz_d[128 * k:128 * (k + 1), :],
                  writes=[("wz", k)])
        a.op("vector", lambda e: e.tensor_copy(wzc16[:, :], wzc[:, :]),
             reads=[("wz", k2) for k2 in range(KC)], writes=[("wz16",)])
        for i, (n0, nn) in enumerate(NT):
            for k in range(KC):
                a.op("tensor", (lambda e, k=k, n0=n0, nn=nn: e.matmul(
                    PR[0][:1, :nn], wzc16[:, k:k + 1],
                    xnl[:, k * T + n0:k * T + n0 + nn],
                    start=(k == 0), stop=(k == KC - 1))),
                    reads=[("wz16",)] + XN, writes=[("PSM", id(PR[0]))])
            a.op("vector", (lambda e, n0=n0, nn=nn: e.tensor_copy(
                drw[:1, n0:n0 + nn], PR[0][:1, :nn])),
                reads=[("PSM", id(PR[0]))], writes=[("zd", i)])
        zdk = [("zd", i2) for i2 in range(len(NT))]
        a.op("scalar", lambda e: e.activation(drw2[:1, 0:T], drw[:1, 0:T], AF.Relu),
             reads=zdk, writes=[("sp1",)])
        a.op("scalar", lambda e: e.activation(drw3[:1, 0:T], drw[:1, 0:T], AF.Abs),
             reads=zdk, writes=[("sp2",)])
        a.op("scalar", lambda e: e.activation(drw3[:1, 0:T], drw3[:1, 0:T], AF.Exp,
                                              scale=-1.0),
             reads=[("sp2",)], writes=[("sp2",)])
        a.op("vector", lambda e: e.tensor_scalar(drw3[:1, 0:T], drw3[:1, 0:T], 1.0,
                                                 None, OP.add),
             reads=[("sp2",)], writes=[("sp2",)])
        a.op("scalar", lambda e: e.activation(drw3[:1, 0:T], drw3[:1, 0:T], AF.Ln),
             reads=[("sp2",)], writes=[("sp2",)])
        a.op("vector", lambda e: e.tensor_tensor(drw[:1, 0:T], drw2[:1, 0:T],
                                                 drw3[:1, 0:T], OP.add),
             reads=[("sp1",), ("sp2",)], writes=[("dl",)])
        a.op("vector", lambda e: e.reduce_sum(srow[:1, 1:2], drw[:1, 0:T], axis=AX.X),
             reads=[("dl",), ("srow0",)], writes=[("sd",)])
        q_scalar_bcast(a, u, srow[:1, 1:2], ("sd",), sdcol[:, 0:1], ("sdc",), PS[2])
        a.phase()

        r2 = contextlib.ExitStack()
        hstg = sb("hstg", [128, 8 * 48], F16, r2)
        halo = sb("halo", [128, 48], F32, r2)
        xe = sb("xe", [128, T + 3], F32, r2)
        xct = sb("xct", [128, T], F32, r2)
        for r in range(8):
            a.dma("sync", hstg[:, r * 48:(r + 1) * 48].rearrange(
                      "p (k c) -> p k c", k=KC, c=3),
                  hal_g[r * D:(r + 1) * D, :].rearrange(
                      "(k p) c -> p k c", k=KC, p=128),
                  reads=["hal_g"], writes=[("hst", r, 0)])
        hstk = [("hst", r, 0) for r in range(8)]
        for r in range(8):
            if r == 0:
                a.op("vector", lambda e: e.tensor_scalar(
                    halo[:, :], hstg[:, 0:48], mcz[:, 0:1], None, OP.mult),
                    reads=hstk + [("hm", 0)], writes=[("halo",)])
            else:
                a.op("vector", (lambda e, r=r: e.scalar_tensor_tensor(
                    halo[:, :], hstg[:, r * 48:(r + 1) * 48], mcz[:, r:r + 1],
                    halo[:, :], OP.mult, OP.add)),
                    reads=hstk + [("hm", r), ("halo",)], writes=[("halo",)])
        for k in range(KC):
            a.op("vector", (lambda e, k=k: e.tensor_copy(
                xe[:, 0:3], halo[:, 3 * k:3 * (k + 1)])),
                reads=[("halo",), ("XE",)], writes=[("xe", "h"), ("XE",)])
            a.op("vector", (lambda e, k=k: e.tensor_copy(
                xe[:, 3:T + 3], xnl[:, k * T:(k + 1) * T])),
                reads=[("xl", k), ("XE",)], writes=[("xe", "m"), ("XE",)])
            a.op("vector", (lambda e, k=k: e.tensor_scalar(
                xct[:, :], xe[:, 0:T], convw_sb[:, 4 * k:4 * k + 1], None, OP.mult)),
                reads=[("xe", "h"), ("xe", "m"), ("XE",), ("cw",), ("XCT",)],
                writes=[("xct",), ("XCT",)])
            for j in range(1, 4):
                a.op("vector", (lambda e, k=k, j=j: e.scalar_tensor_tensor(
                    xct[:, :], xe[:, j:j + T], convw_sb[:, 4 * k + j:4 * k + j + 1],
                    xct[:, :], OP.mult, OP.add)),
                    reads=[("xe", "h"), ("xe", "m"), ("XE",), ("cw",), ("xct",), ("XCT",)],
                    writes=[("xct",), ("XCT",)])
            a.op("vector", (lambda e, k=k: e.tensor_scalar(
                xc[:, k * T:(k + 1) * T], xct[:, :], convb_sb[:, k:k + 1], None,
                OP.add)),
                reads=[("xct",), ("XCT",), ("cb",)], writes=[("xc", k)])
        a.phase()
        r2.close()
        r1x.close()
        XC = [("xc", k) for k in range(KC)]

        r4 = contextlib.ExitStack()
        xs_sb = sb("xs_sb", [128, 4 * T], F32, r4)
        q_stream_weight(a, win_d, w_sb, KC, 512, stage, "pw", "pww")
        sgroups = [(m, n0, nn) for m in range(4) for (n0, nn) in NT]
        q_mm(a, "psmm", sgroups,
             lambda g, k: w_sb[:, k * 512 + g[0] * 128:k * 512 + (g[0] + 1) * 128],
             lambda g, k: xc[:, k * T + g[1]:k * T + g[1] + g[2]],
             KC, PS[0:2],
             lambda a_, g, ps, pk: a_.op(
                 "vector",
                 (lambda e, g=g, ps=ps: e.tensor_copy(
                     xs_sb[:, g[0] * T + g[1]:g[0] * T + g[1] + g[2]], ps[:, :g[2]])),
                 reads=[pk], writes=[("xs", g[0], g[1])]),
             lambda g: XC + [("pww", k) for k in range(KC)],
             lambda g, ps: ps[:, :g[2]])
        a.phase()

        dbc = sb("dbc", [128, T], F32, r4)
        da = sb("da", [128, 4 * T], F32, r4)
        hloc = bc_sb
        for i, (n0, nn) in enumerate(NT):
            a.op("tensor", (lambda e, n0=n0, nn=nn: e.matmul(
                PS[4][:, :nn], u["ones_row32"][:, :], drw[:1, n0:n0 + nn],
                start=True, stop=True)),
                reads=[("dl",), "c"], writes=[("PSM", id(PS[4]))])
            a.op("vector", (lambda e, n0=n0, nn=nn: e.tensor_copy(
                dbc[:, n0:n0 + nn], PS[4][:, :nn])),
                reads=[("PSM", id(PS[4]))], writes=[("dbc", i)])
        DBC = [("dbc", i) for i in range(len(NT))]
        xsk = lambda sc: [("xs", sc, n0) for (n0, _) in NT]
        for sc in range(4):
            a.op("vector", (lambda e, sc=sc: e.tensor_tensor(
                xs_sb[:, sc * T:(sc + 1) * T], xs_sb[:, sc * T:(sc + 1) * T],
                bc_sb[:, sc * T:(sc + 1) * T], OP.mult)),
                reads=xsk(sc) + [("bc", sc, n0) for (n0, _) in NT],
                writes=[("bu", sc)])
            a.op("vector", (lambda e, sc=sc: e.tensor_tensor(
                xs_sb[:, sc * T:(sc + 1) * T], xs_sb[:, sc * T:(sc + 1) * T],
                dbc[:, :], OP.mult)),
                reads=[("bu", sc)] + DBC, writes=[("bu", sc)])
            a.op("vector", (lambda e, sc=sc: e.tensor_scalar(
                da[:, sc * T:(sc + 1) * T], dbc[:, :], acol[:, sc:sc + 1], None,
                OP.mult)),
                reads=DBC + [("ac",)], writes=[("da0", sc)])
            a.op("scalar", (lambda e, sc=sc: e.activation(
                da[:, sc * T:(sc + 1) * T], da[:, sc * T:(sc + 1) * T], AF.Exp)),
                reads=[("da0", sc)], writes=[("da", sc)])
            a.op("vector", (lambda e, sc=sc: e.tensor_tensor_scan(
                hloc[:, sc * T:(sc + 1) * T], da[:, sc * T:(sc + 1) * T],
                xs_sb[:, sc * T:(sc + 1) * T], 0.0, OP.mult, OP.add)),
                reads=[("da", sc), ("bu", sc)],
                writes=[("hl", sc)] + [("bc", sc, n0) for (n0, _) in NT])
            a.op("vector", (lambda e, sc=sc: e.tensor_copy(
                edst[:, sc:sc + 1], hloc[:, sc * T + T - 1:sc * T + T])),
                reads=[("hl", sc)], writes=[("ed", sc)])
            a.op("vector", (lambda e, sc=sc: e.tensor_tensor(
                edst[:, 4 + sc:5 + sc], acol[:, sc:sc + 1], sdcol[:, 0:1], OP.mult)),
                reads=[("ac",), ("sdc",)], writes=[("ed0", sc)])
            a.op("scalar", (lambda e, sc=sc: e.activation(
                edst[:, 4 + sc:5 + sc], edst[:, 4 + sc:5 + sc], AF.Exp)),
                reads=[("ed0", sc)], writes=[("ed", 4 + sc)])
        a.dma("sync", ed_i[:, :], edst[:, :],
              reads=[("ed", j) for j in range(8)], writes=[("ed_i",)])
        a.cc("AllGather", OP.bypass, ed_i, ed_g, [("ed_i",)], "ed_g")
        for r in range(8):
            a.dma("sync", edl[:, 8 * r:8 * (r + 1)], ed_g[128 * r:128 * (r + 1), :],
                  reads=["ed_g"], writes=[("edl", r)])
        edlk = [("edl", r) for r in range(8)]
        a.op("vector", lambda e: e.memset(carry[:, :], 0.0), writes=[("cy",)])
        for r in range(8):
            for sc in range(4):
                a.op("vector", (lambda e, r=r, sc=sc: e.scalar_tensor_tensor(
                    cfold[:, 0:1], edl[:, 8 * r + 4 + sc:8 * r + 5 + sc],
                    mcz[:, 8 + r:9 + r], mcz[:, 16 + r:17 + r], OP.mult, OP.add)),
                    reads=edlk + [("pm", r), ("ipm", r), ("CF",)],
                    writes=[("cfd", r, sc), ("CF",)])
                a.op("vector", (lambda e, r=r, sc=sc: e.tensor_scalar(
                    cfold[:, 1:2], edl[:, 8 * r + sc:8 * r + 1 + sc],
                    mcz[:, 8 + r:9 + r], None, OP.mult)),
                    reads=edlk + [("pm", r), ("CF",)], writes=[("cfe", r, sc), ("CF",)])
                a.op("vector", (lambda e, r=r, sc=sc: e.scalar_tensor_tensor(
                    carry[:, sc:sc + 1], carry[:, sc:sc + 1], cfold[:, 0:1],
                    cfold[:, 1:2], OP.mult, OP.add)),
                    reads=[("cfd", r, sc), ("cfe", r, sc), ("cy",), ("CF",)],
                    writes=[("cy",)])
        for sc in range(4):
            a.op("vector", (lambda e, sc=sc: e.tensor_tensor_scan(
                hloc[:, sc * T:(sc + 1) * T], da[:, sc * T:(sc + 1) * T],
                xs_sb[:, sc * T:(sc + 1) * T], carry[:, sc:sc + 1],
                OP.mult, OP.add)),
                reads=[("da", sc), ("bu", sc), ("cy",), ("ed", sc)],
                writes=[("hl2", sc)])
            a.op("vector", (lambda e, sc=sc: e.tensor_tensor(
                hloc[:, sc * T:(sc + 1) * T], hloc[:, sc * T:(sc + 1) * T],
                bc_sb[:, (4 + sc) * T:(5 + sc) * T], OP.mult)),
                reads=[("hl2", sc)] + [("bc", 4 + sc, n0) for (n0, _) in NT],
                writes=[("hmod", sc)])
            a.op("vector", (lambda e, sc=sc: e.tensor_copy(
                hf16[:, sc * T:(sc + 1) * T], hloc[:, sc * T:(sc + 1) * T])),
                reads=[("hmod", sc)], writes=[("hf", sc)])
        a.phase()
        r4.close()

        r5 = contextlib.ExitStack()
        xstg = sb("xstg", [128, 4 * 512], F32, r5)
        dtt = sb("dtt", [128, 512], F32, r5)
        q_stream_weight(a, wout_d, w_sb, 4, D, stage, "po", "pow")
        HF = [("hf", sc) for sc in range(4)]
        ogroups = [(m, n0, nn) for m in range(KC) for (n0, nn) in NT]
        resid = mk_resid_evac(a, xstg, NT, "rp", xT, x2p_d, None, "x2p")

        def o_evac(a_, g, ps, pk):
            m, n0, nn = g
            a_.op("vector", (lambda e, m=m, n0=n0, nn=nn: e.tensor_scalar(
                dtt[:, :nn], xc[:, m * T + n0:m * T + n0 + nn],
                dd_sb[:, m:m + 1], None, OP.mult)),
                reads=[("xc", m), ("dd",), ("DTT",)], writes=[("dt",), ("DTT",)])
            a_.op("vector", (lambda e, nn=nn, ps=ps: e.tensor_tensor(
                ps[:, :nn], ps[:, :nn], dtt[:, :nn], OP.add)),
                reads=[pk, ("dt",), ("DTT",)], writes=[pk, ("DTT",)])
            resid(a_, g, ps, pk)
        q_mm(a, "pomm", ogroups,
             lambda g, k: w_sb[:, k * D + g[0] * 128:k * D + (g[0] + 1) * 128],
             lambda g, k: hf16[:, k * T + g[1]:k * T + g[1] + g[2]],
             4, PS[0:2], o_evac,
             lambda g: HF + [("pow", k) for k in range(4)],
             lambda g, ps: ps[:, :g[2]])
        _drain_dmas(a)
        a.phase()
        r5.close()
        r1.close()
        rP.close()
        r3 = contextlib.ExitStack()
        return _l2_ffn(nc, es, a, u, sb, PS, PR, locals(), r3, None, None)
    return nc
